# version 06
# baseline (speedup 1.0000x reference)
import numpy as np
import ml_dtypes
import concourse.bass as bass
import concourse.mybir as mybir
from concourse.bass_utils import run_bass_kernel_spmd

F32 = mybir.dt.float32
BF16 = mybir.dt.bfloat16
FP8 = mybir.dt.float8e4
AF = mybir.ActivationFunctionType
ALU = mybir.AluOpType

T = 2048
C = 1024
H = 16
D = 64
KC = 8
NTIL = T // 128
NBLK = 4
TPB = 4
LN_EPS = 1e-5
ATT_EPS = 1e-6

DEBUG_TAPS = 0
BISECT = 7
SUB4 = 9; NKT = 4


class _Sem:
    def __init__(self, handle):
        self.h = handle
        self.count = 0


class _Builder:
    def __init__(self):
        self.eng_ops = {e: [] for e in ("sync", "scalar", "vector", "gpsimd", "tensor")}

    def op(self, eng, fn, waits=(), incs=()):
        for s, v in waits:
            assert v <= s.count, f"wait on {v} > produced {s.count} ({eng})"
        self.eng_ops[eng].append((list(waits), fn, list(incs)))
        for s, n in incs:
            s.count += n


def build_nc():
    nc = bass.Bass("TRN2", target_bir_lowering=False, detect_race_conditions=False)

    xh = nc.dram_tensor("x", [T, C], F32, kind="ExternalInput").ap()
    Wa = nc.dram_tensor("W_attn", [C, 3 * C], F32, kind="ExternalInput").ap()
    ba = nc.dram_tensor("b_attn", [3 * C], F32, kind="ExternalInput").ap()
    Wp = nc.dram_tensor("W_proj", [C, C], F32, kind="ExternalInput").ap()
    bp = nc.dram_tensor("b_proj", [C], F32, kind="ExternalInput").ap()
    Wft = nc.dram_tensor("Wt_ffn", [C, C], F32, kind="ExternalInput").ap()
    bf = nc.dram_tensor("b_ffn", [C], F32, kind="ExternalInput").ap()
    maskh = nc.dram_tensor("mask", [128, 512], BF16, kind="ExternalInput").ap()
    bqkh = nc.dram_tensor("bqk_h", [128, 16], F32, kind="ExternalInput").ap()
    outh = nc.dram_tensor("out", [T, C], F32, kind="ExternalOutput").ap()
    taps = {}
    if DEBUG_TAPS:
        for nm, shape in [("t_lntm", [128, 4, 1024]), ("t_xsb", [128, 2, 4, 1024]),
                          ("t_qfm", [128, 4, KC, 128]), ("t_kfm", [128, 4, KC, 128]),
                          ("t_ktm", [128, 4, 1024]), ("t_vsb", [128, 4, 1024]),
                          ("t_kva", [128, 8, 65]), ("t_yfm", [128, 4, KC, 128]),
                          ("t_stm", [128, 16, 128]), ("t_lnfm", [128, 4, KC, 128])]:
            taps[nm] = nc.dram_tensor(nm, shape, F32, kind="ExternalOutput").ap()

    import contextlib
    ctx = contextlib.ExitStack()
    sb = lambda name, shape, dt: ctx.enter_context(nc.sbuf_tensor(name, shape, dt))
    ps = lambda name, shape, dt: ctx.enter_context(nc.psum_tensor(name, shape, dt))
    sem = lambda name: _Sem(ctx.enter_context(nc.semaphore(name)))

    with ctx:
        W_qk = sb("W_qk", [128, KC, 2 * C], BF16)
        W_v = sb("W_v", [128, KC, C], BF16)
        W_pr = sb("W_pr", [128, KC, C], BF16)
        W_f8 = sb("W_f8", [128, KC, C], BF16)
        bqk = sb("bqk", [128, 16], F32)
        bv_row = sb("bv_row", [1, C], BF16)
        bp_row = sb("bp_row", [1, C], BF16)
        bf_row = sb("bf_row", [1, C], BF16)
        ones_row = sb("ones_row", [1, 128], BF16)
        ones_col = sb("ones_col", [128, 1], BF16)
        zero_row = sb("zero_row", [1, 512], BF16)
        mask_rep = sb("mask_rep", [128, 512], BF16)
        x_sb = sb("x_sb", [128, 2, TPB, C], F32)
        r_sb = sb("r_sb", [128, TPB, C], F32)
        ln_tm = sb("ln_tm", [128, TPB, C], BF16)
        ln_fm = sb("ln_fm", [128, TPB, KC, 128], BF16)
        y_fm = ln_fm
        q_fm = sb("q_fm", [128, TPB, KC, 128], BF16)
        k_fm = sb("k_fm", [128, TPB, KC, 128], BF16)
        k_tm = sb("k_tm", [128, TPB, C], BF16)
        v_sb = sb("v_sb", [128, TPB, C], BF16)
        stm = sb("stm", [128, H, 128], BF16)
        k_mk = sb("k_mk", [128, 2, KC, 128], BF16)
        y_tm = sb("y_tm", [128, 1, C], BF16)
        q8_tm = sb("q8_tm", [128, TPB, C], FP8)
        e_sb = sb("e_sb", [128, 2, 512], BF16)
        rl_sb = sb("rl_sb", [128, 2, 512], BF16)
        kva = sb("kva", [128, H // 2, D + 1], BF16)
        ksum = sb("ksum", [128, 2, H // 2], F32)
        bn12 = sb("bn12", [128, TPB, 12], F32)
        mvab = sb("mvab", [128, TPB, 4], F32)
        mvs4 = sb("mvs4", [128, TPB, 2], F32)
        mvd4 = sb("mvd4", [128, TPB, 2], F32)
        mvp4 = sb("mvp4", [128, TPB, 2], F32)
        mean4 = sb("mean4", [128, TPB], F32)
        var4 = sb("var4", [128, TPB], F32)
        sqv = sb("sqv", [128, TPB], F32)
        invs = sb("invs", [128, TPB], F32)
        nmb = sb("nmb", [128, TPB], F32)
        invs2 = sb("invs2", [128, TPB], F32)
        nmb2 = sb("nmb2", [128, TPB], F32)
        den_f = sb("den_f", [128, 16], F32)
        rcp_f = sb("rcp_f", [128, 16], F32)
        c_eps5 = sb("c_eps5", [128, 1], F32)
        c_eps6 = sb("c_eps6", [128, 1], F32)
        c_m1 = sb("c_m1", [128, 1], F32)
        nc.const_aps.register(F32, LN_EPS, c_eps5[:])
        nc.const_aps.register(F32, ATT_EPS, c_eps6[:])
        nc.const_aps.register(F32, -1.0, c_m1[:])
        pbig = [ps("pbig0", [128, 512], F32), ps("pbig1", [128, 512], F32)]
        pnum = ps("pnum", [128, H, D], F32)
        pden = ps("pden", [128, 16], F32)
        pks = ps("pks", [128, 8], F32)
        pkv = ps("pkv", [128, H // 2, D], F32)

        s_x = [sem("s_x0"), sem("s_x1")]
        s_const = sem("s_const")
        s_w = sem("s_w")
        s_wv = sem("s_wv")
        s_wqk = sem("s_wqk")
        s_wpr = sem("s_wpr")
        s_stg = [sem("s_stg0"), sem("s_stg1")]
        s_trl = sem("s_trl")
        s_trk = sem("s_trk")
        s_try = sem("s_try")
        s_trq = sem("s_trq")
        s_mm = sem("s_mm")
        s_act = sem("s_act")
        s_dve = sem("s_dve")
        s_pl = sem("s_pl")
        s_out = sem("s_out")

        B = _Builder()
        op = B.op

        phi_hist = []
        ksum_read_act = {}
        ytr = {}
        div_dve_prev = 0
        kva_act_prev = 0
        ks_dve_prev = 0

        op("vector", lambda v: v.memset(ones_row[:], 1.0))
        op("vector", lambda v: v.memset(zero_row[:], 0.0))
        op("vector", lambda v: v.memset(ones_col[:], 1.0))
        op("vector", lambda v: v.memset(kva[:], 0.0))
        op("vector", lambda v: v.memset(k_mk[:], 0.0))
        op("vector", lambda v: v.memset(c_eps5[:], LN_EPS))
        op("vector", lambda v: v.memset(c_eps6[:], ATT_EPS))
        op("vector", lambda v: v.memset(c_m1[:], -1.0))
        op("vector", lambda v: v.memset(ksum[:], 0.0), incs=[(s_dve, 1)])

        op("sync", lambda s: s.dma_start(out=mask_rep[:], in_=maskh), incs=[(s_const, 16)])
        op("sync", lambda s: s.dma_start(out=bqk[:], in_=bqkh), incs=[(s_const, 16)])
        mask_loaded = s_const.count

        op("gpsimd", lambda g: g.dma_start(out=bv_row[:], in_=ba[2048:3072].rearrange("(a f) -> a f", a=1)),
           incs=[(s_w, 16)])
        op("gpsimd", lambda g: g.dma_start(out=bp_row[:], in_=bp.rearrange("(a f) -> a f", a=1)),
           incs=[(s_w, 16)])
        op("gpsimd", lambda g: g.dma_start(out=bf_row[:], in_=bf.rearrange("(a f) -> a f", a=1)),
           incs=[(s_w, 16)])
        w_bias = s_w.count
        for c in range(KC):
            op("gpsimd", lambda g, c=c: g.dma_start(out=W_v[:, c, :], in_=Wa[128 * c:128 * (c + 1), 2048:3072]),
               incs=[(s_wv, 16)])
        w_v_all = s_wv.count
        for c in range(KC):
            op("gpsimd", lambda g, c=c: g.dma_start(out=W_qk[:, c, :], in_=Wa[128 * c:128 * (c + 1), 0:2048]),
               incs=[(s_wqk, 16)])
        w_qk_all = s_wqk.count
        for c in range(KC):
            op("gpsimd", lambda g, c=c: g.dma_start(out=W_pr[:, c, :], in_=Wp[128 * c:128 * (c + 1), :]),
               incs=[(s_wpr, 16)])
        w_pr_all = s_wpr.count
        w_main = s_w.count

        w8_emitted = [False]


        bank_free = [[], []]
        x_loaded = {}
        out_done = {}
        ln_tr = {}
        ktm_tr = {}
        qk_ready = {}
        k_ready = {}
        v_ready = {}
        attn_done_mm = {}
        ffn_done_mm = {}
        q8_act = {}

        op("tensor", lambda t: t.matmul(
            pkv[:].rearrange("p a b -> p (a b)"), zero_row[0:1, 0:128], zero_row[0:1, 0:512],
            start=True, stop=False, skip_group_check=True), waits=[(s_dve, 1)])

        for b0 in range(min(2, NBLK)):
            for tt in range(TPB):
                t = b0 * TPB + tt
                op("sync", lambda s, b0=b0, tt=tt, t=t: s.dma_start(
                    out=x_sb[:, b0 % 2, tt, :], in_=xh[128 * t:128 * (t + 1), :]),
                   incs=[(s_x[b0 % 2], 16)])
            x_loaded[b0] = s_x[b0 % 2].count

        ln_sqrt_act = 0
        prev_q8_dve = 0

        for b in range(NBLK):
            bpar = b % 2
            tiles = [b * TPB + i for i in range(TPB)]

            swaits = [(s_x[bpar], x_loaded[b])] + ([(s_act, ln_sqrt_act)] if ln_sqrt_act else [])
            for i in range(TPB):
                op("vector", lambda v, i=i, bpar=bpar: v.bn_stats(bn12[:, i, 0:6], x_sb[:, bpar, i, 0:512]),
                   waits=swaits if i == 0 else [])
                op("vector", lambda v, i=i, bpar=bpar: v.bn_stats(bn12[:, i, 6:12], x_sb[:, bpar, i, 512:1024]),
                   incs=[(s_dve, 1)] if i == TPB - 1 else [])
            op("vector", lambda v, c=s_dve.count: v.wait_ge(s_dve.h, c))
            for i in range(TPB):
                op("vector", lambda v, i=i: v.bn_aggr(mvab[:, i, 0:2], bn12[:, i, 0:6]))
                op("vector", lambda v, i=i: v.bn_aggr(mvab[:, i, 2:4], bn12[:, i, 6:12]),
                   incs=[(s_dve, 1)] if i == TPB - 1 else [])
            op("vector", lambda v, c=s_dve.count: v.wait_ge(s_dve.h, c))
            op("vector", lambda v: v.tensor_tensor(mvs4[:], mvab[:, :, 0:2], mvab[:, :, 2:4], ALU.add))
            op("vector", lambda v: v.tensor_tensor(mvd4[:], mvab[:, :, 0:2], mvab[:, :, 2:4], ALU.subtract),
               incs=[(s_dve, 1)])
            op("vector", lambda v, c=s_dve.count: v.wait_ge(s_dve.h, c))
            op("vector", lambda v: v.tensor_tensor(mvp4[:], mvd4[:], mvd4[:], ALU.mult))
            op("vector", lambda v: v.tensor_scalar(mean4[:], mvs4[:, :, 0], 0.5, None, ALU.mult))
            op("vector", lambda v: v.tensor_scalar(var4[:], mvs4[:, :, 1], 0.5, None, ALU.mult),
               incs=[(s_dve, 1)])
            op("vector", lambda v, c=s_dve.count: v.wait_ge(s_dve.h, c))
            op("vector", lambda v: v.scalar_tensor_tensor(
                var4[:], mvp4[:, :, 0], 0.25, var4[:], ALU.mult, ALU.add), incs=[(s_dve, 1)])
            d_c = s_dve.count
            op("scalar", lambda s: s.activation(out=sqv[:], in_=var4[:], func=AF.Sqrt, bias=LN_EPS),
               waits=[(s_dve, d_c)], incs=[(s_act, 1)])
            ln_sqrt_act = s_act.count
            op("vector", lambda v: v.reciprocal(invs[:], sqv[:]),
               waits=[(s_act, ln_sqrt_act)], incs=[(s_dve, 1)])
            op("vector", lambda v, c=s_dve.count: v.wait_ge(s_dve.h, c))
            op("vector", lambda v: v.scalar_tensor_tensor(
                nmb[:], mean4[:], -1.0, invs[:], ALU.mult, ALU.mult),
               incs=[(s_dve, 1)])
            d_c = s_dve.count
            for i in range(TPB):
                waits = [(s_dve, d_c)]
                if b > 0 and BISECT >= 6:
                    waits.append((s_trq, q8_tr_prev))
                op("scalar", lambda s, i=i, bpar=bpar: s.activation(
                    out=ln_tm[:, i, :], in_=x_sb[:, bpar, i, :], func=AF.Identity,
                    bias=nmb[:, i:i + 1], scale=invs[:, i:i + 1]),
                   waits=waits if i == 0 else [], incs=[(s_act, 1)])
                twaits = []
                if b > 0 and BISECT >= 7 and i == 0:
                    twaits.append((s_mm, ffn_done_mm[b - 1]))
                op("scalar", lambda s, i=i: s.dma_start_transpose(
                    ln_fm[:, i, :, :], ln_tm[:, i, :].rearrange("p (c m) -> p c m", c=KC)),
                   waits=twaits, incs=[(s_trl, 16)])
            ln_tr[b] = s_trl.count

            if DEBUG_TAPS and b == 0:
                op("gpsimd", lambda g: g.dma_start(out=taps["t_lnfm"], in_=ln_fm[:]),
                   waits=[(s_trl, ln_tr[b])], incs=[(s_w, 16)])

            for i in range(TPB if BISECT >= 3 else 0):
                for half in range(2):
                    bank = (i * 2 + half) % 2
                    waits = [(s_trl, ln_tr[b]), (s_w, w_bias), (s_wv, w_v_all)] + bank_free[bank]
                    if b > 0 and BISECT >= 4:
                        waits.append((s_mm, attn_done_mm[b - 1]))
                    for kc in range(KC):
                        op("tensor", lambda t, i=i, half=half, kc=kc, bank=bank: t.matmul(
                            pbig[bank][:], ln_fm[:, i, kc, :],
                            W_v[:, kc, 512 * half:512 * (half + 1)],
                            start=(kc == 0), stop=False),
                           waits=waits if kc == 0 else [])
                        op("tensor", lambda t: t.nop())
                    op("tensor", lambda t: t.nop())
                    op("tensor", lambda t, half=half, bank=bank: t.matmul(
                        pbig[bank][:], ones_row[:, 0:128], bv_row[:, 512 * half:512 * (half + 1)],
                        start=False, stop=True), incs=[(s_mm, 1)])
                    mm_c = s_mm.count
                    op("tensor", lambda t: t.nop())
                    op("tensor", lambda t: t.nop())
                    op("scalar", lambda s, i=i, half=half, bank=bank: s.activation(
                        out=v_sb[:, i, 512 * half:512 * (half + 1)], in_=pbig[bank][:], func=AF.Copy),
                       waits=[(s_mm, mm_c)], incs=[(s_act, 1)])
                    bank_free[bank] = [(s_act, s_act.count)]
            v_ready[b] = s_act.count

            for fc in ([8, 9, 10, 11, 12, 13, 14, 15, 0, 1, 2, 3, 4, 5, 6, 7] if BISECT >= 2 else []):
                bank = fc % 2
                waits = [(s_trl, ln_tr[b]), (s_wqk, w_qk_all)] + bank_free[bank]
                for kc in range(KC):
                    op("tensor", lambda t, fc=fc, kc=kc, bank=bank: t.matmul(
                        pbig[bank][:], W_qk[:, kc, 128 * fc:128 * (fc + 1)], ln_fm[:, :, kc, :],
                        start=(kc == 0), stop=(kc == KC - 1)),
                       waits=waits if kc == 0 else [],
                       incs=[(s_mm, 1)] if kc == KC - 1 else [])
                    op("tensor", lambda t: t.nop())
                    op("tensor", lambda t: t.nop())
                mm_c = s_mm.count
                dst = q_fm[:, :, fc, :] if fc < 8 else k_fm[:, :, fc - 8, :]
                ep = fc % 2
                ewaits = [(s_mm, mm_c)]
                if len(phi_hist) >= 2:
                    ewaits.append((s_pl, phi_hist[-2]))
                op("scalar", lambda s, fc=fc, ep=ep, bank=bank: s.activation(
                    out=e_sb[:, ep, :], in_=pbig[bank][:], func=AF.Exp, bias=bqk[:, fc:fc + 1]),
                   waits=ewaits, incs=[(s_act, 1)])
                a_c = s_act.count
                op("vector", lambda v, fc=fc, ep=ep, bank=bank: v.tensor_scalar(
                    rl_sb[:, ep, :], pbig[bank][:], bqk[:, fc:fc + 1], 0.0, ALU.add, ALU.max),
                   waits=[(s_act, a_c)], incs=[(s_dve, 1)])
                d_c = s_dve.count
                bank_free[bank] = [(s_dve, d_c)]
                op("vector", lambda v, ep=ep: v.tensor_scalar(
                    e_sb[:, ep, :], e_sb[:, ep, :], 1.0, None, ALU.min),
                   waits=[(s_act, a_c)], incs=[(s_dve, 1)])
                op("vector", lambda v, c=s_dve.count: v.wait_ge(s_dve.h, c))
                op("vector", lambda v, ep=ep, dst=dst: v.tensor_tensor(
                    dst, e_sb[:, ep, :].rearrange("p (a m) -> p a m", a=TPB),
                    rl_sb[:, ep, :].rearrange("p (a m) -> p a m", a=TPB), ALU.add),
                   incs=[(s_pl, 1)])
                phi_hist.append(s_pl.count)
                if fc == 15:
                    k_ready[b] = s_pl.count
            qk_ready[b] = s_pl.count

            for i in range(min(TPB, NKT) if BISECT >= 4 else 0):
                waits = [(s_pl, k_ready[b])]
                if b > 0 and BISECT >= 4:
                    waits.append((s_mm, attn_done_mm[b - 1]))
                op("scalar", lambda s, i=i: s.dma_start_transpose(
                    k_tm[:, i, :].rearrange("p (c m) -> p c m", c=KC), k_fm[:, i, :, :]),
                   waits=waits, incs=[(s_trk, 16)])
            ktm_tr[b] = s_trk.count

            if not w8_emitted[0]:
                w8_emitted[0] = True
                w8_done = []
                for c in range(KC):
                    par = c % 2
                    waits = []
                    if c >= 2:
                        waits = [(s_act, w8_done[c - 2][0]), (s_dve, w8_done[c - 2][1])]
                    op("gpsimd", lambda g, c=c, par=par: g.dma_start(out=r_sb[:, par, :], in_=Wft[128 * c:128 * (c + 1), :]),
                       waits=waits, incs=[(s_stg[par], 16)])
                    stg_c = s_stg[par].count
                    op("scalar", lambda s, par=par, stg_c=stg_c: s.activation(out=q8_tm[:, par, :], in_=r_sb[:, par, :], func=AF.Copy),
                       waits=[(s_stg[par], stg_c)], incs=[(s_act, 1)])
                    a_c = s_act.count
                    op("vector", lambda v, c=c, par=par: v.tensor_copy(W_f8[:, c, :], q8_tm[:, par, :]),
                       waits=[(s_act, a_c)], incs=[(s_dve, 1)])
                    w8_done.append((a_c, s_dve.count))
                w8_all = s_dve.count
                w8_act_all = s_act.count


            for ci in range(TPB if BISECT >= 4 else 0):
                c = b * TPB + ci
                mask_dve = {}
                kmk_w = [(s_pl, k_ready[b])]
                if c > 0:
                    kmk_w.append((s_mm, chunk_mm))
                op("vector", lambda v, ci=ci: v.tensor_copy(
                    k_mk[0:64, 0, :, :], k_fm[0:64, ci, :, :]), waits=kmk_w)
                op("vector", lambda v, ci=ci: v.tensor_copy(
                    k_mk[64:128, 1, :, :], k_fm[64:128, ci, :, :]), incs=[(s_dve, 1)])
                kmk_dve = s_dve.count
                for qd in range(4 if SUB4 >= 1 else 0):
                    bank = qd % 2
                    waits = [(s_dve, kmk_dve), (s_trk, ktm_tr[b]),
                             (s_act, v_ready[b]), (s_const, mask_loaded)] + bank_free[bank]
                    for hh in range(4):
                        h = qd * 4 + hh
                        po = 64 * (h % 2)
                        op("tensor", lambda t, h=h, hh=hh, ci=ci, bank=bank: t.matmul(
                            pbig[bank][:, 128 * hh:128 * (hh + 1)],
                            k_mk[:, h % 2, h // 2, :],
                            q_fm[:, ci, h // 2, :],
                            start=(hh == 0), stop=(hh == 3)),
                           waits=waits if hh == 0 else [],
                           incs=[(s_mm, 1)] if hh == 3 else [])
                    st_mm = s_mm.count
                    op("vector", lambda v, qd=qd, bank=bank: v.scalar_tensor_tensor(
                        stm[:, 4 * qd:4 * (qd + 1), :], pbig[bank][:].rearrange("p (a m) -> p a m", a=4),
                        1.0, mask_rep[:].rearrange("p (a m) -> p a m", a=4), ALU.mult, ALU.mult),
                       waits=[(s_mm, st_mm)], incs=[(s_dve, 1)])
                    mask_dve[qd] = s_dve.count
                    bank_free[bank] = [(s_dve, mask_dve[qd])]
                op("tensor", lambda t: t.matmul(
                    pks[:], zero_row[0:1, 0:128], zero_row[0:1, 0:8],
                    start=True, stop=False, skip_group_check=True),
                   waits=[(s_dve, ks_dve_prev)] if c > 0 else [],
                   incs=[(s_mm, 1)] if SUB4 < 2 else [])
                for h in range(H if SUB4 >= 2 else 0):
                    po = 64 * (h % 2)
                    waits = [(s_dve, mask_dve[h // 4])]
                    if c > 0 and h == 0 and SUB4 >= 4:
                        waits.append((s_act, kva_act_prev))
                    op("tensor", lambda t, h=h, ci=ci: t.matmul(
                        pnum[:, h, :], stm[:, h, :], v_sb[:, ci, 64 * h:64 * (h + 1)],
                        start=(h % 8 == 0), stop=False), waits=waits)
                    op("tensor", lambda t, h=h, po=po, ci=ci: t.matmul(
                        pnum[:, h, :], q_fm[po:po + 64, ci, h // 2, :],
                        kva[po:po + 64, h // 2, 0:D], start=False, stop=(h % 8 == 7)))
                    op("tensor", lambda t, h=h, c=c: t.matmul(
                        pden[:, h:h + 1], stm[:, h, :], ones_col[:],
                        start=(h == 0), stop=False).annotate(f"den_intra c{c} h{h}"))
                    op("tensor", lambda t, h=h, po=po, ci=ci: t.matmul(
                        pden[:, h:h + 1], q_fm[po:po + 64, ci, h // 2, :],
                        kva[po:po + 64, h // 2, D:D + 1], start=False, stop=(h == H - 1)))
                    op("tensor", lambda t, h=h, po=po, ci=ci: t.matmul(
                        pkv[po:po + 64, h // 2, :], k_tm[:, ci, 64 * h:64 * (h + 1)],
                        v_sb[:, ci, 64 * h:64 * (h + 1)],
                        start=False, stop=False, skip_group_check=True))
                    op("tensor", lambda t, h=h, po=po, ci=ci: t.matmul(
                        pks[po:po + 64, h // 2:h // 2 + 1], k_tm[:, ci, 64 * h:64 * (h + 1)],
                        ones_col[:], start=False, stop=False, skip_group_check=True),
                       incs=[(s_mm, 1)] if h == H - 1 else [])
                chunk_mm = s_mm.count
                attn_done_mm[b] = chunk_mm

                if SUB4 < 2:
                    mask_dve = {0: s_dve.count, 1: s_dve.count, 2: s_dve.count, 3: s_dve.count}
                kpar, kparn = c % 2, (c + 1) % 2
                waits = [(s_mm, chunk_mm)]
                if c >= 2 and SUB4 >= 4:
                    waits.append((s_act, ksum_read_act[c - 2]))
                if SUB4 >= 3:
                    op("vector", lambda v: v.tensor_scalar(
                        den_f[:], pden[:, 0:16], ATT_EPS, None, ALU.add), waits=waits,
                       incs=[(s_dve, 1)])
                    op("vector", lambda v, c=s_dve.count: v.wait_ge(s_dve.h, c))
                    op("vector", lambda v: v.reciprocal(rcp_f[:], den_f[:]), incs=[(s_dve, 1)])
                    rcp_dve = s_dve.count
                if SUB4 >= 3:
                    op("vector", lambda v, kpar=kpar, kparn=kparn: v.tensor_tensor(
                        ksum[:, kpar, :], ksum[:, kparn, :], pks[:], ALU.add),
                       incs=[(s_dve, 1)])
                ks_dve = s_dve.count
                ks_dve_prev = ks_dve
                ypar = 0
                ywaits = []
                if c >= 1 and SUB4 >= 4:
                    ywaits.append((s_try, ytr[c - 1]))
                if SUB4 >= 3:
                    op("vector", lambda v, c=rcp_dve: v.wait_ge(s_dve.h, c))
                    op("vector", lambda v, ypar=ypar: v.tensor_tensor(
                        y_tm[:, ypar, :].rearrange("p (h d) -> p h d", h=H), pnum[:],
                        rcp_f[:, :, None].broadcast_to((128, 16, D)), ALU.mult),
                       waits=ywaits, incs=[(s_dve, 1)])
                div_dve_prev = s_dve.count

                if SUB4 >= 4:
                    op("scalar", lambda s: s.activation(out=kva[:, :, 0:D], in_=pkv[:], func=AF.Copy),
                       waits=[(s_mm, chunk_mm), (s_dve, ks_dve)])
                    op("scalar", lambda s, kpar=kpar: s.activation(
                        out=kva[:, :, D:D + 1], in_=ksum[:, kpar, :, None], func=AF.Copy),
                       incs=[(s_act, 1)])
                kva_act_prev = s_act.count
                ksum_read_act[c] = kva_act_prev

                if SUB4 >= 4:
                    op("scalar", lambda s, ci=ci, ypar=ypar: s.dma_start_transpose(
                        y_fm[:, ci, :, :], y_tm[:, ypar, :].rearrange("p (c m) -> p c m", c=KC)),
                       waits=[(s_dve, div_dve_prev)], incs=[(s_try, 16)])
                ytr[c] = s_try.count

            if DEBUG_TAPS and b == 0:
                def tapdma(dram, sbuf_ap, waits):
                    op("gpsimd", lambda g, dram=dram, sbuf_ap=sbuf_ap: g.dma_start(out=dram, in_=sbuf_ap),
                       waits=waits, incs=[(s_w, 16)])
                gate = [(s_mm, s_mm.count), (s_dve, s_dve.count), (s_act, s_act.count),
                        (s_try, s_try.count), (s_pl, s_pl.count)]
                tapdma(taps["t_lntm"], ln_tm[:], gate)
                tapdma(taps["t_xsb"], x_sb[:], [])
                tapdma(taps["t_qfm"], q_fm[:], [])
                tapdma(taps["t_kfm"], k_fm[:], [])
                tapdma(taps["t_ktm"], k_tm[:], [])
                tapdma(taps["t_vsb"], v_sb[:], [])
                tapdma(taps["t_kva"], kva[:], [])
                tapdma(taps["t_yfm"], y_fm[:], [])
                tapdma(taps["t_stm"], stm[:], [])

            for i in range(TPB if BISECT >= 5 else 0):
                for half in range(2):
                    bank = (i * 2 + half) % 2
                    waits = [(s_try, ytr[b * TPB + TPB - 1]), (s_w, w_bias), (s_wpr, w_pr_all)] + bank_free[bank]
                    for kc in range(KC):
                        op("tensor", lambda t, i=i, half=half, kc=kc, bank=bank: t.matmul(
                            pbig[bank][:], y_fm[:, i, kc, :],
                            W_pr[:, kc, 512 * half:512 * (half + 1)],
                            start=(kc == 0), stop=False),
                           waits=waits if kc == 0 else [])
                        op("tensor", lambda t: t.nop())
                    op("tensor", lambda t: t.nop())
                    op("tensor", lambda t, half=half, bank=bank: t.matmul(
                        pbig[bank][:], ones_row[:, 0:128], bp_row[:, 512 * half:512 * (half + 1)],
                        start=False, stop=True), incs=[(s_mm, 1)])
                    mm_c = s_mm.count
                    op("tensor", lambda t: t.nop())
                    op("tensor", lambda t: t.nop())
                    ewaits = [(s_mm, mm_c)]
                    if b == 0:
                        ewaits.append((s_act, w8_act_all))
                    if b > 0 and BISECT >= 6:
                        ewaits.append((s_act, q8_act[b - 1]))
                    op("vector", lambda v, i=i, half=half, bank=bank, bpar=bpar: v.tensor_tensor(
                        r_sb[:, i, 512 * half:512 * (half + 1)], pbig[bank][:],
                        x_sb[:, bpar, i, 512 * half:512 * (half + 1)], ALU.add),
                       waits=ewaits, incs=[(s_dve, 1)])
                    bank_free[bank] = [(s_dve, s_dve.count)]
            proj_last_mm = s_mm.count

            if BISECT >= 6:
                swaits = [(s_act, ln_sqrt_act)]
                for i in range(TPB):
                    op("vector", lambda v, i=i: v.bn_stats(bn12[:, i, 0:6], r_sb[:, i, 0:512]),
                       waits=swaits if i == 0 else [])
                    op("vector", lambda v, i=i: v.bn_stats(bn12[:, i, 6:12], r_sb[:, i, 512:1024]),
                       incs=[(s_dve, 1)] if i == TPB - 1 else [])
                op("vector", lambda v, c=s_dve.count: v.wait_ge(s_dve.h, c))
                for i in range(TPB):
                    op("vector", lambda v, i=i: v.bn_aggr(mvab[:, i, 0:2], bn12[:, i, 0:6]))
                    op("vector", lambda v, i=i: v.bn_aggr(mvab[:, i, 2:4], bn12[:, i, 6:12]),
                       incs=[(s_dve, 1)] if i == TPB - 1 else [])
                op("vector", lambda v, c=s_dve.count: v.wait_ge(s_dve.h, c))
                op("vector", lambda v: v.tensor_tensor(mvs4[:], mvab[:, :, 0:2], mvab[:, :, 2:4], ALU.add))
                op("vector", lambda v: v.tensor_tensor(mvd4[:], mvab[:, :, 0:2], mvab[:, :, 2:4], ALU.subtract),
                   incs=[(s_dve, 1)])
                op("vector", lambda v, c=s_dve.count: v.wait_ge(s_dve.h, c))
                op("vector", lambda v: v.tensor_tensor(mvp4[:], mvd4[:], mvd4[:], ALU.mult))
                op("vector", lambda v: v.tensor_scalar(mean4[:], mvs4[:, :, 0], 0.5, None, ALU.mult))
                op("vector", lambda v: v.tensor_scalar(var4[:], mvs4[:, :, 1], 0.5, None, ALU.mult),
                   incs=[(s_dve, 1)])
                op("vector", lambda v, c=s_dve.count: v.wait_ge(s_dve.h, c))
                op("vector", lambda v: v.scalar_tensor_tensor(
                    var4[:], mvp4[:, :, 0], 0.25, var4[:], ALU.mult, ALU.add), incs=[(s_dve, 1)])
                d_c = s_dve.count
                op("scalar", lambda s: s.activation(out=sqv[:], in_=var4[:], func=AF.Sqrt, bias=LN_EPS),
                   waits=[(s_dve, d_c)], incs=[(s_act, 1)])
                ln_sqrt_act = s_act.count
                op("vector", lambda v: v.reciprocal(invs2[:], sqv[:]),
                   waits=[(s_act, ln_sqrt_act)], incs=[(s_dve, 1)])
                op("vector", lambda v, c=s_dve.count: v.wait_ge(s_dve.h, c))
                op("vector", lambda v: v.scalar_tensor_tensor(
                    nmb2[:], mean4[:], -1.0, invs2[:], ALU.mult, ALU.mult),
                   incs=[(s_dve, 1)])
                d_c = s_dve.count
                for i in range(TPB):
                    op("scalar", lambda s, i=i: s.activation(
                        out=q8_tm[:, i, :], in_=r_sb[:, i, :], func=AF.Identity,
                        bias=nmb2[:, i:i + 1], scale=invs2[:, i:i + 1]),
                       waits=[(s_dve, d_c)] if i == 0 else [], incs=[(s_act, 1)])
                    a_c = s_act.count
                    op("vector", lambda v, i=i: v.tensor_copy(ln_tm[:, i, :], q8_tm[:, i, :]),
                       waits=[(s_act, a_c), (s_trl, ln_tr[b])], incs=[(s_dve, 1)])
                    d_c2 = s_dve.count
                    op("scalar", lambda s, i=i: s.dma_start_transpose(
                        ln_fm[:, i, :, :], ln_tm[:, i, :].rearrange("p (c m) -> p c m", c=KC)),
                       waits=[(s_dve, d_c2), (s_mm, proj_last_mm)], incs=[(s_trq, 16)])
            q8_tr = s_trq.count
            q8_tr_prev = q8_tr
            q8_act[b] = s_act.count

            for i in range(TPB if BISECT >= 7 else 0):
                for half in range(2):
                    bank = (i * 2 + half) % 2
                    waits = [(s_trq, q8_tr), (s_dve, w8_all)] + bank_free[bank]
                    for kc in range(KC):
                        op("tensor", lambda t, i=i, half=half, kc=kc, bank=bank: t.matmul(
                            pbig[bank][:], ln_fm[:, i, kc, :],
                            W_f8[:, kc, 512 * half:512 * (half + 1)],
                            start=(kc == 0), stop=False),
                           waits=waits if kc == 0 else [])
                        op("tensor", lambda t: t.nop())
                    op("tensor", lambda t: t.nop())
                    op("tensor", lambda t, half=half, bank=bank: t.matmul(
                        pbig[bank][:], ones_row[:, 0:128], bf_row[:, 512 * half:512 * (half + 1)],
                        start=False, stop=True), incs=[(s_mm, 1)])
                    mm_c = s_mm.count
                    op("tensor", lambda t: t.nop())
                    op("tensor", lambda t: t.nop())
                    if i == TPB - 1 and half == 1:
                        ffn_done_mm[b] = mm_c
                    op("vector", lambda v, i=i, half=half, bank=bank, bpar=bpar: v.tensor_tensor(
                        x_sb[:, bpar, i, 512 * half:512 * (half + 1)], pbig[bank][:],
                        r_sb[:, i, 512 * half:512 * (half + 1)], ALU.add),
                       waits=[(s_mm, mm_c)], incs=[(s_dve, 1)])
                    bank_free[bank] = [(s_dve, s_dve.count)]
                d_c = s_dve.count
                t = tiles[i]
                op("sync", lambda s, i=i, t=t, bpar=bpar: s.dma_start(
                    out=outh[128 * t:128 * (t + 1), :], in_=x_sb[:, bpar, i, :]),
                   waits=[(s_dve, d_c)], incs=[(s_out, 16)])
            if BISECT < 7:
                for i in range(TPB):
                    t = tiles[i]
                    op("sync", lambda s, i=i, t=t, bpar=bpar: s.dma_start(
                        out=outh[128 * t:128 * (t + 1), :], in_=x_sb[:, bpar, i, :]),
                       waits=[(s_dve, s_dve.count), (s_act, s_act.count), (s_pl, s_pl.count),
                              (s_mm, s_mm.count)] if i == 0 else [], incs=[(s_out, 16)])
            out_done[b] = s_out.count

            if b + 2 <= NBLK - 1:
                bn = b + 2
                for tt in range(TPB):
                    t = bn * TPB + tt
                    op("sync", lambda s, bn=bn, tt=tt, t=t: s.dma_start(
                        out=x_sb[:, bn % 2, tt, :], in_=xh[128 * t:128 * (t + 1), :]),
                       waits=[(s_out, out_done[b])] if tt == 0 else [], incs=[(s_x[bn % 2], 16)])
                x_loaded[bn] = s_x[bn % 2].count

        op("sync", lambda s: None, waits=[(s_out, s_out.count)])

        with nc.Block() as blk:
            def emit(eng_obj, name):
                for waits, fn, incs in B.eng_ops[name]:
                    for s, v in waits:
                        eng_obj.wait_ge(s.h, v)
                    inst = fn(eng_obj)
                    for j, (s, n) in enumerate(incs):
                        if j == 0 and inst is not None:
                            inst.then_inc(s.h, n)
                        else:
                            eng_obj.nop().then_inc(s.h, n)

            @blk.sync
            def _(e):
                emit(e, "sync")

            @blk.gpsimd
            def _(e):
                emit(e, "gpsimd")

            @blk.scalar
            def _(e):
                emit(e, "scalar")

            @blk.vector
            def _(e):
                emit(e, "vector")

            @blk.tensor
            def _(e):
                emit(e, "tensor")

    return nc


_NC_CACHE = None


def kernel(L0_attn_x, L0_attn_W_attn, L0_attn_b_attn, L0_attn_W_proj,
           L0_attn_b_proj, L0_ffn_weight, L0_ffn_bias):
    global _NC_CACHE
    x = np.ascontiguousarray(np.asarray(L0_attn_x, dtype=np.float32))
    Wa = np.ascontiguousarray(np.asarray(L0_attn_W_attn, dtype=np.float32))
    ba = np.ascontiguousarray(np.asarray(L0_attn_b_attn, dtype=np.float32))
    Wp = np.ascontiguousarray(np.asarray(L0_attn_W_proj, dtype=np.float32))
    bp = np.ascontiguousarray(np.asarray(L0_attn_b_proj, dtype=np.float32))
    Wf = np.ascontiguousarray(np.asarray(L0_ffn_weight, dtype=np.float32))
    bf = np.ascontiguousarray(np.asarray(L0_ffn_bias, dtype=np.float32))
    Wft = np.ascontiguousarray(Wf.T)
    mask = np.ascontiguousarray(
        np.tile(np.triu(np.ones((128, 128), np.float32)), (1, 4)).astype(ml_dtypes.bfloat16))
    bqk_h = np.ascontiguousarray(ba[:2048].reshape(16, 128).T)

    if _NC_CACHE is None:
        _NC_CACHE = build_nc()
    nc = _NC_CACHE
    in_maps = []
    for b in range(8):
        in_maps.append({"x": np.ascontiguousarray(x[b]), "W_attn": Wa, "b_attn": ba,
                        "W_proj": Wp, "b_proj": bp, "Wt_ffn": Wft, "b_ffn": bf,
                        "mask": mask, "bqk_h": bqk_h})
    res = run_bass_kernel_spmd(nc, in_maps, core_ids=list(range(8)))
    out = np.stack([np.asarray(res.results[b]["out"]) for b in range(8)], axis=0)
    return out.astype(np.float32)


# version 07
# speedup vs baseline: 1.0336x; 1.0336x over previous
import numpy as np
import ml_dtypes
import concourse.bass as bass
import concourse.mybir as mybir
from concourse.bass_utils import run_bass_kernel_spmd

F32 = mybir.dt.float32
BF16 = mybir.dt.bfloat16
FP8 = mybir.dt.float8e4
AF = mybir.ActivationFunctionType
ALU = mybir.AluOpType

T = 2048
C = 1024
H = 16
D = 64
KC = 8
NBLK = 4
TPB = 4
LN_EPS = 1e-5
ATT_EPS = 1e-6


class _Sem:
    def __init__(self, handle):
        self.h = handle
        self.count = 0


class _Builder:
    def __init__(self):
        self.eng_ops = {e: [] for e in ("sync", "scalar", "vector", "gpsimd", "tensor")}

    def op(self, eng, fn, waits=(), incs=()):
        for s, v in waits:
            assert v <= s.count, f"wait {v} > produced {s.count} ({eng})"
        self.eng_ops[eng].append((list(waits), fn, list(incs)))
        for s, n in incs:
            s.count += n


def build_nc():
    nc = bass.Bass("TRN2", target_bir_lowering=False, detect_race_conditions=False)

    xh = nc.dram_tensor("x", [T, C], F32, kind="ExternalInput").ap()
    Wa = nc.dram_tensor("W_attn", [C, 3 * C], F32, kind="ExternalInput").ap()
    ba = nc.dram_tensor("b_attn", [3 * C], F32, kind="ExternalInput").ap()
    Wp = nc.dram_tensor("W_proj", [C, C], F32, kind="ExternalInput").ap()
    bp = nc.dram_tensor("b_proj", [C], F32, kind="ExternalInput").ap()
    Wft = nc.dram_tensor("Wt_ffn", [C, C], F32, kind="ExternalInput").ap()
    bf = nc.dram_tensor("b_ffn", [C], F32, kind="ExternalInput").ap()
    maskh = nc.dram_tensor("mask", [128, 512], BF16, kind="ExternalInput").ap()
    bqkh = nc.dram_tensor("bqk_h", [128, 16], F32, kind="ExternalInput").ap()
    outh = nc.dram_tensor("out", [T, C], F32, kind="ExternalOutput").ap()

    import contextlib
    ctx = contextlib.ExitStack()
    sb = lambda name, shape, dt: ctx.enter_context(nc.sbuf_tensor(name, shape, dt))
    ps = lambda name, shape, dt: ctx.enter_context(nc.psum_tensor(name, shape, dt))
    sem = lambda name: _Sem(ctx.enter_context(nc.semaphore(name)))

    with ctx:
        W_qk = sb("W_qk", [128, KC, 2 * C], BF16)
        W_v = sb("W_v", [128, KC, C], BF16)
        W_pr = sb("W_pr", [128, KC, C], BF16)
        W_f8 = sb("W_f8", [128, KC, C], BF16)
        bqk = sb("bqk", [128, 16], F32)
        bv_row = sb("bv_row", [1, C], BF16)
        bp_row = sb("bp_row", [1, C], BF16)
        bf_row = sb("bf_row", [1, C], BF16)
        ones_row = sb("ones_row", [1, 128], BF16)
        ones_col = sb("ones_col", [128, 1], BF16)
        zero_row = sb("zero_row", [1, 512], BF16)
        mask_rep = sb("mask_rep", [128, 512], BF16)
        x_sb = sb("x_sb", [128, 2, TPB, C], F32)
        r_sb = sb("r_sb", [128, TPB, C], F32)
        ln_tm = sb("ln_tm", [128, TPB, C], BF16)
        ln_fm = sb("ln_fm", [128, TPB, KC, 128], BF16)
        y_fm = ln_fm
        q_fm = sb("q_fm", [128, TPB, KC, 128], BF16)
        k_fm = sb("k_fm", [128, TPB, KC, 128], BF16)
        k_tm = sb("k_tm", [128, TPB, C], BF16)
        v_sb = sb("v_sb", [128, TPB, C], BF16)
        k_mk = sb("k_mk", [128, 2, KC, 128], BF16)
        stm = sb("stm", [128, H, 128], BF16)
        y_tm = sb("y_tm", [128, C], BF16)
        q8_tm = sb("q8_tm", [128, TPB, C], FP8)
        e_sb = sb("e_sb", [128, 2, 512], BF16)
        rl_sb = sb("rl_sb", [128, 2, 512], BF16)
        kva = sb("kva", [128, H // 2, D + 1], BF16)
        ksum = sb("ksum", [128, 2, H // 2], F32)
        stats = []
        for tag in ("a", "b"):
            stats.append(dict(
                bn12=sb(f"bn12{tag}", [128, TPB, 12], F32),
                mvab=sb(f"mvab{tag}", [128, TPB, 4], F32),
                mvs4=sb(f"mvs4{tag}", [128, TPB, 2], F32),
                mvd4=sb(f"mvd4{tag}", [128, TPB, 2], F32),
                mvp4=sb(f"mvp4{tag}", [128, TPB, 2], F32),
                mean4=sb(f"mean4{tag}", [128, TPB], F32),
                var4=sb(f"var4{tag}", [128, TPB], F32),
                sqv=sb(f"sqv{tag}", [128, TPB], F32),
                invs=sb(f"invs{tag}", [128, TPB], F32),
                nmb=sb(f"nmb{tag}", [128, TPB], F32),
                sqrt_act=0,
            ))
        den_f = sb("den_f", [128, 16], F32)
        rcp_f = sb("rcp_f", [128, 16], F32)
        c_eps5 = sb("c_eps5", [128, 1], F32)
        c_eps6 = sb("c_eps6", [128, 1], F32)
        c_m1 = sb("c_m1", [128, 1], F32)
        nc.const_aps.register(F32, LN_EPS, c_eps5[:])
        nc.const_aps.register(F32, ATT_EPS, c_eps6[:])
        nc.const_aps.register(F32, -1.0, c_m1[:])
        pbig = [ps("pbig0", [128, 512], F32), ps("pbig1", [128, 512], F32)]
        pnum = ps("pnum", [128, H, D], F32)
        pden = ps("pden", [128, 16], F32)
        pks = ps("pks", [128, 8], F32)
        pkv = ps("pkv", [128, H // 2, D], F32)

        s_x = [sem("s_x0"), sem("s_x1")]
        s_const = sem("s_const")
        s_w = sem("s_w")
        s_wv = sem("s_wv")
        s_wqk = sem("s_wqk")
        s_wpr = sem("s_wpr")
        s_stg = [sem("s_stg0"), sem("s_stg1")]
        s_trl = sem("s_trl")
        s_trk = sem("s_trk")
        s_try = sem("s_try")
        s_trq = sem("s_trq")
        s_mm = sem("s_mm")
        s_act = sem("s_act")
        s_dve = sem("s_dve")
        s_pl = sem("s_pl")
        s_out = sem("s_out")

        B = _Builder()
        op = B.op

        st = dict(
            x_loaded={}, out_done={}, ln_tr={}, ktm_tr={}, ytr={},
            qk_ready={}, k_ready={}, v_ready={}, q8_tr={}, q8_act={},
            attn_mm={}, proj_mm={}, ffn_mm={}, nm_dve={},
            phi_hist=[], div_dve=0, kva_act=0, ks_dve=0, ksum_read_act={},
            chunk_mm=0, w8_all=0, w8_act_all=0,
        )
        bank_free = [[], []]

        op("vector", lambda v: v.memset(ones_row[:], 1.0))
        op("vector", lambda v: v.memset(zero_row[:], 0.0))
        op("vector", lambda v: v.memset(ones_col[:], 1.0))
        op("vector", lambda v: v.memset(kva[:], 0.0))
        op("vector", lambda v: v.memset(k_mk[:], 0.0))
        op("vector", lambda v: v.memset(c_eps5[:], LN_EPS))
        op("vector", lambda v: v.memset(c_eps6[:], ATT_EPS))
        op("vector", lambda v: v.memset(c_m1[:], -1.0))
        op("vector", lambda v: v.memset(ksum[:], 0.0), incs=[(s_dve, 1)])

        op("sync", lambda s: s.dma_start(out=mask_rep[:], in_=maskh), incs=[(s_const, 16)])
        op("sync", lambda s: s.dma_start(out=bqk[:], in_=bqkh), incs=[(s_const, 16)])
        consts_ld = s_const.count

        op("gpsimd", lambda g: g.dma_start(out=bv_row[:], in_=ba[2048:3072].rearrange("(a f) -> a f", a=1)),
           incs=[(s_w, 16)])
        op("gpsimd", lambda g: g.dma_start(out=bp_row[:], in_=bp.rearrange("(a f) -> a f", a=1)),
           incs=[(s_w, 16)])
        op("gpsimd", lambda g: g.dma_start(out=bf_row[:], in_=bf.rearrange("(a f) -> a f", a=1)),
           incs=[(s_w, 16)])
        w_bias = s_w.count
        for c in range(KC):
            op("gpsimd", lambda g, c=c: g.dma_start(out=W_v[:, c, :], in_=Wa[128 * c:128 * (c + 1), 2048:3072]),
               incs=[(s_wv, 16)])
        w_v_all = s_wv.count
        for c in range(KC):
            op("gpsimd", lambda g, c=c: g.dma_start(out=W_qk[:, c, :], in_=Wa[128 * c:128 * (c + 1), 0:2048]),
               incs=[(s_wqk, 16)])
        w_qk_all = s_wqk.count
        for c in range(KC):
            op("gpsimd", lambda g, c=c: g.dma_start(out=W_pr[:, c, :], in_=Wp[128 * c:128 * (c + 1), :]),
               incs=[(s_wpr, 16)])
        w_pr_all = s_wpr.count

        op("tensor", lambda t: t.matmul(
            pkv[:].rearrange("p a b -> p (a b)"), zero_row[0:1, 0:128], zero_row[0:1, 0:512],
            start=True, stop=False, skip_group_check=True), waits=[(s_dve, 1)])

        for b0 in range(min(2, NBLK)):
            for tt in range(TPB):
                t = b0 * TPB + tt
                op("sync", lambda s, b0=b0, tt=tt, t=t: s.dma_start(
                    out=x_sb[:, b0 % 2, tt, :], in_=xh[128 * t:128 * (t + 1), :]),
                   incs=[(s_x[b0 % 2], 16)])
            st["x_loaded"][b0] = s_x[b0 % 2].count

        def emit_stats(S, src_ap_fn, first_waits):
            for i in range(TPB):
                op("vector", lambda v, i=i: v.bn_stats(S["bn12"][:, i, 0:6], src_ap_fn(i)[:, 0:512]),
                   waits=first_waits if i == 0 else [])
                op("vector", lambda v, i=i: v.bn_stats(S["bn12"][:, i, 6:12], src_ap_fn(i)[:, 512:1024]),
                   incs=[(s_dve, 1)] if i == TPB - 1 else [])
            op("vector", lambda v, c=s_dve.count: v.wait_ge(s_dve.h, c))
            for i in range(TPB):
                op("vector", lambda v, i=i: v.bn_aggr(S["mvab"][:, i, 0:2], S["bn12"][:, i, 0:6]))
                op("vector", lambda v, i=i: v.bn_aggr(S["mvab"][:, i, 2:4], S["bn12"][:, i, 6:12]),
                   incs=[(s_dve, 1)] if i == TPB - 1 else [])
            op("vector", lambda v, c=s_dve.count: v.wait_ge(s_dve.h, c))
            op("vector", lambda v: v.tensor_tensor(S["mvs4"][:], S["mvab"][:, :, 0:2], S["mvab"][:, :, 2:4], ALU.add))
            op("vector", lambda v: v.tensor_tensor(S["mvd4"][:], S["mvab"][:, :, 0:2], S["mvab"][:, :, 2:4], ALU.subtract),
               incs=[(s_dve, 1)])
            op("vector", lambda v, c=s_dve.count: v.wait_ge(s_dve.h, c))
            op("vector", lambda v: v.tensor_tensor(S["mvp4"][:], S["mvd4"][:], S["mvd4"][:], ALU.mult))
            op("vector", lambda v: v.tensor_scalar(S["mean4"][:], S["mvs4"][:, :, 0], 0.5, None, ALU.mult))
            op("vector", lambda v: v.tensor_scalar(S["var4"][:], S["mvs4"][:, :, 1], 0.5, None, ALU.mult),
               incs=[(s_dve, 1)])
            op("vector", lambda v, c=s_dve.count: v.wait_ge(s_dve.h, c))
            op("vector", lambda v: v.scalar_tensor_tensor(
                S["var4"][:], S["mvp4"][:, :, 0], 0.25, S["var4"][:], ALU.mult, ALU.add), incs=[(s_dve, 1)])
            d_c = s_dve.count
            op("scalar", lambda s: s.activation(out=S["sqv"][:], in_=S["var4"][:], func=AF.Sqrt, bias=LN_EPS),
               waits=[(s_dve, d_c)], incs=[(s_act, 1)])
            S["sqrt_act"] = s_act.count
            op("vector", lambda v: v.reciprocal(S["invs"][:], S["sqv"][:]),
               waits=[(s_act, S["sqrt_act"])], incs=[(s_dve, 1)])
            op("vector", lambda v, c=s_dve.count: v.wait_ge(s_dve.h, c))
            op("vector", lambda v: v.scalar_tensor_tensor(
                S["nmb"][:], S["mean4"][:], -1.0, S["invs"][:], ALU.mult, ALU.mult),
               incs=[(s_dve, 1)])
            return s_dve.count

        def emit_ln1(b):
            bpar = b % 2
            S = stats[0]
            fw = [(s_x[bpar], st["x_loaded"][b])]
            if b > 1:
                fw.append((s_act, S["sqrt_act"]))
            d_c = emit_stats(S, lambda i, bpar=bpar: x_sb[:, bpar, i, :], fw)
            for i in range(TPB):
                waits = [(s_dve, d_c)]
                if b > 0 and i == 0:
                    waits.append((s_trl, st["ln_tr"][b - 1]))
                op("scalar", lambda s, i=i, bpar=bpar, S=S: s.activation(
                    out=ln_tm[:, i, :], in_=x_sb[:, bpar, i, :], func=AF.Identity,
                    bias=S["nmb"][:, i:i + 1], scale=S["invs"][:, i:i + 1]),
                   waits=waits, incs=[(s_act, 1)])
                twaits = []
                if b > 0 and i == 0:
                    twaits.append((s_mm, st["proj_mm"][b - 1]))
                op("scalar", lambda s, i=i: s.dma_start_transpose(
                    ln_fm[:, i, :, :], ln_tm[:, i, :].rearrange("p (c m) -> p c m", c=KC)),
                   waits=twaits, incs=[(s_trl, 16)])
            st["ln_tr"][b] = s_trl.count

        def big_nops():
            op("tensor", lambda t: t.nop())
            op("tensor", lambda t: t.nop())

        def emit_v(b):
            for i in range(TPB):
                for half in range(2):
                    bank = (i * 2 + half) % 2
                    waits = [(s_trl, st["ln_tr"][b]), (s_w, w_bias), (s_wv, w_v_all)] + bank_free[bank]
                    for kc in range(KC):
                        op("tensor", lambda t, i=i, half=half, kc=kc, bank=bank: t.matmul(
                            pbig[bank][:], ln_fm[:, i, kc, :], W_v[:, kc, 512 * half:512 * (half + 1)],
                            start=(kc == 0), stop=False),
                           waits=waits if kc == 0 else [])
                        big_nops()
                    op("tensor", lambda t, half=half, bank=bank: t.matmul(
                        pbig[bank][:], ones_row[:, 0:128], bv_row[:, 512 * half:512 * (half + 1)],
                        start=False, stop=True), incs=[(s_mm, 1)])
                    mm_c = s_mm.count
                    big_nops()
                    ewaits = [(s_mm, mm_c)]
                    if b > 0 and i == 0 and half == 0:
                        ewaits.append((s_trq, st["q8_tr"][b - 1]))
                    op("scalar", lambda s, i=i, half=half, bank=bank: s.activation(
                        out=v_sb[:, i, 512 * half:512 * (half + 1)], in_=pbig[bank][:], func=AF.Copy),
                       waits=ewaits, incs=[(s_act, 1)])
                    bank_free[bank] = [(s_act, s_act.count)]
            st["v_ready"][b] = s_act.count

        def emit_qk(b):
            for fc in [8, 9, 10, 11, 12, 13, 14, 15, 0, 1, 2, 3, 4, 5, 6, 7]:
                bank = fc % 2
                waits = [(s_trl, st["ln_tr"][b]), (s_wqk, w_qk_all)] + bank_free[bank]
                for kc in range(KC):
                    op("tensor", lambda t, fc=fc, kc=kc, bank=bank: t.matmul(
                        pbig[bank][:], W_qk[:, kc, 128 * fc:128 * (fc + 1)], ln_fm[:, :, kc, :],
                        start=(kc == 0), stop=(kc == KC - 1)),
                       waits=waits if kc == 0 else [],
                       incs=[(s_mm, 1)] if kc == KC - 1 else [])
                    big_nops()
                mm_c = s_mm.count
                dst = q_fm[:, :, fc, :] if fc < 8 else k_fm[:, :, fc - 8, :]
                ep = fc % 2
                ewaits = [(s_mm, mm_c)]
                if len(st["phi_hist"]) >= 2:
                    ewaits.append((s_pl, st["phi_hist"][-2]))
                op("scalar", lambda s, fc=fc, ep=ep, bank=bank: s.activation(
                    out=e_sb[:, ep, :], in_=pbig[bank][:], func=AF.Exp, bias=bqk[:, fc:fc + 1]),
                   waits=ewaits, incs=[(s_act, 1)])
                a_c = s_act.count
                op("vector", lambda v, fc=fc, ep=ep, bank=bank: v.tensor_scalar(
                    rl_sb[:, ep, :], pbig[bank][:], bqk[:, fc:fc + 1], 0.0, ALU.add, ALU.max),
                   waits=[(s_act, a_c)], incs=[(s_dve, 1)])
                d_c = s_dve.count
                bank_free[bank] = [(s_dve, d_c)]
                pw = [(s_act, a_c)]
                if b > 0 and fc == 8:
                    pw.append((s_mm, st["attn_mm"][b - 1]))
                op("vector", lambda v, ep=ep: v.tensor_scalar(
                    e_sb[:, ep, :], e_sb[:, ep, :], 1.0, None, ALU.min),
                   waits=pw, incs=[(s_dve, 1)])
                op("vector", lambda v, c=s_dve.count: v.wait_ge(s_dve.h, c))
                op("vector", lambda v, ep=ep, dst=dst: v.tensor_tensor(
                    dst, e_sb[:, ep, :].rearrange("p (a m) -> p a m", a=TPB),
                    rl_sb[:, ep, :].rearrange("p (a m) -> p a m", a=TPB), ALU.add),
                   incs=[(s_pl, 1)])
                st["phi_hist"].append(s_pl.count)
                if fc == 15:
                    st["k_ready"][b] = s_pl.count
            st["qk_ready"][b] = s_pl.count

        def emit_ktm(b):
            for i in range(TPB):
                waits = [(s_pl, st["k_ready"][b])]
                if b > 0 and i == 0:
                    waits.append((s_mm, st["ffn_mm"][b - 1]))
                op("scalar", lambda s, i=i: s.dma_start_transpose(
                    k_tm[:, i, :].rearrange("p (c m) -> p c m", c=KC), k_fm[:, i, :, :]),
                   waits=waits, incs=[(s_trk, 16)])
            st["ktm_tr"][b] = s_trk.count

        def emit_w8():
            w8_done = []
            for c in range(KC):
                par = c % 2
                waits = []
                if c >= 2:
                    waits = [(s_act, w8_done[c - 2][0]), (s_dve, w8_done[c - 2][1])]
                op("gpsimd", lambda g, c=c, par=par: g.dma_start(
                    out=r_sb[:, par, :], in_=Wft[128 * c:128 * (c + 1), :]),
                   waits=waits, incs=[(s_stg[par], 16)])
                stg_c = s_stg[par].count
                op("scalar", lambda s, par=par, stg_c=stg_c: s.activation(
                    out=q8_tm[:, par, :], in_=r_sb[:, par, :], func=AF.Copy),
                   waits=[(s_stg[par], stg_c)], incs=[(s_act, 1)])
                a_c = s_act.count
                op("vector", lambda v, c=c, par=par: v.tensor_copy(W_f8[:, c, :], q8_tm[:, par, :]),
                   waits=[(s_act, a_c)], incs=[(s_dve, 1)])
                w8_done.append((a_c, s_dve.count))
            st["w8_all"] = s_dve.count
            st["w8_act_all"] = s_act.count

        def emit_attn(b):
            for ci in range(TPB):
                c = b * TPB + ci
                kmk_w = [(s_pl, st["k_ready"][b])]
                if c > 0:
                    kmk_w.append((s_mm, st["chunk_mm"]))
                op("vector", lambda v, ci=ci: v.tensor_copy(
                    k_mk[0:64, 0, :, :], k_fm[0:64, ci, :, :]), waits=kmk_w)
                op("vector", lambda v, ci=ci: v.tensor_copy(
                    k_mk[64:128, 1, :, :], k_fm[64:128, ci, :, :]), incs=[(s_dve, 1)])
                kmk_dve = s_dve.count
                op("tensor", lambda t: t.matmul(
                    pks[:], zero_row[0:1, 0:128], zero_row[0:1, 0:8],
                    start=True, stop=False, skip_group_check=True),
                   waits=[(s_dve, st["ks_dve"])] if c > 0 else [])
                mask_dve = {}
                for qd in range(4):
                    bank = qd % 2
                    waits = [(s_dve, kmk_dve), (s_trk, st["ktm_tr"][b]),
                             (s_act, st["v_ready"][b]), (s_const, consts_ld)] + bank_free[bank]
                    for hh in range(4):
                        h = qd * 4 + hh
                        op("tensor", lambda t, h=h, hh=hh, ci=ci, bank=bank: t.matmul(
                            pbig[bank][:, 128 * hh:128 * (hh + 1)],
                            k_mk[:, h % 2, h // 2, :], q_fm[:, ci, h // 2, :],
                            start=(hh == 0), stop=(hh == 3)),
                           waits=waits if hh == 0 else [],
                           incs=[(s_mm, 1)] if hh == 3 else [])
                    st_mm = s_mm.count
                    op("vector", lambda v, qd=qd, bank=bank: v.scalar_tensor_tensor(
                        stm[:, 4 * qd:4 * (qd + 1), :], pbig[bank][:].rearrange("p (a m) -> p a m", a=4),
                        1.0, mask_rep[:].rearrange("p (a m) -> p a m", a=4), ALU.mult, ALU.mult),
                       waits=[(s_mm, st_mm)], incs=[(s_dve, 1)])
                    mask_dve[qd] = s_dve.count
                    bank_free[bank] = [(s_dve, mask_dve[qd])]
                for h in range(H):
                    po = 64 * (h % 2)
                    waits = [(s_dve, mask_dve[h // 4])]
                    if c > 0 and h == 0:
                        waits.append((s_act, st["kva_act"]))
                    op("tensor", lambda t, h=h, ci=ci: t.matmul(
                        pnum[:, h, :], stm[:, h, :], v_sb[:, ci, 64 * h:64 * (h + 1)],
                        start=(h % 8 == 0), stop=False), waits=waits)
                    op("tensor", lambda t, h=h, po=po, ci=ci: t.matmul(
                        pnum[:, h, :], q_fm[po:po + 64, ci, h // 2, :],
                        kva[po:po + 64, h // 2, 0:D], start=False, stop=(h % 8 == 7)))
                    op("tensor", lambda t, h=h: t.matmul(
                        pden[:, h:h + 1], stm[:, h, :], ones_col[:],
                        start=(h == 0), stop=False))
                    op("tensor", lambda t, h=h, po=po, ci=ci: t.matmul(
                        pden[:, h:h + 1], q_fm[po:po + 64, ci, h // 2, :],
                        kva[po:po + 64, h // 2, D:D + 1], start=False, stop=(h == H - 1)))
                    op("tensor", lambda t, h=h, po=po, ci=ci: t.matmul(
                        pkv[po:po + 64, h // 2, :], k_tm[:, ci, 64 * h:64 * (h + 1)],
                        v_sb[:, ci, 64 * h:64 * (h + 1)],
                        start=False, stop=False, skip_group_check=True))
                    op("tensor", lambda t, h=h, po=po, ci=ci: t.matmul(
                        pks[po:po + 64, h // 2:h // 2 + 1], k_tm[:, ci, 64 * h:64 * (h + 1)],
                        ones_col[:], start=False, stop=False, skip_group_check=True),
                       incs=[(s_mm, 1)] if h == H - 1 else [])
                st["chunk_mm"] = s_mm.count
                st["attn_mm"][b] = st["chunk_mm"]
                kpar, kparn = c % 2, (c + 1) % 2
                waits = [(s_mm, st["chunk_mm"])]
                if c >= 2:
                    waits.append((s_act, st["ksum_read_act"][c - 2]))
                op("vector", lambda v: v.tensor_scalar(
                    den_f[:], pden[:], ATT_EPS, None, ALU.add), waits=waits, incs=[(s_dve, 1)])
                op("vector", lambda v, c_=s_dve.count: v.wait_ge(s_dve.h, c_))
                op("vector", lambda v: v.reciprocal(rcp_f[:], den_f[:]), incs=[(s_dve, 1)])
                rcp_dve = s_dve.count
                op("vector", lambda v, kpar=kpar, kparn=kparn: v.tensor_tensor(
                    ksum[:, kpar, :], ksum[:, kparn, :], pks[:], ALU.add),
                   incs=[(s_dve, 1)])
                st["ks_dve"] = s_dve.count
                ywaits = [(s_dve, rcp_dve)]
                if c >= 1:
                    ywaits.append((s_try, st["ytr"][c - 1]))
                op("vector", lambda v: v.tensor_tensor(
                    y_tm[:].rearrange("p (h d) -> p h d", h=H), pnum[:],
                    rcp_f[:, :, None].broadcast_to((128, 16, D)), ALU.mult),
                   waits=ywaits, incs=[(s_dve, 1)])
                div_dve = s_dve.count
                st["div_dve"] = div_dve
                op("scalar", lambda s: s.activation(out=kva[:, :, 0:D], in_=pkv[:], func=AF.Copy),
                   waits=[(s_mm, st["chunk_mm"]), (s_dve, st["ks_dve"])])
                op("scalar", lambda s, kpar=kpar: s.activation(
                    out=kva[:, :, D:D + 1], in_=ksum[:, kpar, :, None], func=AF.Copy),
                   incs=[(s_act, 1)])
                st["kva_act"] = s_act.count
                st["ksum_read_act"][c] = st["kva_act"]
                op("scalar", lambda s, ci=ci: s.dma_start_transpose(
                    y_fm[:, ci, :, :], y_tm[:].rearrange("p (c m) -> p c m", c=KC)),
                   waits=[(s_dve, div_dve)], incs=[(s_try, 16)])
                st["ytr"][c] = s_try.count

        def emit_proj(b):
            bpar = b % 2
            for i in range(TPB):
                for half in range(2):
                    bank = (i * 2 + half) % 2
                    waits = [(s_try, st["ytr"][b * TPB + TPB - 1]), (s_w, w_bias),
                             (s_wpr, w_pr_all)] + bank_free[bank]
                    for kc in range(KC):
                        op("tensor", lambda t, i=i, half=half, kc=kc, bank=bank: t.matmul(
                            pbig[bank][:], y_fm[:, i, kc, :], W_pr[:, kc, 512 * half:512 * (half + 1)],
                            start=(kc == 0), stop=False),
                           waits=waits if kc == 0 else [])
                        big_nops()
                    op("tensor", lambda t, half=half, bank=bank: t.matmul(
                        pbig[bank][:], ones_row[:, 0:128], bp_row[:, 512 * half:512 * (half + 1)],
                        start=False, stop=True), incs=[(s_mm, 1)])
                    mm_c = s_mm.count
                    big_nops()
                    ewaits = [(s_mm, mm_c)]
                    if b == 0 and i == 0 and half == 0:
                        ewaits.append((s_act, st["w8_act_all"]))
                    if b > 0 and i == 0 and half == 0:
                        ewaits.append((s_act, st["q8_act"][b - 1]))
                    op("vector", lambda v, i=i, half=half, bank=bank, bpar=bpar: v.tensor_tensor(
                        r_sb[:, i, 512 * half:512 * (half + 1)], pbig[bank][:],
                        x_sb[:, bpar, i, 512 * half:512 * (half + 1)], ALU.add),
                       waits=ewaits, incs=[(s_dve, 1)])
                    bank_free[bank] = [(s_dve, s_dve.count)]
            st["proj_mm"][b] = s_mm.count

        def emit_ln2q8(b):
            S = stats[1]
            fw = []
            if b > 0:
                fw.append((s_act, S["sqrt_act"]))
            d_c = emit_stats(S, lambda i: r_sb[:, i, :], fw)
            for i in range(TPB):
                op("scalar", lambda s, i=i, S=S: s.activation(
                    out=q8_tm[:, i, :], in_=r_sb[:, i, :], func=AF.Identity,
                    bias=S["nmb"][:, i:i + 1], scale=S["invs"][:, i:i + 1]),
                   waits=[(s_dve, d_c)] if i == 0 else [], incs=[(s_act, 1)])
                a_c = s_act.count
                cw = [(s_act, a_c)]
                if i == 0:
                    cw.append((s_mm, st["attn_mm"][b]))
                op("vector", lambda v, i=i: v.tensor_copy(v_sb[:, i, :], q8_tm[:, i, :]),
                   waits=cw, incs=[(s_dve, 1)])
                d_c2 = s_dve.count
                op("scalar", lambda s, i=i: s.dma_start_transpose(
                    k_tm[:, i, :].rearrange("p (c m) -> p c m", c=KC),
                    v_sb[:, i, :].rearrange("p (c m) -> p c m", c=KC)),
                   waits=[(s_dve, d_c2)], incs=[(s_trq, 16)])
            st["q8_tr"][b] = s_trq.count
            st["q8_act"][b] = s_act.count

        def emit_ffn(b):
            bpar = b % 2
            for i in range(TPB):
                for half in range(2):
                    bank = (i * 2 + half) % 2
                    waits = [(s_trq, st["q8_tr"][b]), (s_dve, st["w8_all"]),
                             (s_w, w_bias)] + bank_free[bank]
                    for kc in range(KC):
                        op("tensor", lambda t, i=i, half=half, kc=kc, bank=bank: t.matmul(
                            pbig[bank][:], k_tm[:, i, 128 * kc:128 * (kc + 1)],
                            W_f8[:, kc, 512 * half:512 * (half + 1)],
                            start=(kc == 0), stop=False),
                           waits=waits if kc == 0 else [])
                        big_nops()
                    op("tensor", lambda t, half=half, bank=bank: t.matmul(
                        pbig[bank][:], ones_row[:, 0:128], bf_row[:, 512 * half:512 * (half + 1)],
                        start=False, stop=True), incs=[(s_mm, 1)])
                    mm_c = s_mm.count
                    big_nops()
                    op("vector", lambda v, i=i, half=half, bank=bank, bpar=bpar: v.tensor_tensor(
                        x_sb[:, bpar, i, 512 * half:512 * (half + 1)], pbig[bank][:],
                        r_sb[:, i, 512 * half:512 * (half + 1)], ALU.add),
                       waits=[(s_mm, mm_c)], incs=[(s_dve, 1)])
                    bank_free[bank] = [(s_dve, s_dve.count)]
                d_c = s_dve.count
                t = b * TPB + i
                op("sync", lambda s, i=i, t=t, bpar=bpar: s.dma_start(
                    out=outh[128 * t:128 * (t + 1), :], in_=x_sb[:, bpar, i, :]),
                   waits=[(s_dve, d_c)], incs=[(s_out, 16)])
            st["ffn_mm"][b] = s_mm.count
            st["out_done"][b] = s_out.count
            if b + 2 <= NBLK - 1:
                bn = b + 2
                for tt in range(TPB):
                    t = bn * TPB + tt
                    op("sync", lambda s, bn=bn, tt=tt, t=t: s.dma_start(
                        out=x_sb[:, bn % 2, tt, :], in_=xh[128 * t:128 * (t + 1), :]),
                       waits=[(s_out, st["out_done"][b])] if tt == 0 else [],
                       incs=[(s_x[bn % 2], 16)])
                st["x_loaded"][bn] = s_x[bn % 2].count

        emit_ln1(0)
        emit_v(0)
        emit_qk(0)
        emit_ktm(0)
        emit_w8()
        for b in range(NBLK):
            emit_attn(b)
            emit_proj(b)
            if b + 1 < NBLK:
                emit_ln1(b + 1)
            emit_ln2q8(b)
            if b + 1 < NBLK:
                emit_v(b + 1)
                emit_qk(b + 1)
            emit_ffn(b)
            if b + 1 < NBLK:
                emit_ktm(b + 1)

        op("sync", lambda s: None, waits=[(s_out, s_out.count)])

        with nc.Block() as blk:
            def emit(eng_obj, name):
                for waits, fn, incs in B.eng_ops[name]:
                    for s, v in waits:
                        eng_obj.wait_ge(s.h, v)
                    inst = fn(eng_obj)
                    for jj, (s, n) in enumerate(incs):
                        assert jj == 0 and inst is not None
                        inst.then_inc(s.h, n)

            @blk.sync
            def _(e):
                emit(e, "sync")

            @blk.gpsimd
            def _(e):
                emit(e, "gpsimd")

            @blk.scalar
            def _(e):
                emit(e, "scalar")

            @blk.vector
            def _(e):
                emit(e, "vector")

            @blk.tensor
            def _(e):
                emit(e, "tensor")

    return nc


_NC_CACHE = None


def kernel(L0_attn_x, L0_attn_W_attn, L0_attn_b_attn, L0_attn_W_proj,
           L0_attn_b_proj, L0_ffn_weight, L0_ffn_bias):
    global _NC_CACHE
    x = np.ascontiguousarray(np.asarray(L0_attn_x, dtype=np.float32))
    Wa = np.ascontiguousarray(np.asarray(L0_attn_W_attn, dtype=np.float32))
    ba = np.ascontiguousarray(np.asarray(L0_attn_b_attn, dtype=np.float32))
    Wp = np.ascontiguousarray(np.asarray(L0_attn_W_proj, dtype=np.float32))
    bp = np.ascontiguousarray(np.asarray(L0_attn_b_proj, dtype=np.float32))
    Wf = np.ascontiguousarray(np.asarray(L0_ffn_weight, dtype=np.float32))
    bfv = np.ascontiguousarray(np.asarray(L0_ffn_bias, dtype=np.float32))
    Wft = np.ascontiguousarray(Wf.T)
    mask = np.ascontiguousarray(
        np.tile(np.triu(np.ones((128, 128), np.float32)), (1, 4)).astype(ml_dtypes.bfloat16))
    bqk_h = np.ascontiguousarray(ba[:2048].reshape(16, 128).T)

    if _NC_CACHE is None:
        _NC_CACHE = build_nc()
    nc = _NC_CACHE
    in_maps = []
    for b in range(8):
        in_maps.append({"x": np.ascontiguousarray(x[b]), "W_attn": Wa, "b_attn": ba,
                        "W_proj": Wp, "b_proj": bp, "Wt_ffn": Wft, "b_ffn": bfv,
                        "mask": mask, "bqk_h": bqk_h})
    res = run_bass_kernel_spmd(nc, in_maps, core_ids=list(range(8)))
    out = np.stack([np.asarray(res.results[b]["out"]) for b in range(8)], axis=0)
    return out.astype(np.float32)


# version 08
# speedup vs baseline: 1.0598x; 1.0253x over previous
import numpy as np
import ml_dtypes
import concourse.bass as bass
import concourse.mybir as mybir
from concourse.bass_utils import run_bass_kernel_spmd

F32 = mybir.dt.float32
BF16 = mybir.dt.bfloat16
FP8 = mybir.dt.float8e4
AF = mybir.ActivationFunctionType
ALU = mybir.AluOpType

T = 2048
C = 1024
H = 16
D = 64
KC = 8
NBLK = 4
TPB = 4
LN_EPS = 1e-5
ATT_EPS = 1e-6


class _Sem:
    def __init__(self, handle):
        self.h = handle
        self.count = 0


class _Builder:
    def __init__(self):
        self.eng_ops = {e: [] for e in ("sync", "scalar", "vector", "gpsimd", "tensor")}

    def op(self, eng, fn, waits=(), incs=()):
        for s, v in waits:
            assert v <= s.count, f"wait {v} > produced {s.count} ({eng})"
        self.eng_ops[eng].append((list(waits), fn, list(incs)))
        for s, n in incs:
            s.count += n


def build_nc():
    nc = bass.Bass("TRN2", target_bir_lowering=False, detect_race_conditions=False)

    xh = nc.dram_tensor("x", [T, C], F32, kind="ExternalInput").ap()
    Wa = nc.dram_tensor("W_attn", [C, 3 * C], F32, kind="ExternalInput").ap()
    ba = nc.dram_tensor("b_attn", [3 * C], F32, kind="ExternalInput").ap()
    Wp = nc.dram_tensor("W_proj", [C, C], F32, kind="ExternalInput").ap()
    bp = nc.dram_tensor("b_proj", [C], F32, kind="ExternalInput").ap()
    Wft = nc.dram_tensor("Wt_ffn", [C, C], F32, kind="ExternalInput").ap()
    bf = nc.dram_tensor("b_ffn", [C], F32, kind="ExternalInput").ap()
    maskh = nc.dram_tensor("mask", [128, 512], BF16, kind="ExternalInput").ap()
    bqkh = nc.dram_tensor("bqk_h", [128, 16], F32, kind="ExternalInput").ap()
    outh = nc.dram_tensor("out", [T, C], F32, kind="ExternalOutput").ap()

    import contextlib
    ctx = contextlib.ExitStack()
    sb = lambda name, shape, dt: ctx.enter_context(nc.sbuf_tensor(name, shape, dt))
    ps = lambda name, shape, dt: ctx.enter_context(nc.psum_tensor(name, shape, dt))
    sem = lambda name: _Sem(ctx.enter_context(nc.semaphore(name)))

    with ctx:
        W_qk = sb("W_qk", [128, KC, 2 * C], BF16)
        W_v = sb("W_v", [128, KC, C], BF16)
        W_pr = sb("W_pr", [128, KC, C], BF16)
        W_f8 = sb("W_f8", [128, KC, C], BF16)
        bqk = sb("bqk", [128, 16], F32)
        bv_row = sb("bv_row", [1, C], BF16)
        bp_row = sb("bp_row", [1, C], BF16)
        bf_row = sb("bf_row", [1, C], BF16)
        ones_row = sb("ones_row", [1, 128], BF16)
        ones_col = sb("ones_col", [128, 1], BF16)
        zero_row = sb("zero_row", [1, 512], BF16)
        mask_rep = sb("mask_rep", [128, 512], BF16)
        x_sb = sb("x_sb", [128, 2, TPB, C], F32)
        r_sb = sb("r_sb", [128, TPB, C], F32)
        ln_tm = sb("ln_tm", [128, TPB, C], BF16)
        ln_fm = sb("ln_fm", [128, TPB, KC, 128], BF16)
        y_fm = ln_fm
        q_fm = sb("q_fm", [128, TPB, KC, 128], BF16)
        k_fm = sb("k_fm", [128, TPB, KC, 128], BF16)
        k_tm = sb("k_tm", [128, TPB, C], BF16)
        v_sb = sb("v_sb", [128, TPB, C], BF16)
        k_mk = sb("k_mk", [128, 2, KC, 128], BF16)
        stm = sb("stm", [128, H, 128], BF16)
        y_tm = sb("y_tm", [128, C], BF16)
        q8_tm = sb("q8_tm", [128, TPB, C], FP8)
        e_sb = sb("e_sb", [128, 2, 512], BF16)
        rl_sb = sb("rl_sb", [128, 2, 512], BF16)
        kva = sb("kva", [128, H // 2, D + 1], BF16)
        ksum = sb("ksum", [128, 2, H // 2], F32)
        stats = []
        for tag in ("a", "b"):
            stats.append(dict(
                bn12=sb(f"bn12{tag}", [128, TPB, 12], F32),
                mvab=sb(f"mvab{tag}", [128, TPB, 4], F32),
                mvs4=sb(f"mvs4{tag}", [128, TPB, 2], F32),
                mvd4=sb(f"mvd4{tag}", [128, TPB, 2], F32),
                mvp4=sb(f"mvp4{tag}", [128, TPB, 2], F32),
                mean4=sb(f"mean4{tag}", [128, TPB], F32),
                var4=sb(f"var4{tag}", [128, TPB], F32),
                sqv=sb(f"sqv{tag}", [128, TPB], F32),
                invs=sb(f"invs{tag}", [128, TPB], F32),
                nmb=sb(f"nmb{tag}", [128, TPB], F32),
                sqrt_act=0,
            ))
        den_f = sb("den_f", [128, 16], F32)
        rcp_f = sb("rcp_f", [128, 16], F32)
        c_eps5 = sb("c_eps5", [128, 1], F32)
        c_eps6 = sb("c_eps6", [128, 1], F32)
        c_m1 = sb("c_m1", [128, 1], F32)
        nc.const_aps.register(F32, LN_EPS, c_eps5[:])
        nc.const_aps.register(F32, ATT_EPS, c_eps6[:])
        nc.const_aps.register(F32, -1.0, c_m1[:])
        pbig = [ps("pbig0", [128, 512], F32), ps("pbig1", [128, 512], F32)]
        pnum = ps("pnum", [128, H, D], F32)
        pden = ps("pden", [128, 16], F32)
        pks = ps("pks", [128, 8], F32)
        pkv = ps("pkv", [128, H // 2, D], F32)

        s_x = [sem("s_x0"), sem("s_x1")]
        s_const = sem("s_const")
        s_w = sem("s_w")
        s_wv = sem("s_wv")
        s_wqk = sem("s_wqk")
        s_wpr = sem("s_wpr")
        s_stg = [sem("s_stg0"), sem("s_stg1")]
        s_trl = sem("s_trl")
        s_trk = sem("s_trk")
        s_try = sem("s_try")
        s_trq = sem("s_trq")
        s_mm = sem("s_mm")
        s_act = sem("s_act")
        s_dve = sem("s_dve")
        s_pl = sem("s_pl")
        s_out = sem("s_out")

        B = _Builder()
        op = B.op

        st = dict(
            x_loaded={}, out_done={}, ln_tr={}, ktm_tr={}, ytr={},
            qk_ready={}, k_ready={}, v_ready={}, q8_tr={}, q8_act={},
            attn_mm={}, proj_mm={}, ffn_mm={}, nm_dve={},
            phi_hist=[], div_dve=0, kva_act=0, ks_dve=0, ksum_read_act={},
            chunk_mm=0, w8_all=0, w8_act_all=0,
        )
        bank_free = [[], []]

        op("vector", lambda v: v.memset(ones_row[:], 1.0))
        op("vector", lambda v: v.memset(zero_row[:], 0.0))
        op("vector", lambda v: v.memset(ones_col[:], 1.0))
        op("vector", lambda v: v.memset(kva[:], 0.0))
        op("vector", lambda v: v.memset(k_mk[:], 0.0))
        op("vector", lambda v: v.memset(c_eps5[:], LN_EPS))
        op("vector", lambda v: v.memset(c_eps6[:], ATT_EPS))
        op("vector", lambda v: v.memset(c_m1[:], -1.0))
        op("vector", lambda v: v.memset(ksum[:], 0.0), incs=[(s_dve, 1)])

        op("sync", lambda s: s.dma_start(out=mask_rep[:], in_=maskh), incs=[(s_const, 16)])
        op("sync", lambda s: s.dma_start(out=bqk[:], in_=bqkh), incs=[(s_const, 16)])
        consts_ld = s_const.count

        op("gpsimd", lambda g: g.dma_start(out=bv_row[:], in_=ba[2048:3072].rearrange("(a f) -> a f", a=1)),
           incs=[(s_w, 16)])
        op("gpsimd", lambda g: g.dma_start(out=bp_row[:], in_=bp.rearrange("(a f) -> a f", a=1)),
           incs=[(s_w, 16)])
        op("gpsimd", lambda g: g.dma_start(out=bf_row[:], in_=bf.rearrange("(a f) -> a f", a=1)),
           incs=[(s_w, 16)])
        w_bias = s_w.count
        for c in range(KC):
            op("gpsimd", lambda g, c=c: g.dma_start(out=W_v[:, c, :], in_=Wa[128 * c:128 * (c + 1), 2048:3072]),
               incs=[(s_wv, 16)])
        w_v_all = s_wv.count
        for c in range(KC):
            op("gpsimd", lambda g, c=c: g.dma_start(out=W_qk[:, c, :], in_=Wa[128 * c:128 * (c + 1), 0:2048]),
               incs=[(s_wqk, 16)])
        w_qk_all = s_wqk.count
        for c in range(KC):
            op("gpsimd", lambda g, c=c: g.dma_start(out=W_pr[:, c, :], in_=Wp[128 * c:128 * (c + 1), :]),
               incs=[(s_wpr, 16)])
        w_pr_all = s_wpr.count

        op("tensor", lambda t: t.matmul(
            pkv[:].rearrange("p a b -> p (a b)"), zero_row[0:1, 0:128], zero_row[0:1, 0:512],
            start=True, stop=False, skip_group_check=True), waits=[(s_dve, 1)])

        for b0 in range(min(2, NBLK)):
            for tt in range(TPB):
                t = b0 * TPB + tt
                op("sync", lambda s, b0=b0, tt=tt, t=t: s.dma_start(
                    out=x_sb[:, b0 % 2, tt, :], in_=xh[128 * t:128 * (t + 1), :]),
                   incs=[(s_x[b0 % 2], 16)])
            st["x_loaded"][b0] = s_x[b0 % 2].count

        def emit_stats(S, src_ap_fn, first_waits):
            for i in range(TPB):
                op("vector", lambda v, i=i: v.bn_stats(S["bn12"][:, i, 0:6], src_ap_fn(i)[:, 0:512]),
                   waits=first_waits if i == 0 else [])
                op("vector", lambda v, i=i: v.bn_stats(S["bn12"][:, i, 6:12], src_ap_fn(i)[:, 512:1024]),
                   incs=[(s_dve, 1)] if i == TPB - 1 else [])
            op("vector", lambda v, c=s_dve.count: v.wait_ge(s_dve.h, c))
            for i in range(TPB):
                op("vector", lambda v, i=i: v.bn_aggr(S["mvab"][:, i, 0:2], S["bn12"][:, i, 0:6]))
                op("vector", lambda v, i=i: v.bn_aggr(S["mvab"][:, i, 2:4], S["bn12"][:, i, 6:12]),
                   incs=[(s_dve, 1)] if i == TPB - 1 else [])
            op("vector", lambda v, c=s_dve.count: v.wait_ge(s_dve.h, c))
            op("vector", lambda v: v.tensor_tensor(S["mvs4"][:], S["mvab"][:, :, 0:2], S["mvab"][:, :, 2:4], ALU.add))
            op("vector", lambda v: v.tensor_tensor(S["mvd4"][:], S["mvab"][:, :, 0:2], S["mvab"][:, :, 2:4], ALU.subtract),
               incs=[(s_dve, 1)])
            op("vector", lambda v, c=s_dve.count: v.wait_ge(s_dve.h, c))
            op("vector", lambda v: v.tensor_tensor(S["mvp4"][:], S["mvd4"][:], S["mvd4"][:], ALU.mult))
            op("vector", lambda v: v.tensor_scalar(S["mean4"][:], S["mvs4"][:, :, 0], 0.5, None, ALU.mult))
            op("vector", lambda v: v.tensor_scalar(S["var4"][:], S["mvs4"][:, :, 1], 0.5, None, ALU.mult),
               incs=[(s_dve, 1)])
            op("vector", lambda v, c=s_dve.count: v.wait_ge(s_dve.h, c))
            op("vector", lambda v: v.scalar_tensor_tensor(
                S["var4"][:], S["mvp4"][:, :, 0], 0.25, S["var4"][:], ALU.mult, ALU.add), incs=[(s_dve, 1)])
            d_c = s_dve.count
            op("scalar", lambda s: s.activation(out=S["sqv"][:], in_=S["var4"][:], func=AF.Sqrt, bias=LN_EPS),
               waits=[(s_dve, d_c)], incs=[(s_act, 1)])
            S["sqrt_act"] = s_act.count
            op("vector", lambda v: v.reciprocal(S["invs"][:], S["sqv"][:]),
               waits=[(s_act, S["sqrt_act"])], incs=[(s_dve, 1)])
            op("vector", lambda v, c=s_dve.count: v.wait_ge(s_dve.h, c))
            op("vector", lambda v: v.scalar_tensor_tensor(
                S["nmb"][:], S["mean4"][:], -1.0, S["invs"][:], ALU.mult, ALU.mult),
               incs=[(s_dve, 1)])
            return s_dve.count

        def emit_ln1(b):
            bpar = b % 2
            S = stats[0]
            fw = [(s_x[bpar], st["x_loaded"][b])]
            if b > 1:
                fw.append((s_act, S["sqrt_act"]))
            d_c = emit_stats(S, lambda i, bpar=bpar: x_sb[:, bpar, i, :], fw)
            for i in range(TPB):
                waits = [(s_dve, d_c)]
                if b > 0 and i == 0:
                    waits.append((s_trl, st["ln_tr"][b - 1]))
                op("scalar", lambda s, i=i, bpar=bpar, S=S: s.activation(
                    out=ln_tm[:, i, :], in_=x_sb[:, bpar, i, :], func=AF.Identity,
                    bias=S["nmb"][:, i:i + 1], scale=S["invs"][:, i:i + 1]),
                   waits=waits, incs=[(s_act, 1)])
                twaits = []
                if b > 0 and i == 0:
                    twaits.append((s_mm, st["proj_mm"][b - 1]))
                op("scalar", lambda s, i=i: s.dma_start_transpose(
                    ln_fm[:, i, :, :], ln_tm[:, i, :].rearrange("p (c m) -> p c m", c=KC)),
                   waits=twaits, incs=[(s_trl, 16)])
            st["ln_tr"][b] = s_trl.count

        def big_nops():
            op("tensor", lambda t: t.nop())
            op("tensor", lambda t: t.nop())

        def emit_v(b):
            for i in range(TPB):
                for half in range(2):
                    bank = (i * 2 + half) % 2
                    waits = [(s_trl, st["ln_tr"][b]), (s_w, w_bias), (s_wv, w_v_all)] + bank_free[bank]
                    for kc in range(KC):
                        op("tensor", lambda t, i=i, half=half, kc=kc, bank=bank: t.matmul(
                            pbig[bank][:], ln_fm[:, i, kc, :], W_v[:, kc, 512 * half:512 * (half + 1)],
                            start=(kc == 0), stop=False),
                           waits=waits if kc == 0 else [])
                        big_nops()
                    op("tensor", lambda t, half=half, bank=bank: t.matmul(
                        pbig[bank][:], ones_row[:, 0:128], bv_row[:, 512 * half:512 * (half + 1)],
                        start=False, stop=True), incs=[(s_mm, 1)])
                    mm_c = s_mm.count
                    big_nops()
                    ewaits = [(s_mm, mm_c)]
                    if b > 0 and i == 0 and half == 0:
                        ewaits.append((s_trq, st["q8_tr"][b - 1]))
                    op("scalar", lambda s, i=i, half=half, bank=bank: s.activation(
                        out=v_sb[:, i, 512 * half:512 * (half + 1)], in_=pbig[bank][:], func=AF.Copy),
                       waits=ewaits, incs=[(s_act, 1)])
                    bank_free[bank] = [(s_act, s_act.count)]
            st["v_ready"][b] = s_act.count

        def emit_qk(b):
            for fc in [8, 9, 10, 11, 12, 13, 14, 15, 0, 1, 2, 3, 4, 5, 6, 7]:
                bank = fc % 2
                waits = [(s_trl, st["ln_tr"][b]), (s_wqk, w_qk_all)] + bank_free[bank]
                for kc in range(KC):
                    op("tensor", lambda t, fc=fc, kc=kc, bank=bank: t.matmul(
                        pbig[bank][:], W_qk[:, kc, 128 * fc:128 * (fc + 1)], ln_fm[:, :, kc, :],
                        start=(kc == 0), stop=(kc == KC - 1)),
                       waits=waits if kc == 0 else [],
                       incs=[(s_mm, 1)] if kc == KC - 1 else [])
                    big_nops()
                mm_c = s_mm.count
                dst = q_fm[:, :, fc, :] if fc < 8 else k_fm[:, :, fc - 8, :]
                ep = fc % 2
                ewaits = [(s_mm, mm_c)]
                if len(st["phi_hist"]) >= 2:
                    ewaits.append((s_pl, st["phi_hist"][-2]))
                op("scalar", lambda s, fc=fc, ep=ep, bank=bank: s.activation(
                    out=e_sb[:, ep, :], in_=pbig[bank][:], func=AF.Exp, bias=bqk[:, fc:fc + 1]),
                   waits=ewaits, incs=[(s_act, 1)])
                a_c = s_act.count
                op("vector", lambda v, fc=fc, ep=ep, bank=bank: v.tensor_scalar(
                    rl_sb[:, ep, :], pbig[bank][:], bqk[:, fc:fc + 1], 0.0, ALU.add, ALU.max),
                   waits=[(s_act, a_c)], incs=[(s_dve, 1)])
                d_c = s_dve.count
                bank_free[bank] = [(s_dve, d_c)]
                pw = [(s_act, a_c)]
                if b > 0 and fc == 8:
                    pw.append((s_mm, st["attn_mm"][b - 1]))
                op("vector", lambda v, ep=ep: v.tensor_scalar(
                    e_sb[:, ep, :], e_sb[:, ep, :], 1.0, None, ALU.min),
                   waits=pw, incs=[(s_dve, 1)])
                op("vector", lambda v, c=s_dve.count: v.wait_ge(s_dve.h, c))
                op("vector", lambda v, ep=ep, dst=dst: v.tensor_tensor(
                    dst, e_sb[:, ep, :].rearrange("p (a m) -> p a m", a=TPB),
                    rl_sb[:, ep, :].rearrange("p (a m) -> p a m", a=TPB), ALU.add),
                   incs=[(s_pl, 1)])
                st["phi_hist"].append(s_pl.count)
                if fc == 15:
                    st["k_ready"][b] = s_pl.count
            st["qk_ready"][b] = s_pl.count

        def emit_ktm(b):
            for i in range(TPB):
                waits = [(s_pl, st["k_ready"][b])]
                if b > 0 and i == 0:
                    waits.append((s_mm, st["ffn_mm"][b - 1]))
                op("scalar", lambda s, i=i: s.dma_start_transpose(
                    k_tm[:, i, :].rearrange("p (c m) -> p c m", c=KC), k_fm[:, i, :, :]),
                   waits=waits, incs=[(s_trk, 16)])
            st["ktm_tr"][b] = s_trk.count

        def emit_w8():
            w8_done = []
            for c in range(KC):
                par = c % 2
                waits = []
                if c >= 2:
                    waits = [(s_act, w8_done[c - 2][0]), (s_dve, w8_done[c - 2][1])]
                op("gpsimd", lambda g, c=c, par=par: g.dma_start(
                    out=r_sb[:, par, :], in_=Wft[128 * c:128 * (c + 1), :]),
                   waits=waits, incs=[(s_stg[par], 16)])
                stg_c = s_stg[par].count
                op("scalar", lambda s, par=par, stg_c=stg_c: s.activation(
                    out=q8_tm[:, par, :], in_=r_sb[:, par, :], func=AF.Copy),
                   waits=[(s_stg[par], stg_c)], incs=[(s_act, 1)])
                a_c = s_act.count
                op("vector", lambda v, c=c, par=par: v.tensor_copy(W_f8[:, c, :], q8_tm[:, par, :]),
                   waits=[(s_act, a_c)], incs=[(s_dve, 1)])
                w8_done.append((a_c, s_dve.count))
            st["w8_all"] = s_dve.count
            st["w8_act_all"] = s_act.count

        def emit_attn(b):
            for ci in range(TPB):
                c = b * TPB + ci
                kmk_w = [(s_pl, st["k_ready"][b])]
                if c > 0:
                    kmk_w.append((s_mm, st["chunk_mm"]))
                op("vector", lambda v, ci=ci: v.tensor_copy(
                    k_mk[0:64, 0, :, :], k_fm[0:64, ci, :, :]), waits=kmk_w)
                op("vector", lambda v, ci=ci: v.tensor_copy(
                    k_mk[64:128, 1, :, :], k_fm[64:128, ci, :, :]), incs=[(s_dve, 1)])
                kmk_dve = s_dve.count
                op("tensor", lambda t: t.matmul(
                    pks[:], zero_row[0:1, 0:128], zero_row[0:1, 0:8],
                    start=True, stop=False, skip_group_check=True),
                   waits=[(s_dve, st["ks_dve"])] if c > 0 else [])
                mask_dve = {}
                for qd in range(4):
                    bank = qd % 2
                    waits = [(s_dve, kmk_dve), (s_trk, st["ktm_tr"][b]),
                             (s_act, st["v_ready"][b]), (s_const, consts_ld)] + bank_free[bank]
                    for hh in range(4):
                        h = qd * 4 + hh
                        op("tensor", lambda t, h=h, hh=hh, ci=ci, bank=bank: t.matmul(
                            pbig[bank][:, 128 * hh:128 * (hh + 1)],
                            k_mk[:, h % 2, h // 2, :], q_fm[:, ci, h // 2, :],
                            start=(hh == 0), stop=(hh == 3)),
                           waits=waits if hh == 0 else [],
                           incs=[(s_mm, 1)] if hh == 3 else [])
                    st_mm = s_mm.count
                    op("vector", lambda v, qd=qd, bank=bank: v.scalar_tensor_tensor(
                        stm[:, 4 * qd:4 * (qd + 1), :], pbig[bank][:].rearrange("p (a m) -> p a m", a=4),
                        1.0, mask_rep[:].rearrange("p (a m) -> p a m", a=4), ALU.mult, ALU.mult),
                       waits=[(s_mm, st_mm)], incs=[(s_dve, 1)])
                    mask_dve[qd] = s_dve.count
                    bank_free[bank] = [(s_dve, mask_dve[qd])]
                for h in range(H):
                    po = 64 * (h % 2)
                    waits = [(s_dve, mask_dve[h // 4])]
                    if c > 0 and h == 0:
                        waits.append((s_act, st["kva_act"]))
                    op("tensor", lambda t, h=h, ci=ci: t.matmul(
                        pnum[:, h, :], stm[:, h, :], v_sb[:, ci, 64 * h:64 * (h + 1)],
                        start=(h % 8 == 0), stop=False), waits=waits)
                    op("tensor", lambda t, h=h, po=po, ci=ci: t.matmul(
                        pnum[:, h, :], q_fm[po:po + 64, ci, h // 2, :],
                        kva[po:po + 64, h // 2, 0:D], start=False, stop=(h % 8 == 7)))
                    op("tensor", lambda t, h=h: t.matmul(
                        pden[:, h:h + 1], stm[:, h, :], ones_col[:],
                        start=(h == 0), stop=False))
                    op("tensor", lambda t, h=h, po=po, ci=ci: t.matmul(
                        pden[:, h:h + 1], q_fm[po:po + 64, ci, h // 2, :],
                        kva[po:po + 64, h // 2, D:D + 1], start=False, stop=(h == H - 1)))
                    op("tensor", lambda t, h=h, po=po, ci=ci: t.matmul(
                        pkv[po:po + 64, h // 2, :], k_tm[:, ci, 64 * h:64 * (h + 1)],
                        v_sb[:, ci, 64 * h:64 * (h + 1)],
                        start=False, stop=False, skip_group_check=True))
                    op("tensor", lambda t, h=h, po=po, ci=ci: t.matmul(
                        pks[po:po + 64, h // 2:h // 2 + 1], k_tm[:, ci, 64 * h:64 * (h + 1)],
                        ones_col[:], start=False, stop=False, skip_group_check=True),
                       incs=[(s_mm, 1)] if h == H - 1 else [])
                st["chunk_mm"] = s_mm.count
                st["attn_mm"][b] = st["chunk_mm"]
                kpar, kparn = c % 2, (c + 1) % 2
                waits = [(s_mm, st["chunk_mm"])]
                if c >= 2:
                    waits.append((s_act, st["ksum_read_act"][c - 2]))
                op("vector", lambda v: v.tensor_scalar(
                    den_f[:], pden[:], ATT_EPS, None, ALU.add), waits=waits, incs=[(s_dve, 1)])
                op("vector", lambda v, c_=s_dve.count: v.wait_ge(s_dve.h, c_))
                op("vector", lambda v: v.reciprocal(rcp_f[:], den_f[:]), incs=[(s_dve, 1)])
                rcp_dve = s_dve.count
                op("vector", lambda v, kpar=kpar, kparn=kparn: v.tensor_tensor(
                    ksum[:, kpar, :], ksum[:, kparn, :], pks[:], ALU.add),
                   incs=[(s_dve, 1)])
                st["ks_dve"] = s_dve.count
                ywaits = [(s_dve, rcp_dve)]
                if c >= 1:
                    ywaits.append((s_try, st["ytr"][c - 1]))
                op("vector", lambda v: v.tensor_tensor(
                    y_tm[:].rearrange("p (h d) -> p h d", h=H), pnum[:],
                    rcp_f[:, :, None].broadcast_to((128, 16, D)), ALU.mult),
                   waits=ywaits, incs=[(s_dve, 1)])
                div_dve = s_dve.count
                st["div_dve"] = div_dve
                op("scalar", lambda s: s.activation(out=kva[:, :, 0:D], in_=pkv[:], func=AF.Copy),
                   waits=[(s_mm, st["chunk_mm"]), (s_dve, st["ks_dve"])])
                op("scalar", lambda s, kpar=kpar: s.activation(
                    out=kva[:, :, D:D + 1], in_=ksum[:, kpar, :, None], func=AF.Copy),
                   incs=[(s_act, 1)])
                st["kva_act"] = s_act.count
                st["ksum_read_act"][c] = st["kva_act"]
                op("scalar", lambda s, ci=ci: s.dma_start_transpose(
                    y_fm[:, ci, :, :], y_tm[:].rearrange("p (c m) -> p c m", c=KC)),
                   waits=[(s_dve, div_dve)], incs=[(s_try, 16)])
                st["ytr"][c] = s_try.count
                if ci > 0:
                    emit_proj_tile(b, ci - 1)

        def emit_proj_tile(b, i):
            bpar = b % 2
            if True:
                for half in range(2):
                    bank = (i * 2 + half) % 2
                    waits = [(s_try, st["ytr"][b * TPB + i]), (s_w, w_bias),
                             (s_wpr, w_pr_all)] + bank_free[bank]
                    for kc in range(KC):
                        op("tensor", lambda t, i=i, half=half, kc=kc, bank=bank: t.matmul(
                            pbig[bank][:], y_fm[:, i, kc, :], W_pr[:, kc, 512 * half:512 * (half + 1)],
                            start=(kc == 0), stop=False),
                           waits=waits if kc == 0 else [])
                        big_nops()
                    op("tensor", lambda t, half=half, bank=bank: t.matmul(
                        pbig[bank][:], ones_row[:, 0:128], bp_row[:, 512 * half:512 * (half + 1)],
                        start=False, stop=True), incs=[(s_mm, 1)])
                    mm_c = s_mm.count
                    big_nops()
                    ewaits = [(s_mm, mm_c)]
                    if b == 0 and i == 0 and half == 0:
                        ewaits.append((s_act, st["w8_act_all"]))
                    if b > 0 and i == 0 and half == 0:
                        ewaits.append((s_act, st["q8_act"][b - 1]))
                    op("vector", lambda v, i=i, half=half, bank=bank, bpar=bpar: v.tensor_tensor(
                        r_sb[:, i, 512 * half:512 * (half + 1)], pbig[bank][:],
                        x_sb[:, bpar, i, 512 * half:512 * (half + 1)], ALU.add),
                       waits=ewaits, incs=[(s_dve, 1)])
                    bank_free[bank] = [(s_dve, s_dve.count)]
            st["proj_mm"][b] = s_mm.count

        def emit_ln2q8(b):
            S = stats[1]
            fw = []
            if b > 0:
                fw.append((s_act, S["sqrt_act"]))
            d_c = emit_stats(S, lambda i: r_sb[:, i, :], fw)
            for i in range(TPB):
                op("scalar", lambda s, i=i, S=S: s.activation(
                    out=q8_tm[:, i, :], in_=r_sb[:, i, :], func=AF.Identity,
                    bias=S["nmb"][:, i:i + 1], scale=S["invs"][:, i:i + 1]),
                   waits=[(s_dve, d_c)] if i == 0 else [], incs=[(s_act, 1)])
                a_c = s_act.count
                cw = [(s_act, a_c)]
                if i == 0:
                    cw.append((s_mm, st["attn_mm"][b]))
                op("vector", lambda v, i=i: v.tensor_copy(v_sb[:, i, :], q8_tm[:, i, :]),
                   waits=cw, incs=[(s_dve, 1)])
                d_c2 = s_dve.count
                op("scalar", lambda s, i=i: s.dma_start_transpose(
                    k_tm[:, i, :].rearrange("p (c m) -> p c m", c=KC),
                    v_sb[:, i, :].rearrange("p (c m) -> p c m", c=KC)),
                   waits=[(s_dve, d_c2)], incs=[(s_trq, 16)])
            st["q8_tr"][b] = s_trq.count
            st["q8_act"][b] = s_act.count

        def emit_ffn(b):
            bpar = b % 2
            for i in range(TPB):
                for half in range(2):
                    bank = (i * 2 + half) % 2
                    waits = [(s_trq, st["q8_tr"][b]), (s_dve, st["w8_all"]),
                             (s_w, w_bias)] + bank_free[bank]
                    for kc in range(KC):
                        op("tensor", lambda t, i=i, half=half, kc=kc, bank=bank: t.matmul(
                            pbig[bank][:], k_tm[:, i, 128 * kc:128 * (kc + 1)],
                            W_f8[:, kc, 512 * half:512 * (half + 1)],
                            start=(kc == 0), stop=False),
                           waits=waits if kc == 0 else [])
                        big_nops()
                    op("tensor", lambda t, half=half, bank=bank: t.matmul(
                        pbig[bank][:], ones_row[:, 0:128], bf_row[:, 512 * half:512 * (half + 1)],
                        start=False, stop=True), incs=[(s_mm, 1)])
                    mm_c = s_mm.count
                    big_nops()
                    op("vector", lambda v, i=i, half=half, bank=bank, bpar=bpar: v.tensor_tensor(
                        x_sb[:, bpar, i, 512 * half:512 * (half + 1)], pbig[bank][:],
                        r_sb[:, i, 512 * half:512 * (half + 1)], ALU.add),
                       waits=[(s_mm, mm_c)], incs=[(s_dve, 1)])
                    bank_free[bank] = [(s_dve, s_dve.count)]
                d_c = s_dve.count
                t = b * TPB + i
                op("sync", lambda s, i=i, t=t, bpar=bpar: s.dma_start(
                    out=outh[128 * t:128 * (t + 1), :], in_=x_sb[:, bpar, i, :]),
                   waits=[(s_dve, d_c)], incs=[(s_out, 16)])
            st["ffn_mm"][b] = s_mm.count
            st["out_done"][b] = s_out.count
            if b + 2 <= NBLK - 1:
                bn = b + 2
                for tt in range(TPB):
                    t = bn * TPB + tt
                    op("sync", lambda s, bn=bn, tt=tt, t=t: s.dma_start(
                        out=x_sb[:, bn % 2, tt, :], in_=xh[128 * t:128 * (t + 1), :]),
                       waits=[(s_out, st["out_done"][b])] if tt == 0 else [],
                       incs=[(s_x[bn % 2], 16)])
                st["x_loaded"][bn] = s_x[bn % 2].count

        emit_ln1(0)
        emit_v(0)
        emit_qk(0)
        emit_ktm(0)
        emit_w8()
        for b in range(NBLK):
            emit_attn(b)
            emit_proj_tile(b, TPB - 1)
            if b + 1 < NBLK:
                emit_ln1(b + 1)
            emit_ln2q8(b)
            if b + 1 < NBLK:
                emit_v(b + 1)
                emit_qk(b + 1)
            emit_ffn(b)
            if b + 1 < NBLK:
                emit_ktm(b + 1)

        op("sync", lambda s: None, waits=[(s_out, s_out.count)])

        with nc.Block() as blk:
            def emit(eng_obj, name):
                for waits, fn, incs in B.eng_ops[name]:
                    for s, v in waits:
                        eng_obj.wait_ge(s.h, v)
                    inst = fn(eng_obj)
                    for jj, (s, n) in enumerate(incs):
                        assert jj == 0 and inst is not None
                        inst.then_inc(s.h, n)

            @blk.sync
            def _(e):
                emit(e, "sync")

            @blk.gpsimd
            def _(e):
                emit(e, "gpsimd")

            @blk.scalar
            def _(e):
                emit(e, "scalar")

            @blk.vector
            def _(e):
                emit(e, "vector")

            @blk.tensor
            def _(e):
                emit(e, "tensor")

    return nc


_NC_CACHE = None


def kernel(L0_attn_x, L0_attn_W_attn, L0_attn_b_attn, L0_attn_W_proj,
           L0_attn_b_proj, L0_ffn_weight, L0_ffn_bias):
    global _NC_CACHE
    x = np.ascontiguousarray(np.asarray(L0_attn_x, dtype=np.float32))
    Wa = np.ascontiguousarray(np.asarray(L0_attn_W_attn, dtype=np.float32))
    ba = np.ascontiguousarray(np.asarray(L0_attn_b_attn, dtype=np.float32))
    Wp = np.ascontiguousarray(np.asarray(L0_attn_W_proj, dtype=np.float32))
    bp = np.ascontiguousarray(np.asarray(L0_attn_b_proj, dtype=np.float32))
    Wf = np.ascontiguousarray(np.asarray(L0_ffn_weight, dtype=np.float32))
    bfv = np.ascontiguousarray(np.asarray(L0_ffn_bias, dtype=np.float32))
    Wft = np.ascontiguousarray(Wf.T)
    mask = np.ascontiguousarray(
        np.tile(np.triu(np.ones((128, 128), np.float32)), (1, 4)).astype(ml_dtypes.bfloat16))
    bqk_h = np.ascontiguousarray(ba[:2048].reshape(16, 128).T)

    if _NC_CACHE is None:
        _NC_CACHE = build_nc()
    nc = _NC_CACHE
    in_maps = []
    for b in range(8):
        in_maps.append({"x": np.ascontiguousarray(x[b]), "W_attn": Wa, "b_attn": ba,
                        "W_proj": Wp, "b_proj": bp, "Wt_ffn": Wft, "b_ffn": bfv,
                        "mask": mask, "bqk_h": bqk_h})
    res = run_bass_kernel_spmd(nc, in_maps, core_ids=list(range(8)))
    out = np.stack([np.asarray(res.results[b]["out"]) for b in range(8)], axis=0)
    return out.astype(np.float32)


# version 09
# speedup vs baseline: 1.0979x; 1.0359x over previous
import numpy as np
import ml_dtypes
import concourse.bass as bass
import concourse.mybir as mybir
from concourse.bass_utils import run_bass_kernel_spmd

F32 = mybir.dt.float32
BF16 = mybir.dt.bfloat16
FP8 = mybir.dt.float8e4
AF = mybir.ActivationFunctionType
ALU = mybir.AluOpType

T = 2048
C = 1024
H = 16
D = 64
KC = 8
NBLK = 4
TPB = 4
LN_EPS = 1e-5
ATT_EPS = 1e-6


class _Sem:
    def __init__(self, handle):
        self.h = handle
        self.count = 0


class _Builder:
    def __init__(self):
        self.eng_ops = {e: [] for e in ("sync", "scalar", "vector", "gpsimd", "tensor")}

    def op(self, eng, fn, waits=(), incs=()):
        for s, v in waits:
            assert v <= s.count, f"wait {v} > produced {s.count} ({eng})"
        self.eng_ops[eng].append((list(waits), fn, list(incs)))
        for s, n in incs:
            s.count += n


def build_nc():
    nc = bass.Bass("TRN2", target_bir_lowering=False, detect_race_conditions=False)

    xh = nc.dram_tensor("x", [T, C], F32, kind="ExternalInput").ap()
    Wa = nc.dram_tensor("W_attn", [C, 3 * C], F32, kind="ExternalInput").ap()
    ba = nc.dram_tensor("b_attn", [3 * C], F32, kind="ExternalInput").ap()
    Wp = nc.dram_tensor("W_proj", [C, C], F32, kind="ExternalInput").ap()
    bp = nc.dram_tensor("b_proj", [C], F32, kind="ExternalInput").ap()
    Wft = nc.dram_tensor("Wt_ffn", [C, C], F32, kind="ExternalInput").ap()
    bf = nc.dram_tensor("b_ffn", [C], F32, kind="ExternalInput").ap()
    maskh = nc.dram_tensor("mask", [128, 512], BF16, kind="ExternalInput").ap()
    bqkh = nc.dram_tensor("bqk_h", [128, 16], F32, kind="ExternalInput").ap()
    outh = nc.dram_tensor("out", [T, C], F32, kind="ExternalOutput").ap()

    import contextlib
    ctx = contextlib.ExitStack()
    sb = lambda name, shape, dt: ctx.enter_context(nc.sbuf_tensor(name, shape, dt))
    ps = lambda name, shape, dt: ctx.enter_context(nc.psum_tensor(name, shape, dt))
    sem = lambda name: _Sem(ctx.enter_context(nc.semaphore(name)))

    with ctx:
        W_qk = sb("W_qk", [128, KC, 2 * C], BF16)
        W_v = sb("W_v", [128, KC, C], BF16)
        W_pr = sb("W_pr", [128, KC, C], BF16)
        W_f8 = sb("W_f8", [128, KC, C], BF16)
        bqk = sb("bqk", [128, 16], F32)
        bv_row = sb("bv_row", [1, C], BF16)
        bp_row = sb("bp_row", [1, C], BF16)
        bf_row = sb("bf_row", [1, C], BF16)
        ones_row = sb("ones_row", [1, 128], BF16)
        ones_col = sb("ones_col", [128, 1], BF16)
        zero_row = sb("zero_row", [1, 512], BF16)
        mask_rep = sb("mask_rep", [128, 512], BF16)
        x_sb = sb("x_sb", [128, 2, TPB, C], F32)
        r_sb = sb("r_sb", [128, TPB, C], F32)
        ln_tm = sb("ln_tm", [128, TPB, C], BF16)
        ln_fm = sb("ln_fm", [128, TPB, KC, 128], BF16)
        y_fm = ln_fm
        q_fm = sb("q_fm", [128, TPB, KC, 128], BF16)
        k_fm = sb("k_fm", [128, TPB, KC, 128], BF16)
        k_tm = sb("k_tm", [128, TPB, C], BF16)
        v_sb = sb("v_sb", [128, TPB, H * 65], BF16)
        k_mk = sb("k_mk", [128, 2, KC, 128], BF16)
        stm = sb("stm", [128, H, 128], BF16)
        y_tm = sb("y_tm", [128, C], BF16)
        q8_tm = sb("q8_tm", [128, TPB, C], FP8)
        e_sb = sb("e_sb", [128, 2, 512], BF16)
        rl_sb = sb("rl_sb", [128, 2, 512], BF16)
        kva = sb("kva", [128, H // 2, D + 1], BF16)
        ksum = sb("ksum", [128, 2, H // 2], F32)
        stats = []
        for tag in ("a", "b"):
            stats.append(dict(
                bn12=sb(f"bn12{tag}", [128, TPB, 12], F32),
                mvab=sb(f"mvab{tag}", [128, TPB, 4], F32),
                mvs4=sb(f"mvs4{tag}", [128, TPB, 2], F32),
                mvd4=sb(f"mvd4{tag}", [128, TPB, 2], F32),
                mvp4=sb(f"mvp4{tag}", [128, TPB, 2], F32),
                mean4=sb(f"mean4{tag}", [128, TPB], F32),
                var4=sb(f"var4{tag}", [128, TPB], F32),
                sqv=sb(f"sqv{tag}", [128, TPB], F32),
                invs=sb(f"invs{tag}", [128, TPB], F32),
                nmb=sb(f"nmb{tag}", [128, TPB], F32),
                sqrt_act=0,
            ))
        den_f = sb("den_f", [128, 16], F32)
        rcp_f = sb("rcp_f", [128, 16], F32)
        c_eps5 = sb("c_eps5", [128, 1], F32)
        c_eps6 = sb("c_eps6", [128, 1], F32)
        c_m1 = sb("c_m1", [128, 1], F32)
        nc.const_aps.register(F32, LN_EPS, c_eps5[:])
        nc.const_aps.register(F32, ATT_EPS, c_eps6[:])
        nc.const_aps.register(F32, -1.0, c_m1[:])
        pbig = [ps("pbig0", [128, 512], F32), ps("pbig1", [128, 512], F32)]
        pnd = ps("pnd", [128, 1536], F32)
        pks = ps("pks", [128, 8], F32)
        pkv = ps("pkv", [128, H // 2, D], F32)

        s_x = [sem("s_x0"), sem("s_x1")]
        s_const = sem("s_const")
        s_w = sem("s_w")
        s_wv = sem("s_wv")
        s_wqk = sem("s_wqk")
        s_wpr = sem("s_wpr")
        s_stg = [sem("s_stg0"), sem("s_stg1")]
        s_trl = sem("s_trl")
        s_trk = sem("s_trk")
        s_try = sem("s_try")
        s_trq = sem("s_trq")
        s_mm = sem("s_mm")
        s_act = sem("s_act")
        s_dve = sem("s_dve")
        s_pl = sem("s_pl")
        s_out = sem("s_out")

        B = _Builder()
        op = B.op

        def _noff(h):
            if h < 7:
                return 65 * h
            if h < 14:
                return 512 + 65 * (h - 7)
            return 1024 + 65 * (h - 14)

        st = dict(
            x_loaded={}, out_done={}, ln_tr={}, ktm_tr={}, ytr={},
            qk_ready={}, k_ready={}, v_ready={}, q8_tr={}, q8_act={},
            attn_mm={}, proj_mm={}, ffn_mm={}, nm_dve={},
            phi_hist=[], div_dve=0, kva_act=0, ks_dve=0, ksum_read_act={},
            chunk_mm=0, w8_all=0, w8_act_all=0,
        )
        bank_free = [[], []]

        op("vector", lambda v: v.memset(ones_row[:], 1.0))
        op("vector", lambda v: v.memset(zero_row[:], 0.0))
        op("vector", lambda v: v.memset(ones_col[:], 1.0))
        op("vector", lambda v: v.memset(kva[:], 0.0))
        op("vector", lambda v: v.memset(k_mk[:], 0.0))
        op("vector", lambda v: v.memset(c_eps5[:], LN_EPS))
        op("vector", lambda v: v.memset(c_eps6[:], ATT_EPS))
        op("vector", lambda v: v.memset(c_m1[:], -1.0))
        op("vector", lambda v: v.memset(ksum[:], 0.0), incs=[(s_dve, 1)])

        op("sync", lambda s: s.dma_start(out=mask_rep[:], in_=maskh), incs=[(s_const, 16)])
        op("sync", lambda s: s.dma_start(out=bqk[:], in_=bqkh), incs=[(s_const, 16)])
        consts_ld = s_const.count

        op("gpsimd", lambda g: g.dma_start(out=bv_row[:], in_=ba[2048:3072].rearrange("(a f) -> a f", a=1)),
           incs=[(s_w, 16)])
        op("gpsimd", lambda g: g.dma_start(out=bp_row[:], in_=bp.rearrange("(a f) -> a f", a=1)),
           incs=[(s_w, 16)])
        op("gpsimd", lambda g: g.dma_start(out=bf_row[:], in_=bf.rearrange("(a f) -> a f", a=1)),
           incs=[(s_w, 16)])
        w_bias = s_w.count
        for c in range(KC):
            op("gpsimd", lambda g, c=c: g.dma_start(out=W_v[:, c, :], in_=Wa[128 * c:128 * (c + 1), 2048:3072]),
               incs=[(s_wv, 16)])
        w_v_all = s_wv.count
        for c in range(KC):
            op("gpsimd", lambda g, c=c: g.dma_start(out=W_qk[:, c, :], in_=Wa[128 * c:128 * (c + 1), 0:2048]),
               incs=[(s_wqk, 16)])
        w_qk_all = s_wqk.count
        for c in range(KC):
            op("gpsimd", lambda g, c=c: g.dma_start(out=W_pr[:, c, :], in_=Wp[128 * c:128 * (c + 1), :]),
               incs=[(s_wpr, 16)])
        w_pr_all = s_wpr.count

        op("tensor", lambda t: t.matmul(
            pkv[:].rearrange("p a b -> p (a b)"), zero_row[0:1, 0:128], zero_row[0:1, 0:512],
            start=True, stop=False, skip_group_check=True), waits=[(s_dve, 1)])

        for b0 in range(min(2, NBLK)):
            for tt in range(TPB):
                t = b0 * TPB + tt
                op("sync", lambda s, b0=b0, tt=tt, t=t: s.dma_start(
                    out=x_sb[:, b0 % 2, tt, :], in_=xh[128 * t:128 * (t + 1), :]),
                   incs=[(s_x[b0 % 2], 16)])
            st["x_loaded"][b0] = s_x[b0 % 2].count

        def emit_stats(S, src_ap_fn, first_waits):
            for i in range(TPB):
                op("vector", lambda v, i=i: v.bn_stats(S["bn12"][:, i, 0:6], src_ap_fn(i)[:, 0:512]),
                   waits=first_waits if i == 0 else [])
                op("vector", lambda v, i=i: v.bn_stats(S["bn12"][:, i, 6:12], src_ap_fn(i)[:, 512:1024]),
                   incs=[(s_dve, 1)] if i == TPB - 1 else [])
            op("vector", lambda v, c=s_dve.count: v.wait_ge(s_dve.h, c))
            for i in range(TPB):
                op("vector", lambda v, i=i: v.bn_aggr(S["mvab"][:, i, 0:2], S["bn12"][:, i, 0:6]))
                op("vector", lambda v, i=i: v.bn_aggr(S["mvab"][:, i, 2:4], S["bn12"][:, i, 6:12]),
                   incs=[(s_dve, 1)] if i == TPB - 1 else [])
            op("vector", lambda v, c=s_dve.count: v.wait_ge(s_dve.h, c))
            op("vector", lambda v: v.tensor_tensor(S["mvs4"][:], S["mvab"][:, :, 0:2], S["mvab"][:, :, 2:4], ALU.add))
            op("vector", lambda v: v.tensor_tensor(S["mvd4"][:], S["mvab"][:, :, 0:2], S["mvab"][:, :, 2:4], ALU.subtract),
               incs=[(s_dve, 1)])
            op("vector", lambda v, c=s_dve.count: v.wait_ge(s_dve.h, c))
            op("vector", lambda v: v.tensor_tensor(S["mvp4"][:], S["mvd4"][:], S["mvd4"][:], ALU.mult))
            op("vector", lambda v: v.tensor_scalar(S["mean4"][:], S["mvs4"][:, :, 0], 0.5, None, ALU.mult))
            op("vector", lambda v: v.tensor_scalar(S["var4"][:], S["mvs4"][:, :, 1], 0.5, None, ALU.mult),
               incs=[(s_dve, 1)])
            op("vector", lambda v, c=s_dve.count: v.wait_ge(s_dve.h, c))
            op("vector", lambda v: v.scalar_tensor_tensor(
                S["var4"][:], S["mvp4"][:, :, 0], 0.25, S["var4"][:], ALU.mult, ALU.add), incs=[(s_dve, 1)])
            d_c = s_dve.count
            op("scalar", lambda s: s.activation(out=S["sqv"][:], in_=S["var4"][:], func=AF.Sqrt, bias=LN_EPS),
               waits=[(s_dve, d_c)], incs=[(s_act, 1)])
            S["sqrt_act"] = s_act.count
            op("vector", lambda v: v.reciprocal(S["invs"][:], S["sqv"][:]),
               waits=[(s_act, S["sqrt_act"])], incs=[(s_dve, 1)])
            op("vector", lambda v, c=s_dve.count: v.wait_ge(s_dve.h, c))
            op("vector", lambda v: v.scalar_tensor_tensor(
                S["nmb"][:], S["mean4"][:], -1.0, S["invs"][:], ALU.mult, ALU.mult),
               incs=[(s_dve, 1)])
            return s_dve.count

        def emit_ln1(b):
            bpar = b % 2
            S = stats[0]
            fw = [(s_x[bpar], st["x_loaded"][b])]
            if b > 1:
                fw.append((s_act, S["sqrt_act"]))
            d_c = emit_stats(S, lambda i, bpar=bpar: x_sb[:, bpar, i, :], fw)
            for i in range(TPB):
                waits = [(s_dve, d_c)]
                if b > 0 and i == 0:
                    waits.append((s_trl, st["ln_tr"][b - 1]))
                op("scalar", lambda s, i=i, bpar=bpar, S=S: s.activation(
                    out=ln_tm[:, i, :], in_=x_sb[:, bpar, i, :], func=AF.Identity,
                    bias=S["nmb"][:, i:i + 1], scale=S["invs"][:, i:i + 1]),
                   waits=waits, incs=[(s_act, 1)])
                twaits = []
                if b > 0 and i == 0:
                    twaits.append((s_mm, st["proj_mm"][b - 1]))
                op("scalar", lambda s, i=i: s.dma_start_transpose(
                    ln_fm[:, i, :, :], ln_tm[:, i, :].rearrange("p (c m) -> p c m", c=KC)),
                   waits=twaits, incs=[(s_trl, 16)])
            st["ln_tr"][b] = s_trl.count

        def big_nops():
            op("tensor", lambda t: t.nop())
            op("tensor", lambda t: t.nop())

        def emit_v(b):
            for i in range(TPB):
                for half in range(2):
                    bank = (i * 2 + half) % 2
                    waits = [(s_trl, st["ln_tr"][b]), (s_w, w_bias), (s_wv, w_v_all)] + bank_free[bank]
                    for kc in range(KC):
                        op("tensor", lambda t, i=i, half=half, kc=kc, bank=bank: t.matmul(
                            pbig[bank][:], ln_fm[:, i, kc, :], W_v[:, kc, 512 * half:512 * (half + 1)],
                            start=(kc == 0), stop=False),
                           waits=waits if kc == 0 else [])
                        big_nops()
                    op("tensor", lambda t, half=half, bank=bank: t.matmul(
                        pbig[bank][:], ones_row[:, 0:128], bv_row[:, 512 * half:512 * (half + 1)],
                        start=False, stop=True), incs=[(s_mm, 1)])
                    mm_c = s_mm.count
                    big_nops()
                    ewaits = [(s_mm, mm_c)]
                    if b > 0 and i == 0 and half == 0:
                        ewaits.append((s_trq, st["q8_tr"][b - 1]))
                    op("scalar", lambda s, i=i, half=half, bank=bank: s.activation(
                        out=v_sb[:, i, :].rearrange("p (h w) -> p h w", w=65)[:, 8 * half:8 * (half + 1), 0:64],
                        in_=pbig[bank][:].rearrange("p (h d) -> p h d", d=64), func=AF.Copy),
                       waits=ewaits, incs=[(s_act, 1)])
                    bank_free[bank] = [(s_act, s_act.count)]
            for i in range(TPB):
                op("vector", lambda v, i=i: v.memset(
                    v_sb[:, i, :].rearrange("p (h w) -> p h w", w=65)[:, :, 64:65], 1.0),
                   waits=[(s_trq, st["q8_tr"][b - 1])] if (b > 0 and i == 0) else [],
                   incs=[(s_dve, 1)] if i == TPB - 1 else [])
            st["v_ones_dve"] = s_dve.count
            st["v_ready"][b] = s_act.count

        def emit_qk(b):
            for fc in [8, 9, 10, 11, 12, 13, 14, 15, 0, 1, 2, 3, 4, 5, 6, 7]:
                bank = fc % 2
                waits = [(s_trl, st["ln_tr"][b]), (s_wqk, w_qk_all)] + bank_free[bank]
                for kc in range(KC):
                    op("tensor", lambda t, fc=fc, kc=kc, bank=bank: t.matmul(
                        pbig[bank][:], W_qk[:, kc, 128 * fc:128 * (fc + 1)], ln_fm[:, :, kc, :],
                        start=(kc == 0), stop=(kc == KC - 1)),
                       waits=waits if kc == 0 else [],
                       incs=[(s_mm, 1)] if kc == KC - 1 else [])
                    big_nops()
                mm_c = s_mm.count
                dst = q_fm[:, :, fc, :] if fc < 8 else k_fm[:, :, fc - 8, :]
                ep = fc % 2
                ewaits = [(s_mm, mm_c)]
                if len(st["phi_hist"]) >= 2:
                    ewaits.append((s_pl, st["phi_hist"][-2]))
                op("scalar", lambda s, fc=fc, ep=ep, bank=bank: s.activation(
                    out=e_sb[:, ep, :], in_=pbig[bank][:], func=AF.Exp, bias=bqk[:, fc:fc + 1]),
                   waits=ewaits, incs=[(s_act, 1)])
                a_c = s_act.count
                op("vector", lambda v, fc=fc, ep=ep, bank=bank: v.tensor_scalar(
                    rl_sb[:, ep, :], pbig[bank][:], bqk[:, fc:fc + 1], 0.0, ALU.add, ALU.max),
                   waits=[(s_act, a_c)], incs=[(s_dve, 1)])
                d_c = s_dve.count
                bank_free[bank] = [(s_dve, d_c)]
                pw = [(s_act, a_c)]
                if b > 0 and fc == 8:
                    pw.append((s_mm, st["attn_mm"][b - 1]))
                op("vector", lambda v, ep=ep: v.tensor_scalar(
                    e_sb[:, ep, :], e_sb[:, ep, :], 1.0, None, ALU.min),
                   waits=pw, incs=[(s_dve, 1)])
                op("vector", lambda v, c=s_dve.count: v.wait_ge(s_dve.h, c))
                op("vector", lambda v, ep=ep, dst=dst: v.tensor_tensor(
                    dst, e_sb[:, ep, :].rearrange("p (a m) -> p a m", a=TPB),
                    rl_sb[:, ep, :].rearrange("p (a m) -> p a m", a=TPB), ALU.add),
                   incs=[(s_pl, 1)])
                st["phi_hist"].append(s_pl.count)
                if fc == 15:
                    st["k_ready"][b] = s_pl.count
            st["qk_ready"][b] = s_pl.count

        def emit_ktm(b):
            for i in range(TPB):
                waits = [(s_pl, st["k_ready"][b])]
                if b > 0 and i == 0:
                    waits.append((s_mm, st["ffn_mm"][b - 1]))
                op("scalar", lambda s, i=i: s.dma_start_transpose(
                    k_tm[:, i, :].rearrange("p (c m) -> p c m", c=KC), k_fm[:, i, :, :]),
                   waits=waits, incs=[(s_trk, 16)])
            st["ktm_tr"][b] = s_trk.count

        def emit_w8():
            w8_done = []
            for c in range(KC):
                par = c % 2
                waits = []
                if c >= 2:
                    waits = [(s_act, w8_done[c - 2][0]), (s_dve, w8_done[c - 2][1])]
                op("gpsimd", lambda g, c=c, par=par: g.dma_start(
                    out=r_sb[:, par, :], in_=Wft[128 * c:128 * (c + 1), :]),
                   waits=waits, incs=[(s_stg[par], 16)])
                stg_c = s_stg[par].count
                op("scalar", lambda s, par=par, stg_c=stg_c: s.activation(
                    out=q8_tm[:, par, :], in_=r_sb[:, par, :], func=AF.Copy),
                   waits=[(s_stg[par], stg_c)], incs=[(s_act, 1)])
                a_c = s_act.count
                op("vector", lambda v, c=c, par=par: v.tensor_copy(W_f8[:, c, :], q8_tm[:, par, :]),
                   waits=[(s_act, a_c)], incs=[(s_dve, 1)])
                w8_done.append((a_c, s_dve.count))
            st["w8_all"] = s_dve.count
            st["w8_act_all"] = s_act.count

        def emit_attn(b):
            for ci in range(TPB):
                c = b * TPB + ci
                kmk_w = [(s_pl, st["k_ready"][b])]
                if c > 0:
                    kmk_w.append((s_mm, st["chunk_mm"]))
                op("vector", lambda v, ci=ci: v.tensor_copy(
                    k_mk[0:64, 0, :, :], k_fm[0:64, ci, :, :]), waits=kmk_w)
                op("vector", lambda v, ci=ci: v.tensor_copy(
                    k_mk[64:128, 1, :, :], k_fm[64:128, ci, :, :]), incs=[(s_dve, 1)])
                kmk_dve = s_dve.count
                op("tensor", lambda t: t.matmul(
                    pks[:], zero_row[0:1, 0:128], zero_row[0:1, 0:8],
                    start=True, stop=False, skip_group_check=True),
                   waits=[(s_dve, st["ks_dve"])] if c > 0 else [])
                mask_dve = {}
                for qd in range(4):
                    bank = qd % 2
                    waits = [(s_dve, kmk_dve), (s_trk, st["ktm_tr"][b]),
                             (s_act, st["v_ready"][b]), (s_const, consts_ld)] + bank_free[bank]
                    for hh in range(4):
                        h = qd * 4 + hh
                        op("tensor", lambda t, h=h, hh=hh, ci=ci, bank=bank: t.matmul(
                            pbig[bank][:, 128 * hh:128 * (hh + 1)],
                            k_mk[:, h % 2, h // 2, :], q_fm[:, ci, h // 2, :],
                            start=(hh == 0), stop=(hh == 3)),
                           waits=waits if hh == 0 else [],
                           incs=[(s_mm, 1)] if hh == 3 else [])
                    st_mm = s_mm.count
                    op("vector", lambda v, qd=qd, bank=bank: v.scalar_tensor_tensor(
                        stm[:, 4 * qd:4 * (qd + 1), :], pbig[bank][:].rearrange("p (a m) -> p a m", a=4),
                        1.0, mask_rep[:].rearrange("p (a m) -> p a m", a=4), ALU.mult, ALU.mult),
                       waits=[(s_mm, st_mm)], incs=[(s_dve, 1)])
                    mask_dve[qd] = s_dve.count
                    bank_free[bank] = [(s_dve, mask_dve[qd])]
                for h in range(H):
                    po = 64 * (h % 2)
                    no = _noff(h)
                    waits = [(s_dve, mask_dve[h // 4])]
                    if h == 0:
                        waits.append((s_dve, st["v_ones_dve"]))
                    if c > 0 and h == 0:
                        waits.append((s_act, st["kva_act"]))
                    op("tensor", lambda t, h=h, ci=ci, no=no: t.matmul(
                        pnd[:, no:no + 65], stm[:, h, :],
                        v_sb[:, ci, 65 * h:65 * (h + 1)],
                        start=(h in (0, 7, 14)), stop=False), waits=waits)
                    op("tensor", lambda t, h=h, po=po, ci=ci, no=no: t.matmul(
                        pnd[:, no:no + 65], q_fm[po:po + 64, ci, h // 2, :],
                        kva[po:po + 64, h // 2, 0:D + 1], start=False,
                        stop=(h in (6, 13, 15))))
                    op("tensor", lambda t, h=h, po=po, ci=ci: t.matmul(
                        pkv[po:po + 64, h // 2, :], k_tm[:, ci, 64 * h:64 * (h + 1)],
                        v_sb[:, ci, 65 * h:65 * h + 64],
                        start=False, stop=False, skip_group_check=True))
                    op("tensor", lambda t, h=h, po=po, ci=ci: t.matmul(
                        pks[po:po + 64, h // 2:h // 2 + 1], k_tm[:, ci, 64 * h:64 * (h + 1)],
                        ones_col[:], start=False, stop=False, skip_group_check=True),
                       incs=[(s_mm, 1)] if h == H - 1 else [])
                st["chunk_mm"] = s_mm.count
                st["attn_mm"][b] = st["chunk_mm"]
                kpar, kparn = c % 2, (c + 1) % 2
                waits = [(s_mm, st["chunk_mm"])]
                if c >= 2:
                    waits.append((s_act, st["ksum_read_act"][c - 2]))
                for g, (h0, nh) in enumerate(((0, 7), (7, 7), (14, 2))):
                    op("vector", lambda v, h0=h0, nh=nh, g=g: v.tensor_scalar(
                        den_f[:, h0:h0 + nh],
                        pnd[:, _noff(h0):_noff(h0) + 65 * nh].rearrange(
                            "p (a w) -> p a w", w=65)[:, :, 64],
                        ATT_EPS, None, ALU.add),
                       waits=waits if g == 0 else [],
                       incs=[(s_dve, 1)] if g == 2 else [])
                op("vector", lambda v, c_=s_dve.count: v.wait_ge(s_dve.h, c_))
                op("vector", lambda v: v.reciprocal(rcp_f[:], den_f[:]), incs=[(s_dve, 1)])
                rcp_dve = s_dve.count
                op("vector", lambda v, kpar=kpar, kparn=kparn: v.tensor_tensor(
                    ksum[:, kpar, :], ksum[:, kparn, :], pks[:], ALU.add),
                   incs=[(s_dve, 1)])
                st["ks_dve"] = s_dve.count
                ywaits = [(s_dve, rcp_dve)]
                if c >= 1:
                    ywaits.append((s_try, st["ytr"][c - 1]))
                for g, (h0, nh) in enumerate(((0, 7), (7, 7), (14, 2))):
                    op("vector", lambda v, h0=h0, nh=nh: v.tensor_tensor(
                        y_tm[:, 64 * h0:64 * (h0 + nh)].rearrange("p (a d) -> p a d", d=64),
                        pnd[:, _noff(h0):_noff(h0) + 65 * nh].rearrange(
                            "p (a w) -> p a w", w=65)[:, :, 0:64],
                        rcp_f[:, h0:h0 + nh, None].broadcast_to((128, nh, D)), ALU.mult),
                       waits=ywaits if g == 0 else [],
                       incs=[(s_dve, 1)] if g == 2 else [])
                div_dve = s_dve.count
                st["div_dve"] = div_dve
                op("scalar", lambda s: s.activation(out=kva[:, :, 0:D], in_=pkv[:], func=AF.Copy),
                   waits=[(s_mm, st["chunk_mm"]), (s_dve, st["ks_dve"])])
                op("scalar", lambda s, kpar=kpar: s.activation(
                    out=kva[:, :, D:D + 1], in_=ksum[:, kpar, :, None], func=AF.Copy),
                   incs=[(s_act, 1)])
                st["kva_act"] = s_act.count
                st["ksum_read_act"][c] = st["kva_act"]
                op("scalar", lambda s, ci=ci: s.dma_start_transpose(
                    y_fm[:, ci, :, :], y_tm[:].rearrange("p (c m) -> p c m", c=KC)),
                   waits=[(s_dve, div_dve)], incs=[(s_try, 16)])
                st["ytr"][c] = s_try.count
                if ci > 0:
                    emit_proj_tile(b, ci - 1)

        def emit_proj_tile(b, i):
            bpar = b % 2
            if True:
                for half in range(2):
                    bank = (i * 2 + half) % 2
                    waits = [(s_try, st["ytr"][b * TPB + i]), (s_w, w_bias),
                             (s_wpr, w_pr_all)] + bank_free[bank]
                    for kc in range(KC):
                        op("tensor", lambda t, i=i, half=half, kc=kc, bank=bank: t.matmul(
                            pbig[bank][:], y_fm[:, i, kc, :], W_pr[:, kc, 512 * half:512 * (half + 1)],
                            start=(kc == 0), stop=False),
                           waits=waits if kc == 0 else [])
                        big_nops()
                    op("tensor", lambda t, half=half, bank=bank: t.matmul(
                        pbig[bank][:], ones_row[:, 0:128], bp_row[:, 512 * half:512 * (half + 1)],
                        start=False, stop=True), incs=[(s_mm, 1)])
                    mm_c = s_mm.count
                    big_nops()
                    ewaits = [(s_mm, mm_c)]
                    if b == 0 and i == 0 and half == 0:
                        ewaits.append((s_act, st["w8_act_all"]))
                    if b > 0 and i == 0 and half == 0:
                        ewaits.append((s_act, st["q8_act"][b - 1]))
                    op("vector", lambda v, i=i, half=half, bank=bank, bpar=bpar: v.tensor_tensor(
                        r_sb[:, i, 512 * half:512 * (half + 1)], pbig[bank][:],
                        x_sb[:, bpar, i, 512 * half:512 * (half + 1)], ALU.add),
                       waits=ewaits, incs=[(s_dve, 1)])
                    bank_free[bank] = [(s_dve, s_dve.count)]
            st["proj_mm"][b] = s_mm.count

        def emit_ln2q8(b):
            S = stats[1]
            fw = []
            if b > 0:
                fw.append((s_act, S["sqrt_act"]))
            d_c = emit_stats(S, lambda i: r_sb[:, i, :], fw)
            for i in range(TPB):
                op("scalar", lambda s, i=i, S=S: s.activation(
                    out=q8_tm[:, i, :], in_=r_sb[:, i, :], func=AF.Identity,
                    bias=S["nmb"][:, i:i + 1], scale=S["invs"][:, i:i + 1]),
                   waits=[(s_dve, d_c)] if i == 0 else [], incs=[(s_act, 1)])
                a_c = s_act.count
                cw = [(s_act, a_c)]
                if i == 0:
                    cw.append((s_mm, st["attn_mm"][b]))
                op("vector", lambda v, i=i: v.tensor_copy(v_sb[:, i, 0:C], q8_tm[:, i, :]),
                   waits=cw, incs=[(s_dve, 1)])
                d_c2 = s_dve.count
                op("scalar", lambda s, i=i: s.dma_start_transpose(
                    k_tm[:, i, :].rearrange("p (c m) -> p c m", c=KC),
                    v_sb[:, i, 0:C].rearrange("p (c m) -> p c m", c=KC)),
                   waits=[(s_dve, d_c2)], incs=[(s_trq, 16)])
            st["q8_tr"][b] = s_trq.count
            st["q8_act"][b] = s_act.count

        def emit_ffn(b):
            bpar = b % 2
            for i in range(TPB):
                for half in range(2):
                    bank = (i * 2 + half) % 2
                    waits = [(s_trq, st["q8_tr"][b]), (s_dve, st["w8_all"]),
                             (s_w, w_bias)] + bank_free[bank]
                    for kc in range(KC):
                        op("tensor", lambda t, i=i, half=half, kc=kc, bank=bank: t.matmul(
                            pbig[bank][:], k_tm[:, i, 128 * kc:128 * (kc + 1)],
                            W_f8[:, kc, 512 * half:512 * (half + 1)],
                            start=(kc == 0), stop=False),
                           waits=waits if kc == 0 else [])
                        big_nops()
                    op("tensor", lambda t, half=half, bank=bank: t.matmul(
                        pbig[bank][:], ones_row[:, 0:128], bf_row[:, 512 * half:512 * (half + 1)],
                        start=False, stop=True), incs=[(s_mm, 1)])
                    mm_c = s_mm.count
                    big_nops()
                    op("vector", lambda v, i=i, half=half, bank=bank, bpar=bpar: v.tensor_tensor(
                        x_sb[:, bpar, i, 512 * half:512 * (half + 1)], pbig[bank][:],
                        r_sb[:, i, 512 * half:512 * (half + 1)], ALU.add),
                       waits=[(s_mm, mm_c)], incs=[(s_dve, 1)])
                    bank_free[bank] = [(s_dve, s_dve.count)]
                d_c = s_dve.count
                t = b * TPB + i
                op("sync", lambda s, i=i, t=t, bpar=bpar: s.dma_start(
                    out=outh[128 * t:128 * (t + 1), :], in_=x_sb[:, bpar, i, :]),
                   waits=[(s_dve, d_c)], incs=[(s_out, 16)])
            st["ffn_mm"][b] = s_mm.count
            st["out_done"][b] = s_out.count
            if b + 2 <= NBLK - 1:
                bn = b + 2
                for tt in range(TPB):
                    t = bn * TPB + tt
                    op("sync", lambda s, bn=bn, tt=tt, t=t: s.dma_start(
                        out=x_sb[:, bn % 2, tt, :], in_=xh[128 * t:128 * (t + 1), :]),
                       waits=[(s_out, st["out_done"][b])] if tt == 0 else [],
                       incs=[(s_x[bn % 2], 16)])
                st["x_loaded"][bn] = s_x[bn % 2].count

        emit_ln1(0)
        emit_v(0)
        emit_qk(0)
        emit_ktm(0)
        emit_w8()
        for b in range(NBLK):
            emit_attn(b)
            emit_proj_tile(b, TPB - 1)
            if b + 1 < NBLK:
                emit_ln1(b + 1)
            emit_ln2q8(b)
            if b + 1 < NBLK:
                emit_v(b + 1)
                emit_qk(b + 1)
            emit_ffn(b)
            if b + 1 < NBLK:
                emit_ktm(b + 1)

        op("sync", lambda s: None, waits=[(s_out, s_out.count)])

        with nc.Block() as blk:
            def emit(eng_obj, name):
                for waits, fn, incs in B.eng_ops[name]:
                    for s, v in waits:
                        eng_obj.wait_ge(s.h, v)
                    inst = fn(eng_obj)
                    for jj, (s, n) in enumerate(incs):
                        assert jj == 0 and inst is not None
                        inst.then_inc(s.h, n)

            @blk.sync
            def _(e):
                emit(e, "sync")

            @blk.gpsimd
            def _(e):
                emit(e, "gpsimd")

            @blk.scalar
            def _(e):
                emit(e, "scalar")

            @blk.vector
            def _(e):
                emit(e, "vector")

            @blk.tensor
            def _(e):
                emit(e, "tensor")

    return nc


_NC_CACHE = None


def kernel(L0_attn_x, L0_attn_W_attn, L0_attn_b_attn, L0_attn_W_proj,
           L0_attn_b_proj, L0_ffn_weight, L0_ffn_bias):
    global _NC_CACHE
    x = np.ascontiguousarray(np.asarray(L0_attn_x, dtype=np.float32))
    Wa = np.ascontiguousarray(np.asarray(L0_attn_W_attn, dtype=np.float32))
    ba = np.ascontiguousarray(np.asarray(L0_attn_b_attn, dtype=np.float32))
    Wp = np.ascontiguousarray(np.asarray(L0_attn_W_proj, dtype=np.float32))
    bp = np.ascontiguousarray(np.asarray(L0_attn_b_proj, dtype=np.float32))
    Wf = np.ascontiguousarray(np.asarray(L0_ffn_weight, dtype=np.float32))
    bfv = np.ascontiguousarray(np.asarray(L0_ffn_bias, dtype=np.float32))
    Wft = np.ascontiguousarray(Wf.T)
    mask = np.ascontiguousarray(
        np.tile(np.triu(np.ones((128, 128), np.float32)), (1, 4)).astype(ml_dtypes.bfloat16))
    bqk_h = np.ascontiguousarray(ba[:2048].reshape(16, 128).T)

    if _NC_CACHE is None:
        _NC_CACHE = build_nc()
    nc = _NC_CACHE
    in_maps = []
    for b in range(8):
        in_maps.append({"x": np.ascontiguousarray(x[b]), "W_attn": Wa, "b_attn": ba,
                        "W_proj": Wp, "b_proj": bp, "Wt_ffn": Wft, "b_ffn": bfv,
                        "mask": mask, "bqk_h": bqk_h})
    res = run_bass_kernel_spmd(nc, in_maps, core_ids=list(range(8)))
    out = np.stack([np.asarray(res.results[b]["out"]) for b in range(8)], axis=0)
    return out.astype(np.float32)


# version 10
# speedup vs baseline: 1.1273x; 1.0268x over previous
import numpy as np
import ml_dtypes
import concourse.bass as bass
import concourse.mybir as mybir
from concourse.bass_utils import run_bass_kernel_spmd

F32 = mybir.dt.float32
BF16 = mybir.dt.bfloat16
FP8 = mybir.dt.float8e4
AF = mybir.ActivationFunctionType
ALU = mybir.AluOpType

T = 2048
C = 1024
H = 16
D = 64
KC = 8
NBLK = 4
TPB = 4
LN_EPS = 1e-5
ATT_EPS = 1e-6


class _Sem:
    def __init__(self, handle):
        self.h = handle
        self.count = 0


class _Builder:
    def __init__(self):
        self.eng_ops = {e: [] for e in ("sync", "scalar", "vector", "gpsimd", "tensor")}

    def op(self, eng, fn, waits=(), incs=()):
        for s, v in waits:
            assert v <= s.count, f"wait {v} > produced {s.count} ({eng})"
        self.eng_ops[eng].append((list(waits), fn, list(incs)))
        for s, n in incs:
            s.count += n


def build_nc():
    nc = bass.Bass("TRN2", target_bir_lowering=False, detect_race_conditions=False)

    xh = nc.dram_tensor("x", [T, C], F32, kind="ExternalInput").ap()
    Wa = nc.dram_tensor("W_attn", [C, 3 * C], F32, kind="ExternalInput").ap()
    ba = nc.dram_tensor("b_attn", [3 * C], F32, kind="ExternalInput").ap()
    Wp = nc.dram_tensor("W_proj", [C, C], F32, kind="ExternalInput").ap()
    bp = nc.dram_tensor("b_proj", [C], F32, kind="ExternalInput").ap()
    Wft = nc.dram_tensor("Wt_ffn", [C, C], F32, kind="ExternalInput").ap()
    bf = nc.dram_tensor("b_ffn", [C], F32, kind="ExternalInput").ap()
    maskh = nc.dram_tensor("mask", [128, 512], BF16, kind="ExternalInput").ap()
    bqkh = nc.dram_tensor("bqk_h", [128, 16], F32, kind="ExternalInput").ap()
    outh = nc.dram_tensor("out", [T, C], F32, kind="ExternalOutput").ap()

    import contextlib
    ctx = contextlib.ExitStack()
    sb = lambda name, shape, dt: ctx.enter_context(nc.sbuf_tensor(name, shape, dt))
    ps = lambda name, shape, dt: ctx.enter_context(nc.psum_tensor(name, shape, dt))
    sem = lambda name: _Sem(ctx.enter_context(nc.semaphore(name)))

    with ctx:
        W_qk = sb("W_qk", [128, KC, 2 * C], BF16)
        W_v = sb("W_v", [128, KC, C], BF16)
        W_pr = sb("W_pr", [128, KC, C], BF16)
        W_f8 = sb("W_f8", [128, KC, C], BF16)
        bqk = sb("bqk", [128, 16], F32)
        bv_row = sb("bv_row", [1, C], BF16)
        bp_row = sb("bp_row", [1, C], BF16)
        bf_row = sb("bf_row", [1, C], BF16)
        ones_row = sb("ones_row", [1, 128], BF16)
        ones_col = sb("ones_col", [128, 1], BF16)
        zero_row = sb("zero_row", [1, 512], BF16)
        mask_rep = sb("mask_rep", [128, 512], BF16)
        x_sb = sb("x_sb", [128, 2, TPB, C], F32)
        r_sb = sb("r_sb", [128, TPB, C], F32)
        ln_tm = sb("ln_tm", [128, TPB, C], BF16)
        ln_fm = sb("ln_fm", [128, TPB, KC, 128], BF16)
        y_fm = ln_fm
        q_fm = sb("q_fm", [128, TPB, KC, 128], BF16)
        k_fm = sb("k_fm", [128, TPB, KC, 128], BF16)
        k_tm = sb("k_tm", [128, TPB, C], BF16)
        v_sb = sb("v_sb", [128, TPB, H * 65], BF16)
        k_mk = sb("k_mk", [128, 2, KC, 128], BF16)
        stm = sb("stm", [128, H, 128], BF16)
        y_tm = sb("y_tm", [128, C], BF16)
        q8_tm = sb("q8_tm", [128, TPB, C], FP8)
        e_sb = sb("e_sb", [128, 2, 512], BF16)
        rl_sb = sb("rl_sb", [128, 2, 512], BF16)
        kva = sb("kva", [128, H // 2, D + 1], BF16)
        stats = []
        for tag in ("a", "b"):
            stats.append(dict(
                bn12=sb(f"bn12{tag}", [128, TPB, 12], F32),
                mvab=sb(f"mvab{tag}", [128, TPB, 4], F32),
                mvs4=sb(f"mvs4{tag}", [128, TPB, 2], F32),
                mvd4=sb(f"mvd4{tag}", [128, TPB, 2], F32),
                mvp4=sb(f"mvp4{tag}", [128, TPB, 2], F32),
                mean4=sb(f"mean4{tag}", [128, TPB], F32),
                var4=sb(f"var4{tag}", [128, TPB], F32),
                sqv=sb(f"sqv{tag}", [128, TPB], F32),
                invs=sb(f"invs{tag}", [128, TPB], F32),
                nmb=sb(f"nmb{tag}", [128, TPB], F32),
                sqrt_act=0,
            ))
        den_f = sb("den_f", [128, 16], F32)
        rcp_f = sb("rcp_f", [128, 16], F32)
        c_eps5 = sb("c_eps5", [128, 1], F32)
        c_eps6 = sb("c_eps6", [128, 1], F32)
        c_m1 = sb("c_m1", [128, 1], F32)
        nc.const_aps.register(F32, LN_EPS, c_eps5[:])
        nc.const_aps.register(F32, ATT_EPS, c_eps6[:])
        nc.const_aps.register(F32, -1.0, c_m1[:])
        pbig = [ps("pbig0", [128, 512], F32), ps("pbig1", [128, 512], F32)]
        pnd = ps("pnd", [128, 1536], F32)
        pkv = ps("pkv", [128, 1024], F32)

        s_x = [sem("s_x0"), sem("s_x1")]
        s_const = sem("s_const")
        s_w = sem("s_w")
        s_wv = sem("s_wv")
        s_wqk = sem("s_wqk")
        s_wpr = sem("s_wpr")
        s_stg = [sem("s_stg0"), sem("s_stg1")]
        s_trl = sem("s_trl")
        s_trk = sem("s_trk")
        s_try = sem("s_try")
        s_trq = sem("s_trq")
        s_mm = sem("s_mm")
        s_act = sem("s_act")
        s_dve = sem("s_dve")
        s_pl = sem("s_pl")
        s_out = sem("s_out")

        B = _Builder()
        op = B.op

        def _koff(s):
            return 65 * s if s < 7 else 512

        def _noff(h):
            if h < 7:
                return 65 * h
            if h < 14:
                return 512 + 65 * (h - 7)
            return 1024 + 65 * (h - 14)

        st = dict(
            x_loaded={}, out_done={}, ln_tr={}, ktm_tr={}, ytr={},
            qk_ready={}, k_ready={}, v_ready={}, q8_tr={}, q8_act={},
            attn_mm={}, proj_mm={}, ffn_mm={}, nm_dve={},
            phi_hist=[], div_dve=0, kva_act=0, ks_dve=0, ksum_read_act={},
            chunk_mm=0, w8_all=0, w8_act_all=0,
        )
        bank_free = [[], []]

        op("vector", lambda v: v.memset(ones_row[:], 1.0))
        op("vector", lambda v: v.memset(zero_row[:], 0.0))
        op("vector", lambda v: v.memset(ones_col[:], 1.0))
        op("vector", lambda v: v.memset(kva[:], 0.0))
        op("vector", lambda v: v.memset(k_mk[:], 0.0))
        op("vector", lambda v: v.memset(c_eps5[:], LN_EPS))
        op("vector", lambda v: v.memset(c_eps6[:], ATT_EPS))
        op("vector", lambda v: v.memset(c_m1[:], -1.0), incs=[(s_dve, 1)])

        op("sync", lambda s: s.dma_start(out=mask_rep[:], in_=maskh), incs=[(s_const, 16)])
        op("sync", lambda s: s.dma_start(out=bqk[:], in_=bqkh), incs=[(s_const, 16)])
        consts_ld = s_const.count

        op("gpsimd", lambda g: g.dma_start(out=bv_row[:], in_=ba[2048:3072].rearrange("(a f) -> a f", a=1)),
           incs=[(s_w, 16)])
        op("gpsimd", lambda g: g.dma_start(out=bp_row[:], in_=bp.rearrange("(a f) -> a f", a=1)),
           incs=[(s_w, 16)])
        op("gpsimd", lambda g: g.dma_start(out=bf_row[:], in_=bf.rearrange("(a f) -> a f", a=1)),
           incs=[(s_w, 16)])
        w_bias = s_w.count
        for c in range(KC):
            op("gpsimd", lambda g, c=c: g.dma_start(out=W_v[:, c, :], in_=Wa[128 * c:128 * (c + 1), 2048:3072]),
               incs=[(s_wv, 16)])
        w_v_all = s_wv.count
        for c in range(KC):
            op("gpsimd", lambda g, c=c: g.dma_start(out=W_qk[:, c, :], in_=Wa[128 * c:128 * (c + 1), 0:2048]),
               incs=[(s_wqk, 16)])
        w_qk_all = s_wqk.count
        for c in range(KC):
            op("gpsimd", lambda g, c=c: g.dma_start(out=W_pr[:, c, :], in_=Wp[128 * c:128 * (c + 1), :]),
               incs=[(s_wpr, 16)])
        w_pr_all = s_wpr.count

        op("tensor", lambda t: t.matmul(
            pkv[:, 0:512], zero_row[0:1, 0:128], zero_row[0:1, 0:512],
            start=True, stop=False, skip_group_check=True), waits=[(s_dve, 1)])
        op("tensor", lambda t: t.matmul(
            pkv[:, 512:1024], zero_row[0:1, 0:128], zero_row[0:1, 0:512],
            start=True, stop=False, skip_group_check=True))

        for b0 in range(min(2, NBLK)):
            for tt in range(TPB):
                t = b0 * TPB + tt
                op("sync", lambda s, b0=b0, tt=tt, t=t: s.dma_start(
                    out=x_sb[:, b0 % 2, tt, :], in_=xh[128 * t:128 * (t + 1), :]),
                   incs=[(s_x[b0 % 2], 16)])
            st["x_loaded"][b0] = s_x[b0 % 2].count

        def emit_stats(S, src_ap_fn, first_waits):
            for i in range(TPB):
                op("vector", lambda v, i=i: v.bn_stats(S["bn12"][:, i, 0:6], src_ap_fn(i)[:, 0:512]),
                   waits=first_waits if i == 0 else [])
                op("vector", lambda v, i=i: v.bn_stats(S["bn12"][:, i, 6:12], src_ap_fn(i)[:, 512:1024]),
                   incs=[(s_dve, 1)] if i == TPB - 1 else [])
            op("vector", lambda v, c=s_dve.count: v.wait_ge(s_dve.h, c))
            for i in range(TPB):
                op("vector", lambda v, i=i: v.bn_aggr(S["mvab"][:, i, 0:2], S["bn12"][:, i, 0:6]))
                op("vector", lambda v, i=i: v.bn_aggr(S["mvab"][:, i, 2:4], S["bn12"][:, i, 6:12]),
                   incs=[(s_dve, 1)] if i == TPB - 1 else [])
            op("vector", lambda v, c=s_dve.count: v.wait_ge(s_dve.h, c))
            op("vector", lambda v: v.tensor_tensor(S["mvs4"][:], S["mvab"][:, :, 0:2], S["mvab"][:, :, 2:4], ALU.add))
            op("vector", lambda v: v.tensor_tensor(S["mvd4"][:], S["mvab"][:, :, 0:2], S["mvab"][:, :, 2:4], ALU.subtract),
               incs=[(s_dve, 1)])
            op("vector", lambda v, c=s_dve.count: v.wait_ge(s_dve.h, c))
            op("vector", lambda v: v.tensor_tensor(S["mvp4"][:], S["mvd4"][:], S["mvd4"][:], ALU.mult))
            op("vector", lambda v: v.tensor_scalar(S["mean4"][:], S["mvs4"][:, :, 0], 0.5, None, ALU.mult))
            op("vector", lambda v: v.tensor_scalar(S["var4"][:], S["mvs4"][:, :, 1], 0.5, None, ALU.mult),
               incs=[(s_dve, 1)])
            op("vector", lambda v, c=s_dve.count: v.wait_ge(s_dve.h, c))
            op("vector", lambda v: v.scalar_tensor_tensor(
                S["var4"][:], S["mvp4"][:, :, 0], 0.25, S["var4"][:], ALU.mult, ALU.add), incs=[(s_dve, 1)])
            d_c = s_dve.count
            op("scalar", lambda s: s.activation(out=S["sqv"][:], in_=S["var4"][:], func=AF.Sqrt, bias=LN_EPS),
               waits=[(s_dve, d_c)], incs=[(s_act, 1)])
            S["sqrt_act"] = s_act.count
            op("vector", lambda v: v.reciprocal(S["invs"][:], S["sqv"][:]),
               waits=[(s_act, S["sqrt_act"])], incs=[(s_dve, 1)])
            op("vector", lambda v, c=s_dve.count: v.wait_ge(s_dve.h, c))
            op("vector", lambda v: v.scalar_tensor_tensor(
                S["nmb"][:], S["mean4"][:], -1.0, S["invs"][:], ALU.mult, ALU.mult),
               incs=[(s_dve, 1)])
            return s_dve.count

        def emit_ln1(b):
            bpar = b % 2
            S = stats[0]
            fw = [(s_x[bpar], st["x_loaded"][b])]
            if b > 1:
                fw.append((s_act, S["sqrt_act"]))
            d_c = emit_stats(S, lambda i, bpar=bpar: x_sb[:, bpar, i, :], fw)
            for i in range(TPB):
                waits = [(s_dve, d_c)]
                if b > 0 and i == 0:
                    waits.append((s_trl, st["ln_tr"][b - 1]))
                op("scalar", lambda s, i=i, bpar=bpar, S=S: s.activation(
                    out=ln_tm[:, i, :], in_=x_sb[:, bpar, i, :], func=AF.Identity,
                    bias=S["nmb"][:, i:i + 1], scale=S["invs"][:, i:i + 1]),
                   waits=waits, incs=[(s_act, 1)])
                twaits = []
                if b > 0 and i == 0:
                    twaits.append((s_mm, st["proj_mm"][b - 1]))
                op("scalar", lambda s, i=i: s.dma_start_transpose(
                    ln_fm[:, i, :, :], ln_tm[:, i, :].rearrange("p (c m) -> p c m", c=KC)),
                   waits=twaits, incs=[(s_trl, 16)])
            st["ln_tr"][b] = s_trl.count

        def big_nops():
            op("tensor", lambda t: t.nop())
            op("tensor", lambda t: t.nop())

        def emit_v(b):
            for i in range(TPB):
                for half in range(2):
                    bank = (i * 2 + half) % 2
                    waits = [(s_trl, st["ln_tr"][b]), (s_w, w_bias), (s_wv, w_v_all)] + bank_free[bank]
                    for kc in range(KC):
                        op("tensor", lambda t, i=i, half=half, kc=kc, bank=bank: t.matmul(
                            pbig[bank][:], ln_fm[:, i, kc, :], W_v[:, kc, 512 * half:512 * (half + 1)],
                            start=(kc == 0), stop=False),
                           waits=waits if kc == 0 else [])
                        big_nops()
                    op("tensor", lambda t, half=half, bank=bank: t.matmul(
                        pbig[bank][:], ones_row[:, 0:128], bv_row[:, 512 * half:512 * (half + 1)],
                        start=False, stop=True), incs=[(s_mm, 1)])
                    mm_c = s_mm.count
                    big_nops()
                    ewaits = [(s_mm, mm_c)]
                    if b > 0 and i == 0 and half == 0:
                        ewaits.append((s_trq, st["q8_tr"][b - 1]))
                    op("scalar", lambda s, i=i, half=half, bank=bank: s.activation(
                        out=v_sb[:, i, :].rearrange("p (h w) -> p h w", w=65)[:, 8 * half:8 * (half + 1), 0:64],
                        in_=pbig[bank][:].rearrange("p (h d) -> p h d", d=64), func=AF.Copy),
                       waits=ewaits, incs=[(s_act, 1)])
                    bank_free[bank] = [(s_act, s_act.count)]
            for i in range(TPB):
                op("vector", lambda v, i=i: v.memset(
                    v_sb[:, i, :].rearrange("p (h w) -> p h w", w=65)[:, :, 64:65], 1.0),
                   waits=[(s_trq, st["q8_tr"][b - 1])] if (b > 0 and i == 0) else [],
                   incs=[(s_dve, 1)] if i == TPB - 1 else [])
            st["v_ones_dve"] = s_dve.count
            st["v_ready"][b] = s_act.count

        def emit_qk(b):
            for fc in [8, 9, 10, 11, 12, 13, 14, 15, 0, 1, 2, 3, 4, 5, 6, 7]:
                bank = fc % 2
                waits = [(s_trl, st["ln_tr"][b]), (s_wqk, w_qk_all)] + bank_free[bank]
                for kc in range(KC):
                    op("tensor", lambda t, fc=fc, kc=kc, bank=bank: t.matmul(
                        pbig[bank][:], W_qk[:, kc, 128 * fc:128 * (fc + 1)], ln_fm[:, :, kc, :],
                        start=(kc == 0), stop=(kc == KC - 1)),
                       waits=waits if kc == 0 else [],
                       incs=[(s_mm, 1)] if kc == KC - 1 else [])
                    big_nops()
                mm_c = s_mm.count
                dst = q_fm[:, :, fc, :] if fc < 8 else k_fm[:, :, fc - 8, :]
                ep = fc % 2
                ewaits = [(s_mm, mm_c)]
                if len(st["phi_hist"]) >= 2:
                    ewaits.append((s_pl, st["phi_hist"][-2]))
                op("scalar", lambda s, fc=fc, ep=ep, bank=bank: s.activation(
                    out=e_sb[:, ep, :], in_=pbig[bank][:], func=AF.Exp, bias=bqk[:, fc:fc + 1]),
                   waits=ewaits, incs=[(s_act, 1)])
                a_c = s_act.count
                op("vector", lambda v, fc=fc, ep=ep, bank=bank: v.tensor_scalar(
                    rl_sb[:, ep, :], pbig[bank][:], bqk[:, fc:fc + 1], 0.0, ALU.add, ALU.max),
                   waits=[(s_act, a_c)], incs=[(s_dve, 1)])
                d_c = s_dve.count
                bank_free[bank] = [(s_dve, d_c)]
                pw = [(s_act, a_c)]
                if b > 0 and fc == 8:
                    pw.append((s_mm, st["attn_mm"][b - 1]))
                op("vector", lambda v, ep=ep: v.tensor_scalar(
                    e_sb[:, ep, :], e_sb[:, ep, :], 1.0, None, ALU.min),
                   waits=pw, incs=[(s_dve, 1)])
                op("vector", lambda v, c=s_dve.count: v.wait_ge(s_dve.h, c))
                op("vector", lambda v, ep=ep, dst=dst: v.tensor_tensor(
                    dst, e_sb[:, ep, :].rearrange("p (a m) -> p a m", a=TPB),
                    rl_sb[:, ep, :].rearrange("p (a m) -> p a m", a=TPB), ALU.add),
                   incs=[(s_pl, 1)])
                st["phi_hist"].append(s_pl.count)
                if fc == 15:
                    st["k_ready"][b] = s_pl.count
            st["qk_ready"][b] = s_pl.count

        def emit_ktm(b):
            for i in range(TPB):
                waits = [(s_pl, st["k_ready"][b])]
                if b > 0 and i == 0:
                    waits.append((s_mm, st["ffn_mm"][b - 1]))
                op("scalar", lambda s, i=i: s.dma_start_transpose(
                    k_tm[:, i, :].rearrange("p (c m) -> p c m", c=KC), k_fm[:, i, :, :]),
                   waits=waits, incs=[(s_trk, 16)])
            st["ktm_tr"][b] = s_trk.count

        def emit_w8():
            w8_done = []
            for c in range(KC):
                par = c % 2
                waits = []
                if c >= 2:
                    waits = [(s_act, w8_done[c - 2][0]), (s_dve, w8_done[c - 2][1])]
                op("gpsimd", lambda g, c=c, par=par: g.dma_start(
                    out=r_sb[:, par, :], in_=Wft[128 * c:128 * (c + 1), :]),
                   waits=waits, incs=[(s_stg[par], 16)])
                stg_c = s_stg[par].count
                op("scalar", lambda s, par=par, stg_c=stg_c: s.activation(
                    out=q8_tm[:, par, :], in_=r_sb[:, par, :], func=AF.Copy),
                   waits=[(s_stg[par], stg_c)], incs=[(s_act, 1)])
                a_c = s_act.count
                op("vector", lambda v, c=c, par=par: v.tensor_copy(W_f8[:, c, :], q8_tm[:, par, :]),
                   waits=[(s_act, a_c)], incs=[(s_dve, 1)])
                w8_done.append((a_c, s_dve.count))
            st["w8_all"] = s_dve.count
            st["w8_act_all"] = s_act.count

        def emit_attn(b):
            for ci in range(TPB):
                c = b * TPB + ci
                kmk_w = [(s_pl, st["k_ready"][b])]
                if c > 0:
                    kmk_w.append((s_mm, st["chunk_mm"]))
                op("vector", lambda v, ci=ci: v.tensor_copy(
                    k_mk[0:64, 0, :, :], k_fm[0:64, ci, :, :]), waits=kmk_w)
                op("vector", lambda v, ci=ci: v.tensor_copy(
                    k_mk[64:128, 1, :, :], k_fm[64:128, ci, :, :]), incs=[(s_dve, 1)])
                kmk_dve = s_dve.count
                mask_dve = {}
                for qd in range(4):
                    bank = qd % 2
                    waits = [(s_dve, kmk_dve), (s_trk, st["ktm_tr"][b]),
                             (s_act, st["v_ready"][b]), (s_const, consts_ld)] + bank_free[bank]
                    for hh in range(4):
                        h = qd * 4 + hh
                        op("tensor", lambda t, h=h, hh=hh, ci=ci, bank=bank: t.matmul(
                            pbig[bank][:, 128 * hh:128 * (hh + 1)],
                            k_mk[:, h % 2, h // 2, :], q_fm[:, ci, h // 2, :],
                            start=(hh == 0), stop=(hh == 3)),
                           waits=waits if hh == 0 else [],
                           incs=[(s_mm, 1)] if hh == 3 else [])
                    st_mm = s_mm.count
                    op("vector", lambda v, qd=qd, bank=bank: v.scalar_tensor_tensor(
                        stm[:, 4 * qd:4 * (qd + 1), :], pbig[bank][:].rearrange("p (a m) -> p a m", a=4),
                        1.0, mask_rep[:].rearrange("p (a m) -> p a m", a=4), ALU.mult, ALU.mult),
                       waits=[(s_mm, st_mm)], incs=[(s_dve, 1)])
                    mask_dve[qd] = s_dve.count
                    bank_free[bank] = [(s_dve, mask_dve[qd])]
                for h in range(H):
                    po = 64 * (h % 2)
                    no = _noff(h)
                    waits = [(s_dve, mask_dve[h // 4])]
                    if h == 0:
                        waits.append((s_dve, st["v_ones_dve"]))
                    if c > 0 and h == 0:
                        waits.append((s_act, st["kva_act"]))
                    op("tensor", lambda t, h=h, ci=ci, no=no: t.matmul(
                        pnd[:, no:no + 65], stm[:, h, :],
                        v_sb[:, ci, 65 * h:65 * (h + 1)],
                        start=(h in (0, 7, 14)), stop=False), waits=waits)
                    op("tensor", lambda t, h=h, po=po, ci=ci, no=no: t.matmul(
                        pnd[:, no:no + 65], q_fm[po:po + 64, ci, h // 2, :],
                        kva[po:po + 64, h // 2, 0:D + 1], start=False,
                        stop=(h in (6, 13, 15))))
                    op("tensor", lambda t, h=h, po=po, ci=ci: t.matmul(
                        pkv[po:po + 64, _koff(h // 2):_koff(h // 2) + 65],
                        k_tm[:, ci, 64 * h:64 * (h + 1)],
                        v_sb[:, ci, 65 * h:65 * (h + 1)],
                        start=False, stop=False, skip_group_check=True),
                       incs=[(s_mm, 1)] if h == H - 1 else [])
                st["chunk_mm"] = s_mm.count
                st["attn_mm"][b] = st["chunk_mm"]
                waits = [(s_mm, st["chunk_mm"])]
                for g, (h0, nh) in enumerate(((0, 7), (7, 7), (14, 2))):
                    op("vector", lambda v, h0=h0, nh=nh, g=g: v.tensor_scalar(
                        den_f[:, h0:h0 + nh],
                        pnd[:, _noff(h0):_noff(h0) + 65 * nh].rearrange(
                            "p (a w) -> p a w", w=65)[:, :, 64],
                        ATT_EPS, None, ALU.add),
                       waits=waits if g == 0 else [],
                       incs=[(s_dve, 1)] if g == 2 else [])
                op("vector", lambda v, c_=s_dve.count: v.wait_ge(s_dve.h, c_))
                op("vector", lambda v: v.reciprocal(rcp_f[:], den_f[:]), incs=[(s_dve, 1)])
                rcp_dve = s_dve.count
                ywaits = [(s_dve, rcp_dve)]
                if c >= 1:
                    ywaits.append((s_try, st["ytr"][c - 1]))
                for g, (h0, nh) in enumerate(((0, 7), (7, 7), (14, 2))):
                    op("vector", lambda v, h0=h0, nh=nh: v.tensor_tensor(
                        y_tm[:, 64 * h0:64 * (h0 + nh)].rearrange("p (a d) -> p a d", d=64),
                        pnd[:, _noff(h0):_noff(h0) + 65 * nh].rearrange(
                            "p (a w) -> p a w", w=65)[:, :, 0:64],
                        rcp_f[:, h0:h0 + nh, None].broadcast_to((128, nh, D)), ALU.mult),
                       waits=ywaits if g == 0 else [],
                       incs=[(s_dve, 1)] if g == 2 else [])
                div_dve = s_dve.count
                st["div_dve"] = div_dve
                op("scalar", lambda s: s.activation(
                    out=kva[:, 0:7, :], in_=pkv[:, 0:455].rearrange("p (a w) -> p a w", w=65),
                    func=AF.Copy),
                   waits=[(s_mm, st["chunk_mm"])])
                op("scalar", lambda s: s.activation(
                    out=kva[:, 7, :], in_=pkv[:, 512:577], func=AF.Copy),
                   incs=[(s_act, 1)])
                st["kva_act"] = s_act.count
                op("scalar", lambda s, ci=ci: s.dma_start_transpose(
                    y_fm[:, ci, :, :], y_tm[:].rearrange("p (c m) -> p c m", c=KC)),
                   waits=[(s_dve, div_dve)], incs=[(s_try, 16)])
                st["ytr"][c] = s_try.count
                if ci > 0:
                    emit_proj_tile(b, ci - 1)

        def emit_proj_tile(b, i):
            bpar = b % 2
            if True:
                for half in range(2):
                    bank = (i * 2 + half) % 2
                    waits = [(s_try, st["ytr"][b * TPB + i]), (s_w, w_bias),
                             (s_wpr, w_pr_all)] + bank_free[bank]
                    for kc in range(KC):
                        op("tensor", lambda t, i=i, half=half, kc=kc, bank=bank: t.matmul(
                            pbig[bank][:], y_fm[:, i, kc, :], W_pr[:, kc, 512 * half:512 * (half + 1)],
                            start=(kc == 0), stop=False),
                           waits=waits if kc == 0 else [])
                        big_nops()
                    op("tensor", lambda t, half=half, bank=bank: t.matmul(
                        pbig[bank][:], ones_row[:, 0:128], bp_row[:, 512 * half:512 * (half + 1)],
                        start=False, stop=True), incs=[(s_mm, 1)])
                    mm_c = s_mm.count
                    big_nops()
                    ewaits = [(s_mm, mm_c)]
                    if b == 0 and i == 0 and half == 0:
                        ewaits.append((s_act, st["w8_act_all"]))
                    if b > 0 and i == 0 and half == 0:
                        ewaits.append((s_act, st["q8_act"][b - 1]))
                    op("vector", lambda v, i=i, half=half, bank=bank, bpar=bpar: v.tensor_tensor(
                        r_sb[:, i, 512 * half:512 * (half + 1)], pbig[bank][:],
                        x_sb[:, bpar, i, 512 * half:512 * (half + 1)], ALU.add),
                       waits=ewaits, incs=[(s_dve, 1)])
                    bank_free[bank] = [(s_dve, s_dve.count)]
            st["proj_mm"][b] = s_mm.count

        def emit_ln2q8(b):
            S = stats[1]
            fw = []
            if b > 0:
                fw.append((s_act, S["sqrt_act"]))
            d_c = emit_stats(S, lambda i: r_sb[:, i, :], fw)
            for i in range(TPB):
                op("scalar", lambda s, i=i, S=S: s.activation(
                    out=q8_tm[:, i, :], in_=r_sb[:, i, :], func=AF.Identity,
                    bias=S["nmb"][:, i:i + 1], scale=S["invs"][:, i:i + 1]),
                   waits=[(s_dve, d_c)] if i == 0 else [], incs=[(s_act, 1)])
                a_c = s_act.count
                cw = [(s_act, a_c)]
                if i == 0:
                    cw.append((s_mm, st["attn_mm"][b]))
                op("vector", lambda v, i=i: v.tensor_copy(v_sb[:, i, 0:C], q8_tm[:, i, :]),
                   waits=cw, incs=[(s_dve, 1)])
                d_c2 = s_dve.count
                op("scalar", lambda s, i=i: s.dma_start_transpose(
                    k_tm[:, i, :].rearrange("p (c m) -> p c m", c=KC),
                    v_sb[:, i, 0:C].rearrange("p (c m) -> p c m", c=KC)),
                   waits=[(s_dve, d_c2)], incs=[(s_trq, 16)])
            st["q8_tr"][b] = s_trq.count
            st["q8_act"][b] = s_act.count

        def emit_ffn(b):
            bpar = b % 2
            for i in range(TPB):
                for half in range(2):
                    bank = (i * 2 + half) % 2
                    waits = [(s_trq, st["q8_tr"][b]), (s_dve, st["w8_all"]),
                             (s_w, w_bias)] + bank_free[bank]
                    for kc in range(KC):
                        op("tensor", lambda t, i=i, half=half, kc=kc, bank=bank: t.matmul(
                            pbig[bank][:], k_tm[:, i, 128 * kc:128 * (kc + 1)],
                            W_f8[:, kc, 512 * half:512 * (half + 1)],
                            start=(kc == 0), stop=False),
                           waits=waits if kc == 0 else [])
                        big_nops()
                    op("tensor", lambda t, half=half, bank=bank: t.matmul(
                        pbig[bank][:], ones_row[:, 0:128], bf_row[:, 512 * half:512 * (half + 1)],
                        start=False, stop=True), incs=[(s_mm, 1)])
                    mm_c = s_mm.count
                    big_nops()
                    op("vector", lambda v, i=i, half=half, bank=bank, bpar=bpar: v.tensor_tensor(
                        x_sb[:, bpar, i, 512 * half:512 * (half + 1)], pbig[bank][:],
                        r_sb[:, i, 512 * half:512 * (half + 1)], ALU.add),
                       waits=[(s_mm, mm_c)], incs=[(s_dve, 1)])
                    bank_free[bank] = [(s_dve, s_dve.count)]
                d_c = s_dve.count
                t = b * TPB + i
                op("sync", lambda s, i=i, t=t, bpar=bpar: s.dma_start(
                    out=outh[128 * t:128 * (t + 1), :], in_=x_sb[:, bpar, i, :]),
                   waits=[(s_dve, d_c)], incs=[(s_out, 16)])
            st["ffn_mm"][b] = s_mm.count
            st["out_done"][b] = s_out.count
            if b + 2 <= NBLK - 1:
                bn = b + 2
                for tt in range(TPB):
                    t = bn * TPB + tt
                    op("sync", lambda s, bn=bn, tt=tt, t=t: s.dma_start(
                        out=x_sb[:, bn % 2, tt, :], in_=xh[128 * t:128 * (t + 1), :]),
                       waits=[(s_out, st["out_done"][b])] if tt == 0 else [],
                       incs=[(s_x[bn % 2], 16)])
                st["x_loaded"][bn] = s_x[bn % 2].count

        emit_ln1(0)
        emit_v(0)
        emit_qk(0)
        emit_ktm(0)
        emit_w8()
        for b in range(NBLK):
            emit_attn(b)
            emit_proj_tile(b, TPB - 1)
            if b + 1 < NBLK:
                emit_ln1(b + 1)
            emit_ln2q8(b)
            if b + 1 < NBLK:
                emit_v(b + 1)
                emit_qk(b + 1)
            emit_ffn(b)
            if b + 1 < NBLK:
                emit_ktm(b + 1)

        op("sync", lambda s: None, waits=[(s_out, s_out.count)])

        with nc.Block() as blk:
            def emit(eng_obj, name):
                for waits, fn, incs in B.eng_ops[name]:
                    for s, v in waits:
                        eng_obj.wait_ge(s.h, v)
                    inst = fn(eng_obj)
                    for jj, (s, n) in enumerate(incs):
                        assert jj == 0 and inst is not None
                        inst.then_inc(s.h, n)

            @blk.sync
            def _(e):
                emit(e, "sync")

            @blk.gpsimd
            def _(e):
                emit(e, "gpsimd")

            @blk.scalar
            def _(e):
                emit(e, "scalar")

            @blk.vector
            def _(e):
                emit(e, "vector")

            @blk.tensor
            def _(e):
                emit(e, "tensor")

    return nc


_NC_CACHE = None


def kernel(L0_attn_x, L0_attn_W_attn, L0_attn_b_attn, L0_attn_W_proj,
           L0_attn_b_proj, L0_ffn_weight, L0_ffn_bias):
    global _NC_CACHE
    x = np.ascontiguousarray(np.asarray(L0_attn_x, dtype=np.float32))
    Wa = np.ascontiguousarray(np.asarray(L0_attn_W_attn, dtype=np.float32))
    ba = np.ascontiguousarray(np.asarray(L0_attn_b_attn, dtype=np.float32))
    Wp = np.ascontiguousarray(np.asarray(L0_attn_W_proj, dtype=np.float32))
    bp = np.ascontiguousarray(np.asarray(L0_attn_b_proj, dtype=np.float32))
    Wf = np.ascontiguousarray(np.asarray(L0_ffn_weight, dtype=np.float32))
    bfv = np.ascontiguousarray(np.asarray(L0_ffn_bias, dtype=np.float32))
    Wft = np.ascontiguousarray(Wf.T)
    mask = np.ascontiguousarray(
        np.tile(np.triu(np.ones((128, 128), np.float32)), (1, 4)).astype(ml_dtypes.bfloat16))
    bqk_h = np.ascontiguousarray(ba[:2048].reshape(16, 128).T)

    if _NC_CACHE is None:
        _NC_CACHE = build_nc()
    nc = _NC_CACHE
    in_maps = []
    for b in range(8):
        in_maps.append({"x": np.ascontiguousarray(x[b]), "W_attn": Wa, "b_attn": ba,
                        "W_proj": Wp, "b_proj": bp, "Wt_ffn": Wft, "b_ffn": bfv,
                        "mask": mask, "bqk_h": bqk_h})
    res = run_bass_kernel_spmd(nc, in_maps, core_ids=list(range(8)))
    out = np.stack([np.asarray(res.results[b]["out"]) for b in range(8)], axis=0)
    return out.astype(np.float32)


# version 11
# speedup vs baseline: 1.1381x; 1.0096x over previous
import numpy as np
import ml_dtypes
import concourse.bass as bass
import concourse.mybir as mybir
from concourse.bass_utils import run_bass_kernel_spmd

F32 = mybir.dt.float32
BF16 = mybir.dt.bfloat16
FP8 = mybir.dt.float8e4
AF = mybir.ActivationFunctionType
ALU = mybir.AluOpType

T = 2048
C = 1024
H = 16
D = 64
KC = 8
NBLK = 4
TPB = 4
LN_EPS = 1e-5
ATT_EPS = 1e-6


class _Sem:
    def __init__(self, handle):
        self.h = handle
        self.count = 0


class _Builder:
    def __init__(self):
        self.eng_ops = {e: [] for e in ("sync", "scalar", "vector", "gpsimd", "tensor")}

    def op(self, eng, fn, waits=(), incs=()):
        for s, v in waits:
            assert v <= s.count, f"wait {v} > produced {s.count} ({eng})"
        self.eng_ops[eng].append((list(waits), fn, list(incs)))
        for s, n in incs:
            s.count += n


def build_nc():
    nc = bass.Bass("TRN2", target_bir_lowering=False, detect_race_conditions=False)

    xh = nc.dram_tensor("x", [T, C], F32, kind="ExternalInput").ap()
    Wa = nc.dram_tensor("W_attn", [C, 3 * C], F32, kind="ExternalInput").ap()
    ba = nc.dram_tensor("b_attn", [3 * C], F32, kind="ExternalInput").ap()
    Wp = nc.dram_tensor("W_proj", [C, C], F32, kind="ExternalInput").ap()
    bp = nc.dram_tensor("b_proj", [C], F32, kind="ExternalInput").ap()
    Wft = nc.dram_tensor("Wt_ffn", [C, C], F32, kind="ExternalInput").ap()
    bf = nc.dram_tensor("b_ffn", [C], F32, kind="ExternalInput").ap()
    maskh = nc.dram_tensor("mask", [128, 128], BF16, kind="ExternalInput").ap()
    bqkh = nc.dram_tensor("bqk_h", [128, 16], F32, kind="ExternalInput").ap()
    outh = nc.dram_tensor("out", [T, C], F32, kind="ExternalOutput").ap()

    import contextlib
    ctx = contextlib.ExitStack()
    sb = lambda name, shape, dt: ctx.enter_context(nc.sbuf_tensor(name, shape, dt))
    ps = lambda name, shape, dt: ctx.enter_context(nc.psum_tensor(name, shape, dt))
    sem = lambda name: _Sem(ctx.enter_context(nc.semaphore(name)))

    with ctx:
        W_qk = sb("W_qk", [128, KC, 2 * C], BF16)
        W_v = sb("W_v", [128, KC, C], BF16)
        W_pr = sb("W_pr", [128, KC, C], BF16)
        W_f8 = sb("W_f8", [128, KC, C], BF16)
        bqk = sb("bqk", [128, 16], F32)
        brows = sb("brows", [65, C], BF16)
        bv_row = brows[0:1, :]
        bp_row = brows[32:33, :]
        bf_row = brows[64:65, :]
        ones_row = sb("ones_row", [65, 128], BF16)
        ones_col = sb("ones_col", [128, 1], BF16)
        zero_row = sb("zero_row", [1, 128], BF16)
        mask_rep = sb("mask_rep", [128, 128], BF16)
        x_sb = sb("x_sb", [128, 2, TPB, C], F32)
        r_sb = sb("r_sb", [128, TPB, C], F32)
        ln_tm = sb("ln_tm", [128, TPB, C], BF16)
        ln_fm = sb("ln_fm", [128, TPB, KC, 128], BF16)
        y_fm = ln_fm
        q_fm = sb("q_fm", [128, TPB, KC, 128], BF16)
        k_fm = sb("k_fm", [128, TPB, KC, 128], BF16)
        k_tm = sb("k_tm", [128, TPB, C], BF16)
        q8_fm = sb("q8_fm", [128, TPB, KC, 128], BF16)
        v_sb = sb("v_sb", [128, TPB, H * 65], BF16)
        k_mk = sb("k_mk", [128, 2, KC, 128], BF16)
        stm = sb("stm", [128, H, 128], BF16)
        y_tm = sb("y_tm", [128, C], BF16)
        q8_tm = sb("q8_tm", [128, TPB, C], FP8)
        e_sb = sb("e_sb", [128, 2, 512], BF16)
        rl_sb = sb("rl_sb", [128, 2, 512], BF16)
        kva = sb("kva", [128, H // 2, D + 1], BF16)
        stats = []
        for tag in ("a", "b"):
            stats.append(dict(
                bn12=sb(f"bn12{tag}", [128, TPB, 12], F32),
                mvab=sb(f"mvab{tag}", [128, TPB, 4], F32),
                mvs4=sb(f"mvs4{tag}", [128, TPB, 2], F32),
                mvd4=sb(f"mvd4{tag}", [128, TPB, 2], F32),
                mvp4=sb(f"mvp4{tag}", [128, TPB, 2], F32),
                mean4=sb(f"mean4{tag}", [128, TPB], F32),
                var4=sb(f"var4{tag}", [128, TPB], F32),
                sqv=sb(f"sqv{tag}", [128, TPB], F32),
                invs=sb(f"invs{tag}", [128, TPB], F32),
                nmb=sb(f"nmb{tag}", [128, TPB], F32),
                sqrt_act=0,
            ))
        den_f = sb("den_f", [128, 16], F32)
        rcp_f = sb("rcp_f", [128, 16], F32)
        c_eps5 = sb("c_eps5", [128, 1], F32)
        c_eps6 = sb("c_eps6", [128, 1], F32)
        c_m1 = sb("c_m1", [128, 1], F32)
        nc.const_aps.register(F32, LN_EPS, c_eps5[:])
        nc.const_aps.register(F32, ATT_EPS, c_eps6[:])
        nc.const_aps.register(F32, -1.0, c_m1[:])
        pbig = [ps("pbig0", [128, 512], F32), ps("pbig1", [128, 512], F32)]
        pnd = ps("pnd", [128, 1536], F32)
        pkv = ps("pkv", [128, 1024], F32)

        s_x = [sem("s_x0"), sem("s_x1")]
        s_const = sem("s_const")
        s_w = sem("s_w")
        s_wv = sem("s_wv")
        s_wqk = sem("s_wqk")
        s_wpr = sem("s_wpr")
        s_stg = [sem("s_stg0"), sem("s_stg1")]
        s_trl = sem("s_trl")
        s_trk = sem("s_trk")
        s_try = sem("s_try")
        s_trq = sem("s_trq")
        s_mm = sem("s_mm")
        s_act = sem("s_act")
        s_dve = sem("s_dve")
        s_pl = sem("s_pl")
        s_out = sem("s_out")

        B = _Builder()
        op = B.op

        def _koff(s):
            return 65 * s if s < 7 else 512

        def _noff(h):
            if h < 7:
                return 65 * h
            if h < 14:
                return 512 + 65 * (h - 7)
            return 1024 + 65 * (h - 14)

        st = dict(
            x_loaded={}, out_done={}, ln_tr={}, ktm_tr={}, ytr={},
            qk_ready={}, k_ready={}, v_ready={}, q8_tr={}, q8_act={},
            attn_mm={}, proj_mm={}, ffn_mm={}, nm_dve={},
            phi_hist=[], div_dve=0, kva_act=0, ks_dve=0, ksum_read_act={},
            chunk_mm=0, w8_all=0, w8_act_all=0,
        )
        bank_free = [[], []]

        op("vector", lambda v: v.memset(ones_row[:], 1.0))
        op("vector", lambda v: v.memset(zero_row[:], 0.0))
        op("vector", lambda v: v.memset(ones_col[:], 1.0))
        op("vector", lambda v: v.memset(kva[:], 0.0))
        op("vector", lambda v: v.memset(k_mk[:], 0.0))
        op("vector", lambda v: v.memset(c_eps5[:], LN_EPS))
        op("vector", lambda v: v.memset(c_eps6[:], ATT_EPS))
        op("vector", lambda v: v.memset(c_m1[:], -1.0), incs=[(s_dve, 1)])

        op("sync", lambda s: s.dma_start(out=mask_rep[:], in_=maskh), incs=[(s_const, 16)])
        op("sync", lambda s: s.dma_start(out=bqk[:], in_=bqkh), incs=[(s_const, 16)])
        consts_ld = s_const.count

        op("gpsimd", lambda g: g.dma_start(out=bv_row, in_=ba[2048:3072].rearrange("(a f) -> a f", a=1)),
           incs=[(s_w, 16)])
        op("gpsimd", lambda g: g.dma_start(out=bp_row, in_=bp.rearrange("(a f) -> a f", a=1)),
           incs=[(s_w, 16)])
        op("gpsimd", lambda g: g.dma_start(out=bf_row, in_=bf.rearrange("(a f) -> a f", a=1)),
           incs=[(s_w, 16)])
        w_bias = s_w.count
        for c in range(KC):
            op("gpsimd", lambda g, c=c: g.dma_start(out=W_v[:, c, :], in_=Wa[128 * c:128 * (c + 1), 2048:3072]),
               incs=[(s_wv, 16)])
        w_v_all = s_wv.count
        for c in range(KC):
            op("gpsimd", lambda g, c=c: g.dma_start(out=W_qk[:, c, :], in_=Wa[128 * c:128 * (c + 1), 0:2048]),
               incs=[(s_wqk, 16)])
        w_qk_all = s_wqk.count
        for c in range(KC):
            op("gpsimd", lambda g, c=c: g.dma_start(out=W_pr[:, c, :], in_=Wp[128 * c:128 * (c + 1), :]),
               incs=[(s_wpr, 16)])
        w_pr_all = s_wpr.count

        for z in range(8):
            op("tensor", lambda t, z=z: t.matmul(
                pkv[:, 128 * z:128 * (z + 1)], zero_row[0:1, 0:128], zero_row[0:1, 0:128],
                start=(z % 4 == 0), stop=False, skip_group_check=True),
               waits=[(s_dve, 1)] if z == 0 else [])

        for b0 in range(min(2, NBLK)):
            for tt in range(TPB):
                t = b0 * TPB + tt
                op("sync", lambda s, b0=b0, tt=tt, t=t: s.dma_start(
                    out=x_sb[:, b0 % 2, tt, :], in_=xh[128 * t:128 * (t + 1), :]),
                   incs=[(s_x[b0 % 2], 16)])
            st["x_loaded"][b0] = s_x[b0 % 2].count

        def emit_stats(S, src_ap_fn, first_waits):
            for i in range(TPB):
                op("vector", lambda v, i=i: v.bn_stats(S["bn12"][:, i, 0:6], src_ap_fn(i)[:, 0:512]),
                   waits=first_waits if i == 0 else [])
                op("vector", lambda v, i=i: v.bn_stats(S["bn12"][:, i, 6:12], src_ap_fn(i)[:, 512:1024]),
                   incs=[(s_dve, 1)] if i == TPB - 1 else [])
            op("vector", lambda v, c=s_dve.count: v.wait_ge(s_dve.h, c))
            for i in range(TPB):
                op("vector", lambda v, i=i: v.bn_aggr(S["mvab"][:, i, 0:2], S["bn12"][:, i, 0:6]))
                op("vector", lambda v, i=i: v.bn_aggr(S["mvab"][:, i, 2:4], S["bn12"][:, i, 6:12]),
                   incs=[(s_dve, 1)] if i == TPB - 1 else [])
            op("vector", lambda v, c=s_dve.count: v.wait_ge(s_dve.h, c))
            op("vector", lambda v: v.tensor_tensor(S["mvs4"][:], S["mvab"][:, :, 0:2], S["mvab"][:, :, 2:4], ALU.add))
            op("vector", lambda v: v.tensor_tensor(S["mvd4"][:], S["mvab"][:, :, 0:2], S["mvab"][:, :, 2:4], ALU.subtract),
               incs=[(s_dve, 1)])
            op("vector", lambda v, c=s_dve.count: v.wait_ge(s_dve.h, c))
            op("vector", lambda v: v.tensor_tensor(S["mvp4"][:], S["mvd4"][:], S["mvd4"][:], ALU.mult))
            op("vector", lambda v: v.tensor_scalar(S["mean4"][:], S["mvs4"][:, :, 0], 0.5, None, ALU.mult))
            op("vector", lambda v: v.tensor_scalar(S["var4"][:], S["mvs4"][:, :, 1], 0.5, None, ALU.mult),
               incs=[(s_dve, 1)])
            op("vector", lambda v, c=s_dve.count: v.wait_ge(s_dve.h, c))
            op("vector", lambda v: v.scalar_tensor_tensor(
                S["var4"][:], S["mvp4"][:, :, 0], 0.25, S["var4"][:], ALU.mult, ALU.add), incs=[(s_dve, 1)])
            d_c = s_dve.count
            op("scalar", lambda s: s.activation(out=S["sqv"][:], in_=S["var4"][:], func=AF.Sqrt, bias=LN_EPS),
               waits=[(s_dve, d_c)], incs=[(s_act, 1)])
            S["sqrt_act"] = s_act.count
            op("vector", lambda v: v.reciprocal(S["invs"][:], S["sqv"][:]),
               waits=[(s_act, S["sqrt_act"])], incs=[(s_dve, 1)])
            op("vector", lambda v, c=s_dve.count: v.wait_ge(s_dve.h, c))
            op("vector", lambda v: v.scalar_tensor_tensor(
                S["nmb"][:], S["mean4"][:], -1.0, S["invs"][:], ALU.mult, ALU.mult),
               incs=[(s_dve, 1)])
            return s_dve.count

        def emit_ln1_stats(b):
            bpar = b % 2
            S = stats[0]
            fw = [(s_x[bpar], st["x_loaded"][b])]
            if b > 1:
                fw.append((s_act, S["sqrt_act"]))
            d_c = emit_stats(S, lambda i, bpar=bpar: x_sb[:, bpar, i, :], fw)
            for i in range(TPB):
                waits = [(s_dve, d_c)]
                if b > 0 and i == 0:
                    waits.append((s_trl, st["ln_tr"][b - 1]))
                op("scalar", lambda s, i=i, bpar=bpar, S=S: s.activation(
                    out=ln_tm[:, i, :], in_=x_sb[:, bpar, i, :], func=AF.Identity,
                    bias=S["nmb"][:, i:i + 1], scale=S["invs"][:, i:i + 1]),
                   waits=waits, incs=[(s_act, 1)])
            st["ln_apply_act"] = s_act.count

        def emit_ln1_tr(b):
            for i in range(TPB):
                twaits = [(s_act, st["ln_apply_act"])]
                if b > 0 and i == 0:
                    twaits.append((s_mm, st["proj_mm"][b - 1]))
                op("scalar", lambda s, i=i: s.dma_start_transpose(
                    ln_fm[:, i, :, :], ln_tm[:, i, :].rearrange("p (c m) -> p c m", c=KC)),
                   waits=twaits if i == 0 else [], incs=[(s_trl, 16)])
            st["ln_tr"][b] = s_trl.count

        def big_nops():
            op("tensor", lambda t: t.nop())
            op("tensor", lambda t: t.nop())

        def emit_v(b):
            for i in range(TPB):
                for half in range(2):
                    bank = (i * 2 + half) % 2
                    waits = [(s_trl, st["ln_tr"][b]), (s_w, w_bias), (s_wv, w_v_all)] + bank_free[bank]
                    for kc in range(KC):
                        op("tensor", lambda t, i=i, half=half, kc=kc, bank=bank: t.matmul(
                            pbig[bank][:], ln_fm[:, i, kc, :], W_v[:, kc, 512 * half:512 * (half + 1)],
                            start=(kc == 0), stop=False),
                           waits=waits if kc == 0 else [])
                        big_nops()
                    op("tensor", lambda t, half=half, bank=bank: t.matmul(
                        pbig[bank][:], ones_row[0:1, 0:128], bv_row[:, 512 * half:512 * (half + 1)],
                        start=False, stop=True), incs=[(s_mm, 1)])
                    mm_c = s_mm.count
                    big_nops()
                    ewaits = [(s_mm, mm_c)]
                    if b > 0 and i == 0 and half == 0:
                        ewaits.append((s_trq, st["q8_tr"][b - 1]))
                    op("vector", lambda v, i=i, half=half, bank=bank: v.tensor_copy(
                        v_sb[:, i, :].rearrange("p (h w) -> p h w", w=65)[:, 8 * half:8 * (half + 1), 0:64],
                        pbig[bank][:].rearrange("p (h d) -> p h d", d=64)),
                       waits=ewaits, incs=[(s_dve, 1)])
                    bank_free[bank] = [(s_dve, s_dve.count)]
            for i in range(TPB):
                op("vector", lambda v, i=i: v.memset(
                    v_sb[:, i, :].rearrange("p (h w) -> p h w", w=65)[:, :, 64:65], 1.0),
                   waits=[(s_trq, st["q8_tr"][b - 1])] if (b > 0 and i == 0) else [],
                   incs=[(s_dve, 1)] if i == TPB - 1 else [])
            st["v_ones_dve"] = s_dve.count
            st["v_ready"][b] = s_dve.count

        def emit_qk(b):
            for fc in [8, 9, 10, 11, 12, 13, 14, 15, 0, 1, 2, 3, 4, 5, 6, 7]:
                bank = fc % 2
                waits = [(s_trl, st["ln_tr"][b]), (s_wqk, w_qk_all)] + bank_free[bank]
                for kc in range(KC):
                    op("tensor", lambda t, fc=fc, kc=kc, bank=bank: t.matmul(
                        pbig[bank][:], W_qk[:, kc, 128 * fc:128 * (fc + 1)], ln_fm[:, :, kc, :],
                        start=(kc == 0), stop=(kc == KC - 1)),
                       waits=waits if kc == 0 else [],
                       incs=[(s_mm, 1)] if kc == KC - 1 else [])
                    big_nops()
                mm_c = s_mm.count
                dst = q_fm[:, :, fc, :] if fc < 8 else k_fm[:, :, fc - 8, :]
                ep = fc % 2
                ewaits = [(s_mm, mm_c)]
                if len(st["phi_hist"]) >= 2:
                    ewaits.append((s_pl, st["phi_hist"][-2]))
                op("scalar", lambda s, fc=fc, ep=ep, bank=bank: s.activation(
                    out=e_sb[:, ep, :], in_=pbig[bank][:], func=AF.Exp, bias=bqk[:, fc:fc + 1]),
                   waits=ewaits, incs=[(s_act, 1)])
                a_c = s_act.count
                op("vector", lambda v, fc=fc, ep=ep, bank=bank: v.tensor_scalar(
                    rl_sb[:, ep, :], pbig[bank][:], bqk[:, fc:fc + 1], 0.0, ALU.add, ALU.max),
                   waits=[(s_act, a_c)], incs=[(s_dve, 1)])
                d_c = s_dve.count
                bank_free[bank] = [(s_dve, d_c)]
                pw = [(s_act, a_c)]
                if b > 0 and fc == 8:
                    pw.append((s_mm, st["attn_mm"][b - 1]))
                op("vector", lambda v, ep=ep: v.tensor_scalar(
                    e_sb[:, ep, :], e_sb[:, ep, :], 1.0, None, ALU.min),
                   waits=pw, incs=[(s_dve, 1)])
                op("vector", lambda v, c=s_dve.count: v.wait_ge(s_dve.h, c))
                op("vector", lambda v, ep=ep, dst=dst: v.tensor_tensor(
                    dst, e_sb[:, ep, :].rearrange("p (a m) -> p a m", a=TPB),
                    rl_sb[:, ep, :].rearrange("p (a m) -> p a m", a=TPB), ALU.add),
                   incs=[(s_pl, 1)])
                st["phi_hist"].append(s_pl.count)
                if fc == 15:
                    st["k_ready"][b] = s_pl.count
            st["qk_ready"][b] = s_pl.count

        def emit_ktm(b):
            for i in range(TPB):
                waits = [(s_pl, st["k_ready"][b])]
                if b > 0 and i == 0:
                    waits.append((s_mm, st["attn_mm"][b - 1]))
                op("scalar", lambda s, i=i: s.dma_start_transpose(
                    k_tm[:, i, :].rearrange("p (c m) -> p c m", c=KC), k_fm[:, i, :, :]),
                   waits=waits, incs=[(s_trk, 16)])
            st["ktm_tr"][b] = s_trk.count

        def emit_w8():
            w8_done = []
            for c in range(KC):
                par = c % 2
                waits = []
                if c >= 2:
                    waits = [(s_act, w8_done[c - 2][0]), (s_dve, w8_done[c - 2][1])]
                op("gpsimd", lambda g, c=c, par=par: g.dma_start(
                    out=r_sb[:, par, :], in_=Wft[128 * c:128 * (c + 1), :]),
                   waits=waits, incs=[(s_stg[par], 16)])
                stg_c = s_stg[par].count
                op("scalar", lambda s, par=par, stg_c=stg_c: s.activation(
                    out=q8_tm[:, par, :], in_=r_sb[:, par, :], func=AF.Copy),
                   waits=[(s_stg[par], stg_c)], incs=[(s_act, 1)])
                a_c = s_act.count
                op("vector", lambda v, c=c, par=par: v.tensor_copy(W_f8[:, c, :], q8_tm[:, par, :]),
                   waits=[(s_act, a_c)], incs=[(s_dve, 1)])
                w8_done.append((a_c, s_dve.count))
            st["w8_all"] = s_dve.count
            st["w8_act_all"] = s_act.count

        def emit_attn(b):
            for ci in range(TPB):
                c = b * TPB + ci
                kmk_w = [(s_pl, st["k_ready"][b])]
                if c > 0:
                    kmk_w.append((s_mm, st["chunk_mm"]))
                op("vector", lambda v, ci=ci: v.tensor_copy(
                    k_mk[0:64, 0, :, :], k_fm[0:64, ci, :, :]), waits=kmk_w)
                op("vector", lambda v, ci=ci: v.tensor_copy(
                    k_mk[64:128, 1, :, :], k_fm[64:128, ci, :, :]), incs=[(s_dve, 1)])
                kmk_dve = s_dve.count
                mask_dve = {}
                for qd in range(4):
                    bank = qd % 2
                    waits = [(s_dve, kmk_dve), (s_trk, st["ktm_tr"][b]),
                             (s_dve, st["v_ready"][b]), (s_const, consts_ld)] + bank_free[bank]
                    for hh in range(4):
                        h = qd * 4 + hh
                        op("tensor", lambda t, h=h, hh=hh, ci=ci, bank=bank: t.matmul(
                            pbig[bank][:, 128 * hh:128 * (hh + 1)],
                            k_mk[:, h % 2, h // 2, :], q_fm[:, ci, h // 2, :],
                            start=(hh == 0), stop=(hh == 3)),
                           waits=waits if hh == 0 else [],
                           incs=[(s_mm, 1)] if hh == 3 else [])
                    st_mm = s_mm.count
                    op("vector", lambda v, qd=qd, bank=bank: v.scalar_tensor_tensor(
                        stm[:, 4 * qd:4 * (qd + 1), :], pbig[bank][:].rearrange("p (a m) -> p a m", a=4),
                        1.0, mask_rep[:, None, :].broadcast_to((128, 4, 128)), ALU.mult, ALU.mult),
                       waits=[(s_mm, st_mm)], incs=[(s_dve, 1)])
                    mask_dve[qd] = s_dve.count
                    bank_free[bank] = [(s_dve, mask_dve[qd])]
                for h in range(H):
                    po = 64 * (h % 2)
                    no = _noff(h)
                    waits = [(s_dve, mask_dve[h // 4])]
                    if h == 0:
                        waits.append((s_dve, st["v_ones_dve"]))
                    if c > 0 and h == 0:
                        waits.append((s_act, st["kva_act"]))
                    op("tensor", lambda t, h=h, ci=ci, no=no: t.matmul(
                        pnd[:, no:no + 65], stm[:, h, :],
                        v_sb[:, ci, 65 * h:65 * (h + 1)],
                        start=(h in (0, 7, 14)), stop=False), waits=waits)
                    op("tensor", lambda t, h=h, po=po, ci=ci, no=no: t.matmul(
                        pnd[:, no:no + 65], q_fm[po:po + 64, ci, h // 2, :],
                        kva[po:po + 64, h // 2, 0:D + 1], start=False,
                        stop=(h in (6, 13, 15))))
                    op("tensor", lambda t, h=h, po=po, ci=ci: t.matmul(
                        pkv[po:po + 64, _koff(h // 2):_koff(h // 2) + 65],
                        k_tm[:, ci, 64 * h:64 * (h + 1)],
                        v_sb[:, ci, 65 * h:65 * (h + 1)],
                        start=False, stop=False, skip_group_check=True),
                       incs=[(s_mm, 1)] if h == H - 1 else [])
                st["chunk_mm"] = s_mm.count
                st["attn_mm"][b] = st["chunk_mm"]
                waits = [(s_mm, st["chunk_mm"])]
                for g, (h0, nh) in enumerate(((0, 7), (7, 7), (14, 2))):
                    op("vector", lambda v, h0=h0, nh=nh, g=g: v.tensor_scalar(
                        den_f[:, h0:h0 + nh],
                        pnd[:, _noff(h0):_noff(h0) + 65 * nh].rearrange(
                            "p (a w) -> p a w", w=65)[:, :, 64],
                        ATT_EPS, None, ALU.add),
                       waits=waits if g == 0 else [],
                       incs=[(s_dve, 1)] if g == 2 else [])
                op("vector", lambda v, c_=s_dve.count: v.wait_ge(s_dve.h, c_))
                op("vector", lambda v: v.reciprocal(rcp_f[:], den_f[:]), incs=[(s_dve, 1)])
                rcp_dve = s_dve.count
                ywaits = [(s_dve, rcp_dve)]
                if c >= 1:
                    ywaits.append((s_try, st["ytr"][c - 1]))
                for g, (h0, nh) in enumerate(((0, 7), (7, 7), (14, 2))):
                    op("vector", lambda v, h0=h0, nh=nh: v.tensor_tensor(
                        y_tm[:, 64 * h0:64 * (h0 + nh)].rearrange("p (a d) -> p a d", d=64),
                        pnd[:, _noff(h0):_noff(h0) + 65 * nh].rearrange(
                            "p (a w) -> p a w", w=65)[:, :, 0:64],
                        rcp_f[:, h0:h0 + nh, None].broadcast_to((128, nh, D)), ALU.mult),
                       waits=ywaits if g == 0 else [],
                       incs=[(s_dve, 1)] if g == 2 else [])
                div_dve = s_dve.count
                st["div_dve"] = div_dve
                op("scalar", lambda s: s.activation(
                    out=kva[:, 0:7, :], in_=pkv[:, 0:455].rearrange("p (a w) -> p a w", w=65),
                    func=AF.Copy),
                   waits=[(s_mm, st["chunk_mm"])])
                op("scalar", lambda s: s.activation(
                    out=kva[:, 7, :], in_=pkv[:, 512:577], func=AF.Copy),
                   incs=[(s_act, 1)])
                st["kva_act"] = s_act.count
                op("scalar", lambda s, ci=ci: s.dma_start_transpose(
                    y_fm[:, ci, :, :], y_tm[:].rearrange("p (c m) -> p c m", c=KC)),
                   waits=[(s_dve, div_dve)], incs=[(s_try, 16)])
                st["ytr"][c] = s_try.count
                if ci > 0:
                    emit_proj_tile(b, ci - 1)

        def emit_proj_tile(b, i):
            bpar = b % 2
            if True:
                for half in range(2):
                    bank = (i * 2 + half) % 2
                    waits = [(s_try, st["ytr"][b * TPB + i]), (s_w, w_bias),
                             (s_wpr, w_pr_all)] + bank_free[bank]
                    for kc in range(KC):
                        op("tensor", lambda t, i=i, half=half, kc=kc, bank=bank: t.matmul(
                            pbig[bank][:], y_fm[:, i, kc, :], W_pr[:, kc, 512 * half:512 * (half + 1)],
                            start=(kc == 0), stop=False),
                           waits=waits if kc == 0 else [])
                        big_nops()
                    op("tensor", lambda t, half=half, bank=bank: t.matmul(
                        pbig[bank][:], ones_row[32:33, 0:128], bp_row[:, 512 * half:512 * (half + 1)],
                        start=False, stop=True), incs=[(s_mm, 1)])
                    mm_c = s_mm.count
                    big_nops()
                    ewaits = [(s_mm, mm_c)]
                    if b == 0 and i == 0 and half == 0:
                        ewaits.append((s_act, st["w8_act_all"]))
                    if b > 0 and i == 0 and half == 0:
                        ewaits.append((s_act, st["q8_act"][b - 1]))
                    op("vector", lambda v, i=i, half=half, bank=bank, bpar=bpar: v.tensor_tensor(
                        r_sb[:, i, 512 * half:512 * (half + 1)], pbig[bank][:],
                        x_sb[:, bpar, i, 512 * half:512 * (half + 1)], ALU.add),
                       waits=ewaits, incs=[(s_dve, 1)])
                    bank_free[bank] = [(s_dve, s_dve.count)]
            st["proj_mm"][b] = s_mm.count

        def emit_ln2q8(b):
            S = stats[1]
            fw = []
            if b > 0:
                fw.append((s_act, S["sqrt_act"]))
            d_c = emit_stats(S, lambda i: r_sb[:, i, :], fw)
            for i in range(TPB):
                op("scalar", lambda s, i=i, S=S: s.activation(
                    out=q8_tm[:, i, :], in_=r_sb[:, i, :], func=AF.Identity,
                    bias=S["nmb"][:, i:i + 1], scale=S["invs"][:, i:i + 1]),
                   waits=[(s_dve, d_c)] if i == 0 else [], incs=[(s_act, 1)])
                a_c = s_act.count
                cw = [(s_act, a_c)]
                if i == 0:
                    cw.append((s_mm, st["attn_mm"][b]))
                op("vector", lambda v, i=i: v.tensor_copy(v_sb[:, i, 0:C], q8_tm[:, i, :]),
                   waits=cw, incs=[(s_dve, 1)])
                d_c2 = s_dve.count
                tqw = [(s_dve, d_c2)]
                if b > 0 and i == 0:
                    tqw.append((s_mm, st["ffn_mm"][b - 1]))
                op("scalar", lambda s, i=i: s.dma_start_transpose(
                    q8_fm[:, i, :, :],
                    v_sb[:, i, 0:C].rearrange("p (c m) -> p c m", c=KC)),
                   waits=tqw, incs=[(s_trq, 16)])
            st["q8_tr"][b] = s_trq.count
            st["q8_act"][b] = s_act.count

        def emit_ffn(b):
            bpar = b % 2
            for i in range(TPB):
                for half in range(2):
                    bank = (i * 2 + half) % 2
                    waits = [(s_trq, st["q8_tr"][b]), (s_dve, st["w8_all"]),
                             (s_w, w_bias)] + bank_free[bank]
                    for kc in range(KC):
                        op("tensor", lambda t, i=i, half=half, kc=kc, bank=bank: t.matmul(
                            pbig[bank][:], q8_fm[:, i, kc, :],
                            W_f8[:, kc, 512 * half:512 * (half + 1)],
                            start=(kc == 0), stop=False),
                           waits=waits if kc == 0 else [])
                        big_nops()
                    op("tensor", lambda t, half=half, bank=bank: t.matmul(
                        pbig[bank][:], ones_row[64:65, 0:128], bf_row[:, 512 * half:512 * (half + 1)],
                        start=False, stop=True), incs=[(s_mm, 1)])
                    mm_c = s_mm.count
                    big_nops()
                    op("vector", lambda v, i=i, half=half, bank=bank, bpar=bpar: v.tensor_tensor(
                        x_sb[:, bpar, i, 512 * half:512 * (half + 1)], pbig[bank][:],
                        r_sb[:, i, 512 * half:512 * (half + 1)], ALU.add),
                       waits=[(s_mm, mm_c)], incs=[(s_dve, 1)])
                    bank_free[bank] = [(s_dve, s_dve.count)]
                d_c = s_dve.count
                t = b * TPB + i
                op("sync", lambda s, i=i, t=t, bpar=bpar: s.dma_start(
                    out=outh[128 * t:128 * (t + 1), :], in_=x_sb[:, bpar, i, :]),
                   waits=[(s_dve, d_c)], incs=[(s_out, 16)])
            st["ffn_mm"][b] = s_mm.count
            st["out_done"][b] = s_out.count
            if b + 2 <= NBLK - 1:
                bn = b + 2
                for tt in range(TPB):
                    t = bn * TPB + tt
                    op("sync", lambda s, bn=bn, tt=tt, t=t: s.dma_start(
                        out=x_sb[:, bn % 2, tt, :], in_=xh[128 * t:128 * (t + 1), :]),
                       waits=[(s_out, st["out_done"][b])] if tt == 0 else [],
                       incs=[(s_x[bn % 2], 16)])
                st["x_loaded"][bn] = s_x[bn % 2].count

        emit_ln1_stats(0)
        emit_ln1_tr(0)
        emit_v(0)
        emit_qk(0)
        emit_ktm(0)
        emit_w8()
        for b in range(NBLK):
            if b + 1 < NBLK:
                emit_ln1_stats(b + 1)
            emit_attn(b)
            emit_proj_tile(b, TPB - 1)
            if b + 1 < NBLK:
                emit_ln1_tr(b + 1)
            emit_ln2q8(b)
            if b + 1 < NBLK:
                emit_v(b + 1)
                emit_qk(b + 1)
                emit_ktm(b + 1)
            emit_ffn(b)

        op("sync", lambda s: None, waits=[(s_out, s_out.count)])

        with nc.Block() as blk:
            def emit(eng_obj, name):
                for waits, fn, incs in B.eng_ops[name]:
                    for s, v in waits:
                        eng_obj.wait_ge(s.h, v)
                    inst = fn(eng_obj)
                    for jj, (s, n) in enumerate(incs):
                        assert jj == 0 and inst is not None
                        inst.then_inc(s.h, n)

            @blk.sync
            def _(e):
                emit(e, "sync")

            @blk.gpsimd
            def _(e):
                emit(e, "gpsimd")

            @blk.scalar
            def _(e):
                emit(e, "scalar")

            @blk.vector
            def _(e):
                emit(e, "vector")

            @blk.tensor
            def _(e):
                emit(e, "tensor")

    return nc


_NC_CACHE = None


def kernel(L0_attn_x, L0_attn_W_attn, L0_attn_b_attn, L0_attn_W_proj,
           L0_attn_b_proj, L0_ffn_weight, L0_ffn_bias):
    global _NC_CACHE
    x = np.ascontiguousarray(np.asarray(L0_attn_x, dtype=np.float32))
    Wa = np.ascontiguousarray(np.asarray(L0_attn_W_attn, dtype=np.float32))
    ba = np.ascontiguousarray(np.asarray(L0_attn_b_attn, dtype=np.float32))
    Wp = np.ascontiguousarray(np.asarray(L0_attn_W_proj, dtype=np.float32))
    bp = np.ascontiguousarray(np.asarray(L0_attn_b_proj, dtype=np.float32))
    Wf = np.ascontiguousarray(np.asarray(L0_ffn_weight, dtype=np.float32))
    bfv = np.ascontiguousarray(np.asarray(L0_ffn_bias, dtype=np.float32))
    Wft = np.ascontiguousarray(Wf.T)
    mask = np.ascontiguousarray(np.triu(np.ones((128, 128), np.float32)).astype(ml_dtypes.bfloat16))
    bqk_h = np.ascontiguousarray(ba[:2048].reshape(16, 128).T)

    if _NC_CACHE is None:
        _NC_CACHE = build_nc()
    nc = _NC_CACHE
    in_maps = []
    for b in range(8):
        in_maps.append({"x": np.ascontiguousarray(x[b]), "W_attn": Wa, "b_attn": ba,
                        "W_proj": Wp, "b_proj": bp, "Wt_ffn": Wft, "b_ffn": bfv,
                        "mask": mask, "bqk_h": bqk_h})
    res = run_bass_kernel_spmd(nc, in_maps, core_ids=list(range(8)))
    out = np.stack([np.asarray(res.results[b]["out"]) for b in range(8)], axis=0)
    return out.astype(np.float32)
